# version 06
# baseline (speedup 1.0000x reference)
import numpy as np
import contextlib
import concourse.bass as bass
import concourse.mybir as mybir
from concourse.bass_utils import run_bass_kernel_spmd
from concourse.alu_op_type import AluOpType as OP

F32 = mybir.dt.float32
BF16 = mybir.dt.bfloat16
U8 = mybir.dt.uint8
AF = mybir.ActivationFunctionType

T = 4096
D = 1024
NCK = 8
H = 16
HD = 64
EXP_SHIFT = 12.0

LN_EPS = 1e-5
GN_EPS = 1e-5 * 64.0
RMS_EPS = 1e-6

ENGS = ("sync", "tensor", "vector", "scalar", "gpsimd")


class Prog:

    def __init__(self, nc, name):
        self.nc = nc
        self.name = name
        self.steps = []
        self.cum = []

    def step(self, engine, fn, deps=(), inc=1):
        assert engine in ENGS
        for d in deps:
            assert 0 <= d < len(self.steps), (d, len(self.steps))
        cum_prev = 0
        for i in range(len(self.steps) - 1, -1, -1):
            if self.steps[i][0] == engine:
                cum_prev = self.cum[i]
                break
        self.steps.append([engine, fn, inc, tuple(deps)])
        self.cum.append(cum_prev + inc)
        return len(self.steps) - 1

    def dma(self, engine, fn, deps=()):
        return self.step(engine, fn, deps, inc=16)

    def play(self):
        nc = self.nc
        with nc.Block() as blk:
            sems = {}
            ctx = []
            for e in ENGS:
                if any(s[0] == e for s in self.steps):
                    cm = nc.semaphore(f"{self.name}_{e}")
                    sems[e] = cm.__enter__()
                    ctx.append(cm)
            watermark = {}

            def play_engine(eng_name, eng):
                dma_cum = 0
                for idx, (e, fn, inc, deps) in enumerate(self.steps):
                    if e != eng_name:
                        continue
                    if inc == 16:
                        if dma_cum > 0:
                            eng.wait_ge(sems[eng_name], dma_cum)
                        dma_cum = self.cum[idx]
                    for d in deps:
                        de, _, _, _ = self.steps[d]
                        need = self.cum[d]
                        key = (eng_name, de)
                        if watermark.get(key, -1) >= need:
                            continue
                        watermark[key] = need
                        eng.wait_ge(sems[de], need)
                    ret = fn(eng)
                    if isinstance(ret, int):
                        assert ret == inc, (idx, ret, inc)
                    elif ret is None:
                        raise RuntimeError(f"step {idx} returned None; must return last instruction")
                    else:
                        ret.then_inc(sems[eng_name], inc)

            for e in ENGS:
                if e not in sems:
                    continue
                if e == "sync":
                    @blk.sync
                    def _(s, e=e):
                        play_engine(e, s)
                elif e == "tensor":
                    @blk.tensor
                    def _(s, e=e):
                        play_engine(e, s)
                elif e == "vector":
                    @blk.vector
                    def _(s, e=e):
                        play_engine(e, s)
                elif e == "scalar":
                    @blk.scalar
                    def _(s, e=e):
                        play_engine(e, s)
                elif e == "gpsimd":
                    @blk.gpsimd
                    def _(s, e=e):
                        play_engine(e, s)
            for cm in ctx:
                cm.__exit__(None, None, None)


def dma16(ap_out, ap_in):
    def fn(eng):
        return eng.dma_start(ap_out, ap_in)
    return fn


def rearr(dram, p=128):
    return dram.ap().rearrange("(c p) t -> p c t", p=p)


class Builder:
    def __init__(self, nc: bass.Bass, debug=(), preload=()):
        self.nc = nc
        self.debug = set(debug)
        self.preload = set(preload)
        self.inp = {}
        self.scr = {}

    def dram_in(self, name, shape, dtype=F32):
        h = self.nc.dram_tensor(name, shape, dtype, kind="ExternalInput")
        self.inp[name] = h
        return h

    def scratch(self, name, shape):
        kind = "ExternalOutput" if name in self.debug else "Internal"
        if name in self.preload:
            kind = "ExternalInput"
        h = self.nc.dram_tensor(name, shape, F32, kind=kind)
        self.scr[name] = h
        return h


def phase_p1a(nc, B):
    TT = 512
    NT = T // TT
    inp, scr = B.inp, B.scr
    with (
        nc.sbuf_tensor("p1a_x", [128, 2, NCK, TT], F32) as xb,
        nc.sbuf_tensor("p1a_sq", [128, NCK, TT], F32) as sqb,
        nc.sbuf_tensor("p1a_xn", [128, 2, NCK, TT], F32) as xnb,
        nc.sbuf_tensor("p1a_ones", [128, 1], F32) as onescol,
        nc.sbuf_tensor("p1a_ones1", [1, 128], F32) as onesrow,
        nc.sbuf_tensor("p1a_rows", [1, 6 * TT], F32) as rows,
        nc.psum_tensor("p1a_ps_s", [1, 2 * TT], F32) as ps_s,
        nc.psum_tensor("p1a_ps_q", [1, 2 * TT], F32) as ps_q,
        nc.psum_tensor("p1a_ps_mu", [128, TT], F32) as ps_mu,
        nc.psum_tensor("p1a_ps_rs", [128, TT], F32) as ps_rs,
    ):
        P = Prog(nc, "p1a")

        def R(i):
            return rows[0:1, i * TT:(i + 1) * TT]

        c_ones = P.dma("sync", dma16(onescol[:, :], inp["ones128"].ap()))
        c_ones1 = P.dma("sync", dma16(onesrow[:, :], inp["ones1x128"].ap()))

        last_use = [None, None]
        bc_use = None
        stat_use = None
        use_sq = None
        for i in range(NT):
            r = i % 2
            deps = [last_use[r]] if last_use[r] is not None else []
            ldx = P.dma("sync", dma16(xb[:, r, :, :], rearr(inp["xT"])[:, :, i * TT:(i + 1) * TT]), deps)
            sq = P.step("scalar", lambda s, r=r: s.activation(sqb[:, :, :], xb[:, r, :, :], AF.Square),
                        [ldx] + ([use_sq] if use_sq else []))

            def mm_stats(t, r=r, i=i):
                p = (i % 2) * TT
                for c in range(NCK):
                    t.matmul(ps_s[0:1, p:p + TT], onescol[:, :], xb[:, r, c, :],
                             start=(c == 0), stop=(c == NCK - 1))
                last = None
                for c in range(NCK):
                    last = t.matmul(ps_q[0:1, p:p + TT], onescol[:, :], sqb[:, c, :],
                                    start=(c == 0), stop=(c == NCK - 1))
                return last
            mst = P.step("tensor", mm_stats, [ldx, sq, c_ones] + ([stat_use] if stat_use else []))
            use_sq = mst

            def v_stats(v, i=i):
                p = (i % 2) * TT
                v.tensor_scalar(R(2), ps_s[0:1, p:p + TT], 1.0 / D, None, OP.mult)
                v.tensor_scalar(R(1), ps_q[0:1, p:p + TT], 1.0 / D, None, OP.mult)
                v.tensor_tensor(R(0), R(2), R(2), OP.mult)
                v.tensor_tensor(R(3), R(1), R(0), OP.subtract)
                return v.tensor_scalar(R(3), R(3), LN_EPS, None, OP.add)
            vst = P.step("vector", v_stats, [mst])
            stat_use = vst

            def s_rstd(s):
                s.activation(R(5), R(3), AF.Ln)
                return s.activation(R(4), R(5), AF.Exp, scale=-0.5)
            srs = P.step("scalar", s_rstd, [vst])

            def v_newton(v):
                v.tensor_tensor(R(5), R(4), R(4), OP.mult)
                v.tensor_tensor(R(5), R(5), R(3), OP.mult)
                v.tensor_scalar(R(5), R(5), -0.5, 1.5, OP.mult, OP.add)
                return v.tensor_tensor(R(4), R(4), R(5), OP.mult)
            vnw = P.step("vector", v_newton, [srs])

            def mm_bcast(t):
                t.matmul(ps_mu[:, :], onesrow[:, :], R(2), start=True, stop=True)
                return t.matmul(ps_rs[:, :], onesrow[:, :], R(4), start=True, stop=True)
            bc = P.step("tensor", mm_bcast, [vnw, c_ones1] + ([bc_use] if bc_use else []))

            def v_xn(v, r=r):
                last = None
                for c in range(NCK):
                    v.tensor_tensor(xnb[:, r, c, :], xb[:, r, c, :], ps_mu[:, :], OP.subtract)
                    last = v.tensor_tensor(xnb[:, r, c, :], xnb[:, r, c, :], ps_rs[:, :], OP.mult)
                return last
            vxn = P.step("vector", v_xn, [bc])
            bc_use = vxn

            st = P.dma("gpsimd", dma16(rearr(scr["xnT"])[:, :, i * TT:(i + 1) * TT], xnb[:, r, :, :]), [vxn])
            last_use[r] = st
        P.play()



def phase_p1b(nc, B, part):
    TT = 256
    NT = T // TT
    inp, scr = B.inp, B.scr
    projs = ["w", "k", "r"] if part == 1 else ["g", "v"]
    f_of = {"w": 0, "k": 1, "v": 2, "r": 3, "g": 4}
    out_of = {"k": scr["kT"], "r": scr["rT"], "g": scr["gpreT"]}
    pfx = f"p1b{part}"
    with contextlib.ExitStack() as _es:
        xnb = _es.enter_context(nc.sbuf_tensor(f"{pfx}_xn", [128, 2, NCK, TT + 2], F32))
        dxb = _es.enter_context(nc.sbuf_tensor(f"{pfx}_dx", [128, NCK, TT], F32))
        mixb = _es.enter_context(nc.sbuf_tensor(f"{pfx}_mix", [128, NCK, TT] if part == 1 else [1, 1], F32))
        xxxb = _es.enter_context(nc.sbuf_tensor(f"{pfx}_xxx", [32, 2, 5, TT], F32))
        xsb = _es.enter_context(nc.sbuf_tensor(f"{pfx}_xs", [128, 2, NCK, TT], F32))
        stage = _es.enter_context(nc.sbuf_tensor(f"{pfx}_stage", [128, 2, NCK, TT], F32))
        w1b = _es.enter_context(nc.sbuf_tensor(f"{pfx}_w1", [128, NCK, 160] if part == 1 else [1, 1], F32))
        w2b = _es.enter_context(nc.sbuf_tensor(f"{pfx}_w2h", [32, 5 * D], F32))
        c5b = _es.enter_context(nc.sbuf_tensor(f"{pfx}_c5", [128, 5 * NCK], F32))
        tmxb = _es.enter_context(nc.sbuf_tensor(f"{pfx}_tmx", [128, NCK] if part == 1 else [1, 1], F32))
        WA = _es.enter_context(nc.sbuf_tensor(f"{pfx}_WA", [128, NCK, D], F32))
        WB = _es.enter_context(nc.sbuf_tensor(f"{pfx}_WB", [128, NCK, D], F32))
        dw1b = _es.enter_context(nc.sbuf_tensor(f"{pfx}_dw1", [128, NCK, 64] if part == 1 else [1, 1], F32))
        dw2b = _es.enter_context(nc.sbuf_tensor(f"{pfx}_dw2", [64, D] if part == 1 else [1, 1], F32))
        tdecb = _es.enter_context(nc.sbuf_tensor(f"{pfx}_tdec", [128, NCK] if part == 1 else [1, 1], F32))
        tdb = _es.enter_context(nc.sbuf_tensor(f"{pfx}_td", [64, TT] if part == 1 else [1, 1], F32))
        ewb = _es.enter_context(nc.sbuf_tensor(f"{pfx}_ew", [128, NCK, TT] if part == 1 else [1, 1], F32))
        csb = _es.enter_context(nc.sbuf_tensor(f"{pfx}_cs", [128, 2, NCK, TT] if part == 1 else [1, 1], F32))
        psx1 = _es.enter_context(nc.psum_tensor(f"{pfx}_psx1", [96, 512], F32))
        psx2 = _es.enter_context(nc.psum_tensor(f"{pfx}_psx2", [64, 512], F32))
        psm0 = _es.enter_context(nc.psum_tensor(f"{pfx}_psm0", [128, 512], F32))
        psm1 = _es.enter_context(nc.psum_tensor(f"{pfx}_psm1", [128, 512], F32))
        psp0 = _es.enter_context(nc.psum_tensor(f"{pfx}_psp0", [128, 512], F32))
        psp1 = _es.enter_context(nc.psum_tensor(f"{pfx}_psp1", [128, 512], F32))
        psd = _es.enter_context(nc.psum_tensor(f"{pfx}_psd", [64, 512] if part == 1 else [1, 1], F32))
        psm = [psm0, psm1]
        psp = [psp0, psp1]
        P = Prog(nc, pfx)

        pre = {}
        pre["w1"] = P.dma("sync", dma16(w1b[:, :, :], inp["maa_w1"].ap().rearrange("(c p) o -> p c o", p=128))) if part == 1 else None
        pre["w2"] = P.dma("sync", dma16(w2b[:, :], inp["w2h"].ap()))
        pre["c5"] = P.dma("sync", dma16(c5b[:, :], inp["c5col"].ap()))
        pre["tmx"] = P.dma("sync", dma16(tmxb[:, :], inp["tmx"].ap())) if part == 1 else None
        WA_src = inp["wkT"] if part == 1 else inp["wgT"]
        WB_src = inp["wrT"] if part == 1 else inp["wvT"]
        pre["WA"] = P.dma("sync", dma16(WA[:, :, :], WA_src.ap().rearrange("(c p) o -> p c o", p=128)))
        pre["WB"] = P.dma("sync", dma16(WB[:, :, :], WB_src.ap().rearrange("(c p) o -> p c o", p=128)))
        if part == 1:
            pre["dw1"] = P.dma("sync", dma16(dw1b[:, :, :], inp["dw1"].ap().rearrange("(c p) o -> p c o", p=128)))
            pre["dw2"] = P.dma("sync", dma16(dw2b[:, :], inp["dw2"].ap()))
            pre["tdec"] = P.dma("sync", dma16(tdecb[:, :], inp["tdec"].ap()))
        ring_xn = [None, None]
        ring_ew = None
        ring_xs = [None, None]
        ring_stage = [None, None]
        ring_ewcs = [None, None]
        ring_xxx = [None, None]
        use_mix = None
        use_dx = None
        use_psx = None
        use_psm = [None, None]
        use_psp = [None, None]
        use_psd = None
        prev_scan = None

        for i in range(NT):
            r = i % 2
            lo, hi = i * TT - 1, (i + 1) * TT + 1
            s_lo, s_hi = max(lo, 0), min(hi, T)
            d_lo = s_lo - lo
            d_hi = d_lo + (s_hi - s_lo)
            deps = [ring_xn[r]] if ring_xn[r] is not None else []
            ldx = P.dma("sync", dma16(xnb[:, r, :, d_lo:d_hi], rearr(scr["xnT"])[:, :, s_lo:s_hi]), deps)
            if part == 2:
                xdeps = [ring_xxx[r]] if ring_xxx[r] is not None else []
                ldxxx = P.dma("sync", dma16(
                    xxxb[0:32, r, :, :],
                    scr["xxxT"].ap().rearrange("p (f t) -> p f t", f=5)[:, :, i * TT:(i + 1) * TT]), xdeps)

            def v_prep(v, r=r, i=i):
                bx = xnb[:, r, :, :]
                if i == 0:
                    v.memset(xnb[:, r, :, 0:1], 0.0)
                if i == NT - 1:
                    v.memset(xnb[:, r, :, TT + 1:TT + 2], 0.0)
                v.tensor_tensor(dxb[:, :, :], bx[:, :, 0:TT], bx[:, :, 2:TT + 2], OP.add)
                last = v.scalar_tensor_tensor(dxb[:, :, :], dxb[:, :, :], 0.5, bx[:, :, 1:TT + 1],
                                              OP.mult, OP.subtract)
                if part == 1:
                    for c in range(NCK):
                        last = v.scalar_tensor_tensor(mixb[:, c, :], dxb[:, c, :], tmxb[:, c:c + 1],
                                                      bx[:, c, 1:TT + 1], OP.mult, OP.add)
                return last
            vp_deps = [ldx]
            if part == 1:
                vp_deps.append(pre["tmx"])
                if use_mix is not None:
                    vp_deps.append(use_mix)
            if ring_xn[r] is not None:
                vp_deps.append(ring_xn[r])
            if use_dx is not None:
                vp_deps.append(use_dx)
            vprep = P.step("vector", v_prep, vp_deps)

            if part == 1:
                def mm_maa1(t):
                    for c in range(NCK):
                        t.matmul(psx1[:, 0:TT], w1b[:, c, 0:96], mixb[:, c, :],
                                 start=(c == 0), stop=(c == NCK - 1))
                    last = None
                    for c in range(NCK):
                        last = t.matmul(psx2[:, 0:TT], w1b[:, c, 96:160], mixb[:, c, :],
                                        start=(c == 0), stop=(c == NCK - 1))
                    return last
                mmx = P.step("tensor", mm_maa1, [vprep, pre["w1"]] + ([use_psx] if use_psx else []))
                use_mix = mmx

                def s_xxx(s, r=r):
                    s.activation(xxxb[0:32, r, 0, :], psx1[0:32, 0:TT], AF.Tanh)
                    s.activation(xxxb[0:32, r, 1, :], psx1[32:64, 0:TT], AF.Tanh)
                    s.activation(xxxb[0:32, r, 2, :], psx1[64:96, 0:TT], AF.Tanh)
                    s.activation(xxxb[0:32, r, 3, :], psx2[0:32, 0:TT], AF.Tanh)
                    return s.activation(xxxb[0:32, r, 4, :], psx2[32:64, 0:TT], AF.Tanh)
                sx_deps = [mmx]
                if ring_xxx[r] is not None:
                    sx_deps.append(ring_xxx[r])
                sxxx = P.step("scalar", s_xxx, sx_deps)
                use_psx = sxxx
                stxxx = P.dma("gpsimd", dma16(
                    scr["xxxT"].ap().rearrange("p (f t) -> p f t", f=5)[:, :, i * TT:(i + 1) * TT],
                    xxxb[0:32, r, :, :]), [sxxx])
                xxx_ready = sxxx
                xxx_consumers = [stxxx]
            else:
                xxx_ready = ldxxx
                xxx_consumers = []

            last_step_of_tile = None
            for pi, p in enumerate(projs):
                pr = pi % 2
                f = f_of[p]
                xs_steps = []
                for oc in range(NCK):
                    def mm_one(t, pr=pr, f=f, r=r, oc=oc):
                        return t.matmul(psm[pr][:, 0:TT], w2b[:, f * D + oc * 128: f * D + (oc + 1) * 128],
                                        xxxb[0:32, r, f, :], start=True, stop=True)
                    mdeps = [xxx_ready, vprep, pre["w2"]]
                    if use_psm[pr] is not None:
                        mdeps.append(use_psm[pr])
                    mt = P.step("tensor", mm_one, mdeps)

                    def v_xs(v, pr=pr, r=r, oc=oc, f=f):
                        v.scalar_tensor_tensor(xsb[:, pr, oc, :], psm[pr][:, 0:TT],
                                               c5b[:, f * NCK + oc:f * NCK + oc + 1],
                                               dxb[:, oc, :], OP.add, OP.mult)
                        return v.tensor_tensor(xsb[:, pr, oc, :], xsb[:, pr, oc, :],
                                               xnb[:, r, oc, 1:TT + 1], OP.add)
                    xdeps2 = [mt, pre["c5"]]
                    if ring_xs[pr] is not None:
                        xdeps2.append(ring_xs[pr])
                    vxs = P.step("vector", v_xs, xdeps2)
                    use_psm[pr] = vxs
                    xs_steps.append(vxs)
                xxx_consumers.append(xs_steps[-1])

                if p == "w":
                    def mm_d1(t, pr=pr):
                        last = None
                        for c in range(NCK):
                            last = t.matmul(psd[:, 0:TT], dw1b[:, c, :], xsb[:, pr, c, :],
                                            start=(c == 0), stop=(c == NCK - 1))
                        return last
                    d1deps = xs_steps + [pre["dw1"]]
                    if use_psd is not None:
                        d1deps.append(use_psd)
                    md1 = P.step("tensor", mm_d1, d1deps)
                    std = P.step("scalar", lambda s: s.activation(tdb[:, :], psd[:, 0:TT], AF.Tanh), [md1])
                    use_psd = std
                    ew_steps = []
                    for oc in range(NCK):
                        def mm_d2(t, oc=oc):
                            return t.matmul(psp[oc % 2][:, 0:TT], dw2b[:, oc * 128:(oc + 1) * 128],
                                            tdb[:, :], start=True, stop=True)
                        ddeps = [std, pre["dw2"]]
                        if use_psp[oc % 2] is not None:
                            ddeps.append(use_psp[oc % 2])
                        md2 = P.step("tensor", mm_d2, ddeps)

                        def s_ew(s, oc=oc):
                            return s.activation(ewb[:, oc, :], psp[oc % 2][:, 0:TT],
                                                AF.Exp, bias=tdecb[:, oc:oc + 1], scale=1.0)
                        ewdeps = [md2, pre["tdec"]]
                        if ring_ew is not None:
                            ewdeps.append(ring_ew)
                        if prev_scan is not None:
                            ewdeps.append(prev_scan)
                        sew = P.step("scalar", s_ew, ewdeps)
                        use_psp[oc % 2] = sew
                        ew_steps.append(sew)

                    def v_scan(v, r=r, i=i):
                        last = None
                        for oc in range(NCK):
                            init = 0.0 if i == 0 else csb[:, 1 - r, oc, TT - 1:TT]
                            last = v.tensor_tensor_scan(csb[:, r, oc, :], ewb[:, oc, :],
                                                        ewb[:, oc, :], init, OP.add, OP.bypass)
                        return last
                    sdeps = list(ew_steps) + ([prev_scan] if prev_scan else [])
                    if ring_ewcs[r] is not None:
                        sdeps.append(ring_ewcs[r])
                    vscan = P.step("vector", v_scan, sdeps)
                    prev_scan = vscan
                    stew = P.dma("gpsimd", dma16(rearr(scr["ewT"])[:, :, i * TT:(i + 1) * TT], ewb[:, :, :]),
                          list(ew_steps))
                    st2 = P.dma("gpsimd", dma16(rearr(scr["csT"])[:, :, i * TT:(i + 1) * TT], csb[:, r, :, :]),
                                [vscan])
                    ring_ewcs[r] = st2
                    ring_ew = stew
                    ring_xs[pr] = md1
                    last_step_of_tile = md1
                elif p != "v":
                    W = WA if (p in ("k", "g")) else WB
                    Wdep = pre["WA"] if (p in ("k", "g")) else pre["WB"]
                    evac_steps = []
                    for oc in range(NCK):
                        def mm_p(t, pr=pr, oc=oc, W=W):
                            last = None
                            for c in range(NCK):
                                last = t.matmul(psp[oc % 2][:, 0:TT], W[:, c, oc * 128:(oc + 1) * 128],
                                                xsb[:, pr, c, :], start=(c == 0), stop=(c == NCK - 1))
                            return last
                        pdeps = xs_steps + [Wdep]
                        if use_psp[oc % 2] is not None:
                            pdeps.append(use_psp[oc % 2])
                        mp = P.step("tensor", mm_p, pdeps)

                        def v_evac(v, oc=oc, r=r):
                            return v.tensor_copy(stage[:, r, oc, :], psp[oc % 2][:, 0:TT])
                        edeps = [mp]
                        if ring_stage[r] is not None:
                            edeps.append(ring_stage[r])
                        ev = P.step("vector", v_evac, edeps)
                        use_psp[oc % 2] = ev
                        evac_steps.append(ev)
                    stp = P.dma("gpsimd", dma16(rearr(out_of[p])[:, :, i * TT:(i + 1) * TT],
                                                stage[:, r, :, :]), evac_steps)
                    ring_stage[r] = stp
                    ring_xs[pr] = evac_steps[-1]
                    last_step_of_tile = evac_steps[-1]
                else:
                    evac_steps = []
                    stage_flat = stage[:, r, :, :].rearrange("p c t -> p (c t)")
                    for ts in range(TT // 128):
                        for half in range(2):
                            def mm_v(t, pr=pr, ts=ts, half=half):
                                last = None
                                for c in range(NCK):
                                    last = t.matmul(psp[half][:, :],
                                                    xsb[:, pr, c, ts * 128:(ts + 1) * 128],
                                                    WB[:, c, half * 512:(half + 1) * 512],
                                                    start=(c == 0), stop=(c == NCK - 1))
                                return last
                            vdeps = xs_steps + [pre["WB"]]
                            if use_psp[half] is not None:
                                vdeps.append(use_psp[half])
                            mv = P.step("tensor", mm_v, vdeps)

                            def v_evacv(v, ts=ts, half=half, stage_flat=stage_flat):
                                base = ts * 1024 + half * 512
                                return v.tensor_copy(stage_flat[:, base:base + 512], psp[half][:, :])
                            edeps = [mv]
                            if ring_stage[r] is not None:
                                edeps.append(ring_stage[r])
                            ev = P.step("vector", v_evacv, edeps)
                            use_psp[half] = ev
                            evac_steps.append(ev)
                    stv = P.dma("gpsimd", dma16(
                        scr["vtok"].ap().rearrange("(n p) d -> p n d", p=128)[:, 2 * i:2 * i + 2, :],
                        stage_flat.rearrange("p (n d) -> p n d", d=1024)),
                        evac_steps)
                    ring_stage[r] = stv
                    ring_xs[pr] = evac_steps[-1]
                    last_step_of_tile = evac_steps[-1]
            ring_xn[r] = last_step_of_tile
            use_dx = xs_steps[-1]
            ring_xxx[r] = xxx_consumers[-1] if xxx_consumers else None
        P.play()



P2_NPAIR = 8
P2_FWDLVL = 3
P2_FWD = True
P2_BWD = True
P2_GN = True


def phase_p2(nc, B):
    inp, scr = B.inp, B.scr
    CH = 128
    NCH = T // CH
    with contextlib.ExitStack() as _es:
        b_r = _es.enter_context(nc.sbuf_tensor("p2_r", [128, T], F32))
        b_k = _es.enter_context(nc.sbuf_tensor("p2_k", [128, T], F32))
        b_cs = _es.enter_context(nc.sbuf_tensor("p2_cs", [128, NCH, CH], F32))
        b_t0 = _es.enter_context(nc.sbuf_tensor("p2_t0", [128, NCH, CH], F32))
        b_t1 = _es.enter_context(nc.sbuf_tensor("p2_t1", [128, T], F32))
        b_rf = _es.enter_context(nc.sbuf_tensor("p2_rf", [128, T], F32))
        b_kf = _es.enter_context(nc.sbuf_tensor("p2_kf", [128, T], F32))
        b_rb = _es.enter_context(nc.sbuf_tensor("p2_rb", [128, T], F32))
        b_kb = _es.enter_context(nc.sbuf_tensor("p2_kb", [128, T], F32))
        b_v = _es.enter_context(nc.sbuf_tensor("p2_v", [128, NCH, CH], F32))
        selbuf = _es.enter_context(nc.sbuf_tensor("p2_sel", [128, 2, 256], F32))
        S_sb = _es.enter_context(nc.sbuf_tensor("p2_S", [128, 64], F32))
        Sb_sb = _es.enter_context(nc.sbuf_tensor("p2_Sb", [128, 64], F32))
        ysum = _es.enter_context(nc.sbuf_tensor("p2_ysum", [2, T], F32))
        ysqs = _es.enter_context(nc.sbuf_tensor("p2_ysqs", [2, T], F32))
        st1 = _es.enter_context(nc.sbuf_tensor("p2_st1", [128, 64], F32))
        st2 = _es.enter_context(nc.sbuf_tensor("p2_st2", [128, 64], F32))
        st3 = _es.enter_context(nc.sbuf_tensor("p2_st3", [128, 64], F32))
        st1x = _es.enter_context(nc.sbuf_tensor("p2_st4", [128, 64], F32))
        identb = _es.enter_context(nc.sbuf_tensor("p2_ident", [128, 128], F32))
        ktf_r = _es.enter_context(nc.sbuf_tensor("p2_ktfr", [128, 2, 128], F32))
        ktb_r = _es.enter_context(nc.sbuf_tensor("p2_ktbr", [128, 2, 128], F32))
        gn2b = _es.enter_context(nc.sbuf_tensor("p2_gn2", [128, 2], F32))
        gnexpb = _es.enter_context(nc.sbuf_tensor("p2_gnexp", [2, 128], F32))
        trib = _es.enter_context(nc.sbuf_tensor("p2_tri", [128, 256], U8))
        lnwb = _es.enter_context(nc.sbuf_tensor("p2_lnw", [128, NCK], F32))
        lnbb = _es.enter_context(nc.sbuf_tensor("p2_lnb", [128, NCK], F32))
        psA = [_es.enter_context(nc.psum_tensor(f"p2_psA{j}", [128, 512], F32)) for j in range(2)]
        psB = [_es.enter_context(nc.psum_tensor(f"p2_psB{j}", [128, 512], F32)) for j in range(2)]
        psY = [_es.enter_context(nc.psum_tensor(f"p2_psY{j}", [128, 512], F32)) for j in range(2)]
        psS = _es.enter_context(nc.psum_tensor("p2_psS", [128, 512], F32))
        psT = _es.enter_context(nc.psum_tensor("p2_psT", [128, 512], F32))

        y_sb = b_r
        csf_flat = None
        ynb = b_k
        ysq = b_t1
        rstd_s = ysum
        mus_s = ysqs

        P = Prog(nc, "p2")
        c_id = P.dma("sync", dma16(identb[:, :], inp["ident"].ap()))
        c_gn2 = P.dma("sync", dma16(gn2b[:, :], inp["gn2"].ap()))
        c_gne = P.dma("sync", dma16(gnexpb[:, :], inp["gnexp"].ap()))
        c_tri = P.dma("sync", dma16(trib[:, :], inp["trimask2"].ap()))
        c_lnw = P.dma("sync", dma16(lnwb[:, :], inp["lnw0"].ap()))
        c_lnb = P.dma("sync", dma16(lnbb[:, :], inp["lnb0"].ap()))
        prev_pair = None

        for hp in range(P2_NPAIR):
            rows = slice(128 * hp, 128 * hp + 128)
            pdeps = [prev_pair] if prev_pair is not None else []
            ld_r = P.dma("sync", dma16(b_r[:, :], scr["rT"].ap()[rows, :]), pdeps)
            ld_k = P.dma("sync", dma16(b_k[:, :], scr["kT"].ap()[rows, :]), pdeps)
            ld_cs = P.dma("sync", dma16(b_cs[:, :, :].rearrange("p c t -> p (c t)"), scr["csT"].ap()[rows, :]), pdeps)
            ld_ew = P.dma("sync", dma16(b_t1[:, :], scr["ewT"].ap()[rows, :]), pdeps)
            ld_v = P.dma("sync", dma16(
                b_v[:, :, :], scr["vtok"].ap().rearrange("(n p) d -> p n d", p=128)[:, :, 128 * hp:128 * hp + 128]),
                pdeps)
            csf = b_cs[:, :, :].rearrange("p c t -> p (c t)")
            t0f = b_t0[:, :, :].rearrange("p c t -> p (c t)")

            def v_csb(v):
                return v.tensor_tensor(t0f, csf, b_t1[:, :], OP.subtract)
            s_csb = P.step("vector", v_csb, [ld_cs, ld_ew] + pdeps)

            def v_cfp(v):
                v.tensor_scalar(b_t1[:, :], csf, csf[:, 2048:2049], 60.0, OP.subtract, OP.min)
                return v.tensor_scalar(b_t1[:, :], b_t1[:, :], -60.0, None, OP.max)
            s_cfp = P.step("vector", v_cfp, [s_csb])

            def v_cbp(v):
                v.tensor_scalar(csf, t0f, t0f[:, 2048:2049], 60.0, OP.subtract, OP.min)
                return v.tensor_scalar(csf, csf, -60.0, None, OP.max)
            s_cbp = P.step("vector", v_cbp, [s_cfp])

            e1 = P.step("scalar", lambda s: s.activation(t0f, b_t1[:, :], AF.Exp, scale=-1.0), [s_cbp])
            m_rf = P.step("vector", lambda v: v.tensor_tensor(b_rf[:, :], b_r[:, :], t0f, OP.mult),
                          [e1, ld_r])
            e2 = P.step("scalar", lambda s: s.activation(t0f, b_t1[:, :], AF.Exp, scale=1.0), [m_rf])
            m_kf = P.step("vector", lambda v: v.tensor_tensor(b_kf[:, :], b_k[:, :], t0f, OP.mult),
                          [e2, ld_k])
            e3 = P.step("scalar", lambda s: s.activation(b_t1[:, :], csf, AF.Exp, scale=1.0), [m_kf])
            m_rb = P.step("vector", lambda v: v.tensor_tensor(b_rb[:, :], b_r[:, :], b_t1[:, :], OP.mult),
                          [e3])
            e4 = P.step("scalar", lambda s: s.activation(b_t1[:, :], csf, AF.Exp, scale=-1.0), [m_rb])
            m_kb = P.step("vector", lambda v: v.tensor_tensor(b_kb[:, :], b_k[:, :], b_t1[:, :], OP.mult),
                          [e4])

            def v_zero(v):
                v.memset(S_sb[:, :], 0.0)
                return v.memset(Sb_sb[:, :], 0.0)
            s_zero = P.step("vector", v_zero, pdeps)
            rfh1 = csf
            kfh1 = t0f
            rbh1 = b_t1
            kbh1 = b_k
            cp1 = P.dma("gpsimd", dma16(rfh1[0:64, :], b_rf[64:128, :]), [m_rf, e3, e4])
            cp2 = P.dma("gpsimd", dma16(kfh1[0:64, :], b_kf[64:128, :]), [m_kf])
            cp3 = P.dma("gpsimd", dma16(rbh1[0:64, :], b_rb[64:128, :]), [m_rb])
            cp4 = P.dma("gpsimd", dma16(kbh1[0:64, :], b_kb[64:128, :]), [m_kb])
            prep = [m_rf, m_kf, m_rb, m_kb, s_zero, ld_v, c_id, c_tri, cp1, cp2, cp3, cp4]

            if not P2_FWD:
                st_yn = P.dma("gpsimd", dma16(scr["ynT"].ap()[rows, :], b_rf[:, :]), prep)
                prev_pair = st_yn
                continue
            TA = [None] * NCH
            VA = [None] * NCH
            TB = [None] * NCH
            VB = [None] * NCH
            for c in range(NCH):
                cc = slice(c * CH, (c + 1) * CH)
                j = c % 2

                def f_TA(t, cc=cc, j=j, c=c):
                    t.matmul(psA[j][:, 0:128], b_kf[0:64, cc], b_rf[0:64, cc], start=True, stop=True)
                    t.matmul(psA[j][:, 128:256], kfh1[0:64, cc], rfh1[0:64, cc], start=True, stop=True)
                    t.matmul(psB[j][:, 0:128], b_kb[0:64, cc], b_rb[0:64, cc], start=True, stop=True)
                    t.matmul(psB[j][:, 128:256], kbh1[0:64, cc], rbh1[0:64, cc], start=True, stop=True)
                    return t.transpose(psT[:, 0:128], b_kf[:, cc], identb[:, :])
                d = list(prep)
                if c >= 1:
                    d.append(VA[c - 1])
                TA[c] = P.step("tensor", f_TA, d)

                def f_VA(v, j=j, c=c):
                    if P2_FWDLVL >= 1:
                        v.select(selbuf[:, c % 2, :], trib[:, :], psA[j][:, 0:256], psB[j][:, 0:256])
                    return v.tensor_copy(ktf_r[:, c % 2, :], psT[:, 0:128])
                d = [TA[c]]
                if c >= 2:
                    d.append(TB[c - 2])
                VA[c] = P.step("vector", f_VA, d)

                if P2_FWDLVL < 2:
                    TB[c] = VA[c]
                    VB[c] = VA[c]
                    continue

                def f_TB(t, cc=cc, j=j, c=c):
                    t.matmul(psY[j][0:64, 0:128], b_v[:, c, 0:64], selbuf[:, c % 2, 0:128],
                             start=True, stop=False)
                    t.matmul(psY[j][0:64, 0:128], S_sb[0:64, :], b_rf[0:64, cc],
                             start=False, stop=True)
                    t.matmul(psY[j][64:128, 0:128], b_v[:, c, 64:128], selbuf[:, c % 2, 128:256],
                             start=True, stop=False, tile_position=(0, 64))
                    t.matmul(psY[j][64:128, 0:128], S_sb[64:128, :], b_rf[64:128, cc],
                             start=False, stop=True, tile_position=(64, 64))
                    t.matmul(psS[0:64, 0:64], ktf_r[:, c % 2, 0:64], b_v[:, c, 0:64],
                             start=True, stop=True)
                    return t.matmul(psS[64:128, 0:64], ktf_r[:, c % 2, 64:128], b_v[:, c, 64:128],
                                    start=True, stop=True, tile_position=(0, 64))
                d = [VA[c]]
                if c >= 1:
                    d.append(VB[c - 1])
                TB[c] = P.step("tensor", f_TB, d)

                if P2_FWDLVL < 3:
                    VB[c] = TB[c]
                    continue

                def f_VB(v, j=j, c=c):
                    v.tensor_copy(y_sb[:, c * CH:(c + 1) * CH], psY[j][:, 0:128])
                    return v.tensor_tensor(S_sb[:, :], S_sb[:, :], psS[:, 0:64], OP.add)
                VB[c] = P.step("vector", f_VB, [TB[c]])

            if not P2_BWD:
                st_yn = P.dma("gpsimd", dma16(scr["ynT"].ap()[rows, :], y_sb[:, :]), [VB[NCH - 1]])
                prev_pair = st_yn
                continue
            TC1 = [None] * NCH
            VC1 = [None] * NCH
            TC2 = [None] * NCH
            VC2 = [None] * NCH
            for c in range(NCH - 1, -1, -1):
                cc = slice(c * CH, (c + 1) * CH)
                j = c % 2

                def f_TC1(t, cc=cc, j=j, c=c):
                    t.matmul(psY[j][0:64, 0:128], Sb_sb[0:64, :], b_rb[0:64, cc],
                             start=True, stop=True)
                    t.matmul(psY[j][64:128, 0:128], Sb_sb[64:128, :], b_rb[64:128, cc],
                             start=True, stop=True, tile_position=(64, 64))
                    return t.transpose(psT[:, 128:256], b_kb[:, cc], identb[:, :])
                if c == NCH - 1:
                    d = [VB[NCH - 1], VA[NCH - 1]]
                else:
                    d = [VC2[c + 1], VC1[c + 1]]
                TC1[c] = P.step("tensor", f_TC1, d)

                def f_VC1(v, c=c):
                    return v.tensor_copy(ktb_r[:, c % 2, :], psT[:, 128:256])
                VC1[c] = P.step("vector", f_VC1, [TC1[c]])

                def f_TC2(t, j=j, c=c):
                    t.matmul(psS[0:64, 64:128], ktb_r[:, c % 2, 0:64], b_v[:, c, 0:64],
                             start=True, stop=True)
                    return t.matmul(psS[64:128, 64:128], ktb_r[:, c % 2, 64:128], b_v[:, c, 64:128],
                                    start=True, stop=True, tile_position=(0, 64))
                TC2[c] = P.step("tensor", f_TC2, [VC1[c]])

                def f_VC2(v, j=j, c=c):
                    v.tensor_tensor(Sb_sb[:, :], Sb_sb[:, :], psS[:, 64:128], OP.add)
                    return v.tensor_tensor(y_sb[:, c * CH:(c + 1) * CH], y_sb[:, c * CH:(c + 1) * CH],
                                           psY[j][:, 0:128], OP.add)
                VC2[c] = P.step("vector", f_VC2, [TC2[c]])

            scan_done = VC2[0]
            if not P2_GN:
                st_yn = P.dma("gpsimd", dma16(scr["ynT"].ap()[rows, :], y_sb[:, :]), [scan_done])
                prev_pair = st_yn
                continue

            g_sq = P.step("scalar", lambda s: s.activation(ysq[:, :], y_sb[:, :], AF.Square), [scan_done])
            ev_list = []
            for jt in range(8):
                ts_ = slice(jt * 512, (jt + 1) * 512)

                def f_gmm(t, ts_=ts_):
                    t.matmul(psA[0][0:2, 0:512], gn2b[:, :], y_sb[:, ts_], start=True, stop=True)
                    return t.matmul(psB[0][0:2, 0:512], gn2b[:, :], ysq[:, ts_], start=True, stop=True)
                d = [g_sq, c_gn2, scan_done]
                if ev_list:
                    d.append(ev_list[-1])
                gmm = P.step("tensor", f_gmm, d)

                def f_gev(v, ts_=ts_):
                    v.tensor_copy(ysum[:, ts_], psA[0][0:2, 0:512])
                    return v.tensor_copy(ysqs[:, ts_], psB[0][0:2, 0:512])
                ev_list.append(P.step("vector", f_gev, [gmm]))

            rs1a = P.dma("gpsimd", dma16(scr["gn_a"].ap(), ysum[:, :]), [ev_list[-1]])
            rs2a = P.dma("gpsimd", dma16(scr["gn_b"].ap(), ysqs[:, :]), [ev_list[-1]])
            rs1 = P.dma("gpsimd", dma16(
                st1[:, :],
                scr["gn_a"].ap().rearrange("h (b c) -> (h b) c", c=64)), [rs1a])
            rs2 = P.dma("gpsimd", dma16(
                st2[:, :],
                scr["gn_b"].ap().rearrange("h (b c) -> (h b) c", c=64)), [rs2a])

            def f_var(v):
                v.tensor_scalar(st1[:, :], st1[:, :], 1.0 / 64.0, None, OP.mult)
                v.tensor_scalar(st2[:, :], st2[:, :], 1.0 / 64.0, None, OP.mult)
                v.tensor_tensor(st3[:, :], st1[:, :], st1[:, :], OP.mult)
                v.tensor_tensor(st2[:, :], st2[:, :], st3[:, :], OP.subtract)
                return v.tensor_scalar(st2[:, :], st2[:, :], GN_EPS, None, OP.add)
            g_var = P.step("vector", f_var, [rs1, rs2])

            def f_rs(s):
                s.activation(st3[:, :], st2[:, :], AF.Ln)
                return s.activation(st3[:, :], st3[:, :], AF.Exp, scale=-0.5)
            g_rs = P.step("scalar", f_rs, [g_var])

            def f_fix(v):
                v.tensor_tensor(st1x[:, :], st3[:, :], st3[:, :], OP.mult)
                v.tensor_tensor(st1x[:, :], st1x[:, :], st2[:, :], OP.mult)
                v.tensor_scalar(st1x[:, :], st1x[:, :], -0.5, 1.5, OP.mult, OP.add)
                v.tensor_tensor(st3[:, :], st3[:, :], st1x[:, :], OP.mult)
                v.tensor_tensor(st1[:, :], st1[:, :], st3[:, :], OP.mult)
                return v.tensor_scalar(st1[:, :], st1[:, :], -1.0, None, OP.mult)
            g_fix = P.step("vector", f_fix, [g_rs])

            bs1a = P.dma("gpsimd", dma16(
                scr["gn_a"].ap().rearrange("h (b c) -> (h b) c", c=64), st3[:, :]), [g_fix, rs1, rs2])
            bs2a = P.dma("gpsimd", dma16(
                scr["gn_b"].ap().rearrange("h (b c) -> (h b) c", c=64), st1[:, :]), [g_fix, rs1, rs2])
            bs1 = P.dma("gpsimd", dma16(rstd_s[:, :], scr["gn_a"].ap()), [bs1a])
            bs2 = P.dma("gpsimd", dma16(mus_s[:, :], scr["gn_b"].ap()), [bs2a])

            app_list = []
            for jt in range(8):
                ts_ = slice(jt * 512, (jt + 1) * 512)

                def f_emm(t, ts_=ts_):
                    t.matmul(psA[1][:, 0:512], gnexpb[:, :], rstd_s[0:2, ts_], start=True, stop=True)
                    return t.matmul(psB[1][:, 0:512], gnexpb[:, :], mus_s[0:2, ts_], start=True, stop=True)
                d = [bs1, bs2, c_gne]
                if app_list:
                    d.append(app_list[-1])
                emm = P.step("tensor", f_emm, d)

                def f_app(v, ts_=ts_, hp=hp):
                    v.tensor_tensor(ynb[:, ts_], y_sb[:, ts_], psA[1][:, 0:512], OP.mult)
                    v.tensor_tensor(ynb[:, ts_], ynb[:, ts_], psB[1][:, 0:512], OP.add)
                    return v.tensor_scalar(ynb[:, ts_], ynb[:, ts_], lnwb[:, hp:hp + 1],
                                           lnbb[:, hp:hp + 1], OP.mult, OP.add)
                app_list.append(P.step("vector", f_app, [emm, c_lnw, c_lnb]))

            st_yn = P.dma("gpsimd", dma16(scr["ynT"].ap()[rows, :], ynb[:, :]), app_list)
            prev_pair = st_yn
        P.play()


def _proj_steps(P, W, Wdep, rhs_of, psp, use_psp, evac_fn, evac_deps_extra, mm_deps, TTn):
    evs = []
    for oc in range(NCK):
        def f_mm(t, oc=oc):
            last = None
            for c in range(NCK):
                last = t.matmul(psp[oc % 2][:, 0:TTn], W[:, c, oc * 128:(oc + 1) * 128],
                                rhs_of(c), start=(c == 0), stop=(c == NCK - 1))
            return last
        d = list(mm_deps) + [Wdep]
        if use_psp[oc % 2] is not None:
            d.append(use_psp[oc % 2])
        mp = P.step("tensor", f_mm, d)
        ev = P.step(evac_fn[0], lambda e, oc=oc: evac_fn[1](e, oc), [mp] + list(evac_deps_extra))
        use_psp[oc % 2] = ev
        evs.append(ev)
    return evs


def phase_p3a(nc, B):
    TTn = 256
    NT = T // TTn
    inp, scr = B.inp, B.scr
    with contextlib.ExitStack() as _es:
        yn = _es.enter_context(nc.sbuf_tensor("p3a_yn", [128, 2, NCK, TTn], F32))
        gp = _es.enter_context(nc.sbuf_tensor("p3a_gp", [128, 2, NCK, TTn], F32))
        gs = _es.enter_context(nc.sbuf_tensor("p3a_gs", [128, NCK, TTn], F32))
        yng = _es.enter_context(nc.sbuf_tensor("p3a_yng", [128, NCK, TTn], F32))
        stage = _es.enter_context(nc.sbuf_tensor("p3a_stage", [128, 2, NCK, TTn], F32))
        gout = _es.enter_context(nc.sbuf_tensor("p3a_gout", [128, 2, NCK, TTn], F32))
        WO = _es.enter_context(nc.sbuf_tensor("p3a_wo", [128, NCK, D], F32))
        psp0 = _es.enter_context(nc.psum_tensor("p3a_psp0", [128, 512], F32))
        psp1 = _es.enter_context(nc.psum_tensor("p3a_psp1", [128, 512], F32))
        psp = [psp0, psp1]
        P = Prog(nc, "p3a")
        wo_ld = P.dma("sync", dma16(WO[:, :, :], inp["woT"].ap().rearrange("(c p) o -> p c o", p=128)))
        ring_ld = [None, None]
        ring_st = [None, None]
        ring_g = [None, None]
        use_psp = [None, None]
        use_gs = None
        use_yng = None
        for i in range(NT):
            r = i % 2
            tt = slice(i * TTn, (i + 1) * TTn)
            d0 = [ring_ld[r]] if ring_ld[r] is not None else []
            ld_yn = P.dma("sync", dma16(yn[:, r, :, :], rearr(scr["ynT"])[:, :, tt]), d0)
            ld_gp = P.dma("sync", dma16(gp[:, r, :, :], rearr(scr["gpreT"])[:, :, tt]), d0)

            s_sig = P.step("scalar", lambda s, r=r: s.activation(gs[:, :, :], gp[:, r, :, :], AF.Sigmoid),
                           [ld_gp] + ([use_gs] if use_gs else []))

            def f_g(v, r=r):
                v.tensor_tensor(gout[:, r, :, :], gp[:, r, :, :], gs[:, :, :], OP.mult)
                return v.tensor_tensor(yng[:, :, :], yn[:, r, :, :], gout[:, r, :, :], OP.mult)
            d = [s_sig, ld_yn]
            if use_yng is not None:
                d.append(use_yng)
            if ring_g[r] is not None:
                d.append(ring_g[r])
            v_g = P.step("vector", f_g, d)
            use_gs = v_g

            st_g = P.dma("gpsimd", dma16(rearr(scr["gT"])[:, :, tt], gout[:, r, :, :]), [v_g])
            ring_g[r] = st_g

            def mk_evac(e, oc, r=r):
                return e.tensor_copy(stage[:, r, oc, :], psp[oc % 2][:, 0:TTn])
            evs = _proj_steps(P, WO, wo_ld, lambda c: yng[:, c, :], psp, use_psp,
                              ("vector", mk_evac), [ring_st[r]] if ring_st[r] is not None else [],
                              [v_g], TTn)
            use_yng = evs[-1]
            st_s0 = P.dma("gpsimd", dma16(rearr(scr["s0T"])[:, :, tt], stage[:, r, :, :]), evs)
            ring_st[r] = st_s0
            ring_ld[r] = evs[-1]
        P.play()


def phase_p3b(nc, B):
    TTn = 256
    NT = T // TTn
    inp, scr = B.inp, B.scr
    with contextlib.ExitStack() as _es:
        xn = _es.enter_context(nc.sbuf_tensor("p3b_xn", [128, 2, NCK, TTn + 1], F32))
        xb = _es.enter_context(nc.sbuf_tensor("p3b_x", [128, 2, NCK, TTn], F32))
        s0 = _es.enter_context(nc.sbuf_tensor("p3b_s0", [128, 2, NCK, TTn], F32))
        db = _es.enter_context(nc.sbuf_tensor("p3b_d", [128, NCK, TTn], F32))
        xr0 = _es.enter_context(nc.sbuf_tensor("p3b_xr0", [128, NCK, TTn], F32))
        xk0 = _es.enter_context(nc.sbuf_tensor("p3b_xk0", [128, NCK, TTn], F32))
        r0 = _es.enter_context(nc.sbuf_tensor("p3b_r0", [128, NCK, TTn], F32))
        k0 = _es.enter_context(nc.sbuf_tensor("p3b_k0", [128, NCK, TTn], F32))
        ffn = _es.enter_context(nc.sbuf_tensor("p3b_ffn", [128, NCK, TTn], F32))
        k0q = xr0
        fft = xk0
        WR = _es.enter_context(nc.sbuf_tensor("p3b_wr", [128, NCK, D], F32))
        WK = _es.enter_context(nc.sbuf_tensor("p3b_wk", [128, NCK, D], F32))
        WV = _es.enter_context(nc.sbuf_tensor("p3b_wv", [128, NCK, D], F32))
        murb = _es.enter_context(nc.sbuf_tensor("p3b_mur", [128, NCK], F32))
        mukb = _es.enter_context(nc.sbuf_tensor("p3b_muk", [128, NCK], F32))
        psp0 = _es.enter_context(nc.psum_tensor("p3b_psp0", [128, 512], F32))
        psp1 = _es.enter_context(nc.psum_tensor("p3b_psp1", [128, 512], F32))
        psp = [psp0, psp1]
        P = Prog(nc, "p3b")
        wld = [P.dma("sync", dma16(WR[:, :, :], inp["wrfT"].ap().rearrange("(c p) o -> p c o", p=128))),
               P.dma("sync", dma16(WK[:, :, :], inp["wkfT"].ap().rearrange("(c p) o -> p c o", p=128))),
               P.dma("sync", dma16(WV[:, :, :], inp["wvfT"].ap().rearrange("(c p) o -> p c o", p=128))),
               P.dma("sync", dma16(murb[:, :], inp["mur"].ap())),
               P.dma("sync", dma16(mukb[:, :], inp["muk"].ap()))]
        ring = [None, None]
        use_psp = [None, None]
        use_mid = None
        for i in range(NT):
            r = i % 2
            tt = slice(i * TTn, (i + 1) * TTn)
            lo = i * TTn - 1
            s_lo = max(lo, 0)
            d_lo = s_lo - lo
            d0 = [ring[r]] if ring[r] is not None else []
            ld_xn = P.dma("sync", dma16(xn[:, r, :, d_lo:TTn + 1],
                                        rearr(scr["xnT"])[:, :, s_lo:(i + 1) * TTn]), d0)
            ld_x = P.dma("sync", dma16(xb[:, r, :, :], rearr(inp["xT"])[:, :, tt]), d0)
            ld_s0 = P.dma("sync", dma16(s0[:, r, :, :], rearr(scr["s0T"])[:, :, tt]), d0)

            def f_mix(v, r=r, i=i):
                if i == 0:
                    v.memset(xn[:, r, :, 0:1], 0.0)
                bx = xn[:, r, :, :]
                v.tensor_tensor(db[:, :, :], bx[:, :, 1:TTn + 1], bx[:, :, 0:TTn], OP.subtract)
                for c in range(NCK):
                    v.scalar_tensor_tensor(xr0[:, c, :], db[:, c, :], murb[:, c:c + 1],
                                           bx[:, c, 0:TTn], OP.mult, OP.add)
                last = None
                for c in range(NCK):
                    last = v.scalar_tensor_tensor(xk0[:, c, :], db[:, c, :], mukb[:, c:c + 1],
                                                  bx[:, c, 0:TTn], OP.mult, OP.add)
                return last
            d = [ld_xn, wld[3], wld[4]]
            if use_mid is not None:
                d.append(use_mid)
            v_mix = P.step("vector", f_mix, d)

            def ev_r0(e, oc):
                return e.activation(r0[:, oc, :], psp[oc % 2][:, 0:TTn], AF.Sigmoid)
            evs_r = _proj_steps(P, WR, wld[0], lambda c: xr0[:, c, :], psp, use_psp,
                                ("scalar", ev_r0), [], [v_mix], TTn)

            def ev_k0(e, oc):
                return e.activation(k0[:, oc, :], psp[oc % 2][:, 0:TTn], AF.Relu)
            evs_k = _proj_steps(P, WK, wld[1], lambda c: xk0[:, c, :], psp, use_psp,
                                ("scalar", ev_k0), [], [v_mix], TTn)

            v_k0q = P.step("vector", lambda v: v.tensor_tensor(k0q[:, :, :], k0[:, :, :], k0[:, :, :], OP.mult),
                           evs_k + evs_r)

            def ev_ffn(e, oc):
                return e.tensor_tensor(ffn[:, oc, :], psp[oc % 2][:, 0:TTn], r0[:, oc, :], OP.mult)
            evs_v = _proj_steps(P, WV, wld[2], lambda c: k0q[:, c, :], psp, use_psp,
                                ("vector", ev_ffn), evs_r, [v_k0q], TTn)

            s_g0 = P.step("scalar", lambda s: s.activation(fft[:, :, :], ffn[:, :, :], AF.Sigmoid),
                          evs_v)

            def f_out(v, r=r):
                v.tensor_tensor(fft[:, :, :], fft[:, :, :], ffn[:, :, :], OP.mult)
                v.tensor_tensor(fft[:, :, :], fft[:, :, :], s0[:, r, :, :], OP.mult)
                return v.tensor_tensor(xn[:, r, :, 0:TTn], xb[:, r, :, :], fft[:, :, :], OP.add)
            v_out = P.step("vector", f_out, [s_g0, ld_x, ld_s0])
            use_mid = v_out

            st = P.dma("gpsimd", dma16(rearr(scr["x1T"])[:, :, tt], xn[:, r, :, 0:TTn]), [v_out])
            ring[r] = st
        P.play()


def phase_p4(nc, B):
    TTn = 256
    NT = T // TTn
    inp, scr = B.inp, B.scr
    with contextlib.ExitStack() as _es:
        xb = _es.enter_context(nc.sbuf_tensor("p4_x", [128, 2, NCK, TTn], F32))
        sq = _es.enter_context(nc.sbuf_tensor("p4_sq", [128, NCK, TTn], F32))
        xn1 = _es.enter_context(nc.sbuf_tensor("p4_xn1", [128, 2, NCK, TTn], F32))
        xn2 = _es.enter_context(nc.sbuf_tensor("p4_xn2", [128, 2, NCK, TTn], F32))
        stage = _es.enter_context(nc.sbuf_tensor("p4_stage", [128, 2, NCK, TTn], F32))
        vstage = _es.enter_context(nc.sbuf_tensor("p4_vstage", [128, 2, 2 * D], F32))
        rows = _es.enter_context(nc.sbuf_tensor("p4_rows", [1, 6 * TTn], F32))
        onescol = _es.enter_context(nc.sbuf_tensor("p4_ones", [128, 1], F32))
        onesrow = _es.enter_context(nc.sbuf_tensor("p4_ones1", [1, 128], F32))
        WQ = _es.enter_context(nc.sbuf_tensor("p4_wq", [128, NCK, D], F32))
        WK = _es.enter_context(nc.sbuf_tensor("p4_wk", [128, NCK, D], F32))
        WV = _es.enter_context(nc.sbuf_tensor("p4_wv", [128, NCK, D], F32))
        nwb = _es.enter_context(nc.sbuf_tensor("p4_nw", [128, NCK], F32))
        bqb = _es.enter_context(nc.sbuf_tensor("p4_bq", [128, NCK], F32))
        bkb = _es.enter_context(nc.sbuf_tensor("p4_bk", [128, NCK], F32))
        bvb = _es.enter_context(nc.sbuf_tensor("p4_bv", [128, NCK], F32))
        ps_s = _es.enter_context(nc.psum_tensor("p4_ps_s", [1, 512], F32))
        ps_q = _es.enter_context(nc.psum_tensor("p4_ps_q", [1, 512], F32))
        ps_mu = _es.enter_context(nc.psum_tensor("p4_ps_mu", [128, 512], F32))
        ps_rs = _es.enter_context(nc.psum_tensor("p4_ps_rs", [128, 512], F32))
        psp0 = _es.enter_context(nc.psum_tensor("p4_psp0", [128, 512], F32))
        psp1 = _es.enter_context(nc.psum_tensor("p4_psp1", [128, 512], F32))
        psp = [psp0, psp1]
        P = Prog(nc, "p4")

        def R(i):
            return rows[0:1, i * TTn:(i + 1) * TTn]

        wld = [P.dma("sync", dma16(WQ[:, :, :], inp["wqT1"].ap().rearrange("(c p) o -> p c o", p=128))),
               P.dma("sync", dma16(WK[:, :, :], inp["wkT1"].ap().rearrange("(c p) o -> p c o", p=128))),
               P.dma("sync", dma16(WV[:, :, :], inp["wvT1"].ap().rearrange("(c p) o -> p c o", p=128))),
               P.dma("sync", dma16(onescol[:, :], inp["ones128"].ap())),
               P.dma("sync", dma16(onesrow[:, :], inp["ones1x128"].ap())),
               P.dma("sync", dma16(nwb[:, :], inp["normw1"].ap())),
               P.dma("sync", dma16(bqb[:, :], inp["bq1"].ap())),
               P.dma("sync", dma16(bkb[:, :], inp["bk1"].ap())),
               P.dma("sync", dma16(bvb[:, :], inp["bv1"].ap()))]
        ring = [None, None]
        use_psp = [None, None]
        use_sq = None
        use_stats = None
        use_bc = None
        for i in range(NT):
            r = i % 2
            tt = slice(i * TTn, (i + 1) * TTn)
            d0 = [ring[r]] if ring[r] is not None else []
            ldx = P.dma("sync", dma16(xb[:, r, :, :], rearr(scr["x1T"])[:, :, tt]), d0)

            def norm_pass(src_ap, dst, eps, vr_extra, tag, i=i, r=r):
                sqst = P.step("scalar", lambda s: s.activation(sq[:, :, :], src_ap, AF.Square),
                              [ldx] + ([use_sq[tag]] if use_sq and use_sq.get(tag) else []))
                return sqst
            sq1 = P.step("scalar", lambda s, r=r: s.activation(sq[:, :, :], xb[:, r, :, :], AF.Square),
                         [ldx] + ([use_sq] if use_sq else []))

            def f_stats1(t, r=r):
                for c in range(NCK):
                    t.matmul(ps_s[0:1, 0:TTn], onescol[:, :], xb[:, r, c, :],
                             start=(c == 0), stop=(c == NCK - 1))
                last = None
                for c in range(NCK):
                    last = t.matmul(ps_q[0:1, 0:TTn], onescol[:, :], sq[:, c, :],
                                    start=(c == 0), stop=(c == NCK - 1))
                return last
            mst1 = P.step("tensor", f_stats1, [ldx, sq1, wld[3]] + ([use_stats] if use_stats else []))

            def f_vst1(v):
                v.tensor_scalar(R(2), ps_s[0:1, 0:TTn], 1.0 / D, None, OP.mult)
                v.tensor_scalar(R(1), ps_q[0:1, 0:TTn], 1.0 / D, None, OP.mult)
                v.tensor_tensor(R(0), R(2), R(2), OP.mult)
                v.tensor_tensor(R(3), R(1), R(0), OP.subtract)
                return v.tensor_scalar(R(3), R(3), LN_EPS, None, OP.add)
            vst1 = P.step("vector", f_vst1, [mst1])
            use_stats = vst1

            def f_rs1(s):
                s.activation(R(5), R(3), AF.Ln)
                return s.activation(R(4), R(5), AF.Exp, scale=-0.5)
            srs1 = P.step("scalar", f_rs1, [vst1])

            def f_nw1(v):
                v.tensor_tensor(R(5), R(4), R(4), OP.mult)
                v.tensor_tensor(R(5), R(5), R(3), OP.mult)
                v.tensor_scalar(R(5), R(5), -0.5, 1.5, OP.mult, OP.add)
                return v.tensor_tensor(R(4), R(4), R(5), OP.mult)
            vnw1 = P.step("vector", f_nw1, [srs1])

            def f_bc1(t):
                t.matmul(ps_mu[:, 0:TTn], onesrow[:, :], R(2), start=True, stop=True)
                return t.matmul(ps_rs[:, 0:TTn], onesrow[:, :], R(4), start=True, stop=True)
            bc1 = P.step("tensor", f_bc1, [vnw1, wld[4]] + ([use_bc] if use_bc else []))

            def f_xn1(v, r=r):
                last = None
                for c in range(NCK):
                    v.tensor_tensor(xn1[:, r, c, :], xb[:, r, c, :], ps_mu[:, 0:TTn], OP.subtract)
                    last = v.tensor_tensor(xn1[:, r, c, :], xn1[:, r, c, :], ps_rs[:, 0:TTn], OP.mult)
                return last
            vxn1 = P.step("vector", f_xn1, [bc1])
            st_xn1 = P.dma("gpsimd", dma16(rearr(scr["xn1T"])[:, :, tt], xn1[:, r, :, :]), [vxn1])

            sq2 = P.step("scalar", lambda s, r=r: s.activation(sq[:, :, :], xn1[:, r, :, :], AF.Square),
                         [vxn1, mst1])
            use_sq = sq2

            def f_stats2(t):
                last = None
                for c in range(NCK):
                    last = t.matmul(ps_q[0:1, 0:TTn], onescol[:, :], sq[:, c, :],
                                    start=(c == 0), stop=(c == NCK - 1))
                return last
            mst2 = P.step("tensor", f_stats2, [sq2, vst1])

            def f_vst2(v):
                v.tensor_scalar(R(3), ps_q[0:1, 0:TTn], 1.0 / D, RMS_EPS, OP.mult, OP.add)
                return v.tensor_scalar(R(3), R(3), 1.0, None, OP.mult)
            vst2 = P.step("vector", f_vst2, [mst2])

            def f_rs2(s):
                s.activation(R(5), R(3), AF.Ln)
                return s.activation(R(4), R(5), AF.Exp, scale=-0.5)
            srs2 = P.step("scalar", f_rs2, [vst2])

            def f_nw2(v):
                v.tensor_tensor(R(5), R(4), R(4), OP.mult)
                v.tensor_tensor(R(5), R(5), R(3), OP.mult)
                v.tensor_scalar(R(5), R(5), -0.5, 1.5, OP.mult, OP.add)
                return v.tensor_tensor(R(4), R(4), R(5), OP.mult)
            vnw2 = P.step("vector", f_nw2, [srs2])

            def f_bc2(t):
                return t.matmul(ps_rs[:, 0:TTn], onesrow[:, :], R(4), start=True, stop=True)
            bc2 = P.step("tensor", f_bc2, [vnw2, vxn1])
            use_bc = None

            def f_xn2(v, r=r):
                last = None
                for c in range(NCK):
                    v.tensor_tensor(xn2[:, r, c, :], xn1[:, r, c, :], ps_rs[:, 0:TTn], OP.mult)
                    last = v.tensor_scalar(xn2[:, r, c, :], xn2[:, r, c, :], nwb[:, c:c + 1],
                                           None, OP.mult)
                return last
            vxn2 = P.step("vector", f_xn2, [bc2, wld[5]])
            use_bc = vxn2
            st_xn2 = P.dma("gpsimd", dma16(rearr(scr["xn2T"])[:, :, tt], xn2[:, r, :, :]), [vxn2])

            def ev_q(e, oc, r=r):
                return e.tensor_scalar(stage[:, r, oc, :], psp[oc % 2][:, 0:TTn],
                                       bqb[:, oc:oc + 1], None, OP.add)
            evq = _proj_steps(P, WQ, wld[0], lambda c, r=r: xn2[:, r, c, :], psp, use_psp,
                              ("vector", ev_q), [wld[6]] + ([ring[r]] if ring[r] is not None else []),
                              [vxn2], TTn)
            st_q = P.dma("gpsimd", dma16(rearr(scr["q1T"])[:, :, tt], stage[:, r, :, :]), evq)

            def ev_k(e, oc, r=r):
                return e.tensor_scalar(stage[:, r, oc, :], psp[oc % 2][:, 0:TTn],
                                       bkb[:, oc:oc + 1], None, OP.add)
            evk = _proj_steps(P, WK, wld[1], lambda c, r=r: xn2[:, r, c, :], psp, use_psp,
                              ("vector", ev_k), [wld[7], st_q], [vxn2], TTn)
            st_k = P.dma("gpsimd", dma16(rearr(scr["k1T"])[:, :, tt], stage[:, r, :, :]), evk)

            evs_v = []
            vsf = vstage[:, r, :]
            for ts_ in range(TTn // 128):
                for half in range(2):
                    def f_mmv(t, ts_=ts_, half=half, r=r):
                        last = None
                        for c in range(NCK):
                            last = t.matmul(psp[half][:, :],
                                            xn2[:, r, c, ts_ * 128:(ts_ + 1) * 128],
                                            WV[:, c, half * 512:(half + 1) * 512],
                                            start=(c == 0), stop=(c == NCK - 1))
                        return last
                    d = [vxn2, wld[2]]
                    if use_psp[half] is not None:
                        d.append(use_psp[half])
                    mv = P.step("tensor", f_mmv, d)

                    def f_evv(v, ts_=ts_, half=half, vsf=vsf):
                        base = ts_ * 1024 + half * 512
                        return v.tensor_copy(vsf[:, base:base + 512], psp[half][:, :])
                    d = [mv, wld[8]]
                    if ring[r] is not None:
                        d.append(ring[r])
                    ev = P.step("vector", f_evv, d)
                    use_psp[half] = ev
                    evs_v.append(ev)
            st_v = P.dma("gpsimd", dma16(
                scr["v1tok"].ap().rearrange("(n p) d -> p n d", p=128)[:, 2 * i:2 * i + 2, :],
                vsf.rearrange("p (n d) -> p n d", d=1024)), evs_v)
            ring[r] = st_v
        P.play()


P5_NH = 16
P5_NI = 8


def phase_p5(nc, B):
    inp, scr = B.inp, B.scr
    NI = P5_NI
    with contextlib.ExitStack() as _es:
        qb = _es.enter_context(nc.sbuf_tensor("p5_q", [64, T], F32))
        kb = _es.enter_context(nc.sbuf_tensor("p5_k", [64, T], F32))
        vb = _es.enter_context(nc.sbuf_tensor("p5_v", [128, 32, 66], F32))
        pb = _es.enter_context(nc.sbuf_tensor("p5_p", [128, 4, 512], F32))
        zz = _es.enter_context(nc.sbuf_tensor("p5_z", [128, 128], F32))
        shb = _es.enter_context(nc.sbuf_tensor("p5_shift", [128, 1], F32))
        dmk = _es.enter_context(nc.sbuf_tensor("p5_dmk", [128, 128], U8))
        dmki = _es.enter_context(nc.sbuf_tensor("p5_dmki", [128, 128], U8))
        yst = _es.enter_context(nc.sbuf_tensor("p5_yst", [66, 2, 512], F32))
        lrow = _es.enter_context(nc.sbuf_tensor("p5_l", [16, T], F32))
        psS = [_es.enter_context(nc.psum_tensor(f"p5_psS{j}", [128, 512], F32)) for j in range(4)]
        psYl = [_es.enter_context(nc.psum_tensor(f"p5_psY{j}", [128, 512], F32)) for j in range(2)]
        P = Prog(nc, "p5")
        c_dm0 = P.dma("sync", dma16(dmk[:, :], inp["dmask"].ap()))
        c_dm = P.dma("sync", dma16(dmki[:, :], inp["dmaski"].ap()))
        c_z = P.step("vector", lambda v: (v.memset(zz[:, :], 0.0),
                                          v.memset(shb[:, :], -EXP_SHIFT))[1])
        prev_head = None
        for h in range(P5_NH):
            hrow = slice(64 * (h % 2), 64 * (h % 2) + 64) if False else None
            pdeps = [prev_head] if prev_head is not None else []
            ld_q = P.dma("sync", dma16(qb[:, :], scr["q1T"].ap()[64 * h:64 * h + 64, :]), pdeps)
            ld_k = P.dma("sync", dma16(kb[:, :], scr["k1T"].ap()[64 * h:64 * h + 64, :]), pdeps)
            ld_v = P.dma("sync", dma16(
                vb[:, :, 0:64],
                scr["v1tok"].ap().rearrange("(n p) d -> p n d", p=128)[:, :, 64 * h:64 * h + 64]), pdeps)
            vload = P.step("vector", lambda v: (v.memset(vb[:, :, 64:65], 1.0),
                                                v.memset(vb[:, :, 65:66], 0.0))[1], [ld_v] + pdeps)
            head_tail = []
            use_ps = [None, None, None, None]
            use_pb = [None, None, None, None]
            use_py = [None, None]
            LA = 4
            seq = [(ib, jc) for ib in range(NI) for jc in range(4 * ib + 4)]
            pready_of = {}
            pv_of = {}
            last_pv_of_ib = {}
            for ib in range(NI):
                last_pv_of_ib[ib] = 4 * ib + 3

            def rec_score(idx):
                ib, jc = seq[idx]
                i0 = ib * 512
                d_off = max(0, jc * 128 - i0)
                jj = idx % 4

                def f_sc(t, jc=jc, d_off=d_off, jj=jj, i0=i0):
                    return t.matmul(psS[jj][:, d_off:512], kb[:, jc * 128:(jc + 1) * 128],
                                    qb[:, i0 + d_off:i0 + 512], start=True, stop=True)
                d = [ld_q, ld_k]
                if use_ps[jj] is not None:
                    d.append(use_ps[jj])
                msc = P.step("tensor", f_sc, d)

                def f_exp(s, d_off=d_off, jj=jj):
                    return s.activation(pb[:, jj, d_off:512], psS[jj][:, d_off:512],
                                        AF.Exp, scale=0.125, bias=shb[:, 0:1])
                d = [msc, c_z]
                if use_pb[jj] is not None:
                    d.append(use_pb[jj])
                sexp = P.step("scalar", f_exp, d)
                use_ps[jj] = sexp
                pready = sexp
                if d_off > 0 or jc * 128 == i0:
                    dd = d_off

                    def f_msk(v, jj=jj, dd=dd):
                        return v.copy_predicated(pb[:, jj, dd:dd + 128], dmki[:, :], zz[:, :])
                    pready = P.step("vector", f_msk, [sexp, c_z, c_dm])
                pready_of[idx] = pready

            def rec_pv(idx):
                ib, jc = seq[idx]
                i0 = ib * 512
                d_off = max(0, jc * 128 - i0)
                jj = idx % 4
                njc = 4 * ib + 4
                psY = psYl[ib % 2]

                def f_pv(t, jc=jc, d_off=d_off, jj=jj, first=(jc == 0), last=(jc == njc - 1), psY=psY):
                    return t.matmul(psY[0:66, d_off:512], vb[:, jc, :], pb[:, jj, d_off:512],
                                    start=first, stop=last)
                d = [pready_of[idx], vload]
                if use_py[ib % 2] is not None:
                    d.append(use_py[ib % 2])
                mpv = P.step("tensor", f_pv, d)
                use_pb[jj] = mpv
                pv_of[idx] = mpv
                if jc == njc - 1:
                    def f_ev(s, ib=ib, psY=psY):
                        s.copy(yst[0:64, ib % 2, :], psY[0:64, 0:512])
                        return s.copy(yst[64:65, ib % 2, :], psY[64:65, 0:512])
                    d = [mpv]
                    if len(head_tail) >= 2:
                        d.append(head_tail[-2])
                    evy = P.step("scalar", f_ev, d)
                    use_py[ib % 2] = evy
                    st_y = P.dma("gpsimd", dma16(scr["attraw"].ap()[64 * h:64 * h + 64, i0:i0 + 512],
                                                 yst[0:64, ib % 2, :]), [evy])
                    st_l = P.dma("gpsimd", dma16(scr["l1s"].ap()[h:h + 1, i0:i0 + 512],
                                                 yst[64:65, ib % 2, :]), [evy])
                    head_tail.append(st_l)

            for idx in range(len(seq) + LA):
                if idx >= LA:
                    rec_pv(idx - LA)
                if idx < len(seq):
                    rec_score(idx)
            prev_head = head_tail[-1]

        if P5_NH < 16:
            P.play()
            return
        ld_l = P.dma("sync", dma16(lrow[:, :], scr["l1s"].ap()), [prev_head])
        e_ln = P.step("scalar", lambda s: (s.activation(lrow[:, :], lrow[:, :], AF.Ln),
                                           s.activation(lrow[:, :], lrow[:, :], AF.Exp, scale=-1.0))[1],
                      [ld_l])
        st_li = P.dma("gpsimd", dma16(scr["linv1"].ap(), lrow[:, :]), [e_ln])
        P.play()


def phase_p6(nc, B):
    TTn = 256
    NT = T // TTn
    inp, scr = B.inp, B.scr
    outT = B.out
    with contextlib.ExitStack() as _es:
        ar = _es.enter_context(nc.sbuf_tensor("p6_ar", [128, 2, NCK, TTn], F32))
        li = _es.enter_context(nc.sbuf_tensor("p6_li", [16, 2, TTn], F32))
        att = _es.enter_context(nc.sbuf_tensor("p6_att", [128, NCK, TTn], F32))
        xn1 = _es.enter_context(nc.sbuf_tensor("p6_xn1", [128, 2, NCK, TTn], F32))
        xn2 = _es.enter_context(nc.sbuf_tensor("p6_xn2", [128, 2, NCK, TTn], F32))
        x1b = _es.enter_context(nc.sbuf_tensor("p6_x1", [128, 2, NCK, TTn], F32))
        s1 = _es.enter_context(nc.sbuf_tensor("p6_s1", [128, NCK, TTn], F32))
        sm = _es.enter_context(nc.sbuf_tensor("p6_sm", [64, TTn], F32))
        hb = _es.enter_context(nc.sbuf_tensor("p6_h", [128, NCK, TTn], F32))
        gsm = _es.enter_context(nc.sbuf_tensor("p6_gsm", [64, TTn], F32))
        of = _es.enter_context(nc.sbuf_tensor("p6_of", [128, NCK, TTn], F32))
        oft = _es.enter_context(nc.sbuf_tensor("p6_oft", [128, NCK, TTn], F32))
        outs = _es.enter_context(nc.sbuf_tensor("p6_out", [128, 2, NCK, TTn], F32))
        WO = _es.enter_context(nc.sbuf_tensor("p6_wo", [128, NCK, D], F32))
        WIN = _es.enter_context(nc.sbuf_tensor("p6_win", [128, NCK, 64], F32))
        WOUT = _es.enter_context(nc.sbuf_tensor("p6_wout", [64, D], F32))
        W1E = _es.enter_context(nc.sbuf_tensor("p6_w1e", [64, NCK * 128], F32))
        W2R = _es.enter_context(nc.sbuf_tensor("p6_w2r", [128, NCK * 64], F32))
        LEX = _es.enter_context(nc.sbuf_tensor("p6_lex", [16, NCK * 128], F32))
        bob = _es.enter_context(nc.sbuf_tensor("p6_bo", [128, NCK], F32))
        binb = _es.enter_context(nc.sbuf_tensor("p6_bin", [64, 1], F32))
        b1b = _es.enter_context(nc.sbuf_tensor("p6_b1", [128, NCK], F32))
        b2b = _es.enter_context(nc.sbuf_tensor("p6_b2", [64, 1], F32))
        boutb = _es.enter_context(nc.sbuf_tensor("p6_bout", [128, NCK], F32))
        psp0 = _es.enter_context(nc.psum_tensor("p6_psp0", [128, 512], F32))
        psp1 = _es.enter_context(nc.psum_tensor("p6_psp1", [128, 512], F32))
        psE = _es.enter_context(nc.psum_tensor("p6_psE", [128, 512], F32))
        psF = _es.enter_context(nc.psum_tensor("p6_psF", [128, 512], F32))
        psp = [psp0, psp1]
        P = Prog(nc, "p6")
        wld = [P.dma("sync", dma16(WO[:, :, :], inp["woT1"].ap().rearrange("(c p) o -> p c o", p=128))),
               P.dma("sync", dma16(WIN[:, :, :], inp["win1"].ap().rearrange("(c p) o -> p c o", p=128))),
               P.dma("sync", dma16(WOUT[:, :], inp["wout1"].ap())),
               P.dma("sync", dma16(W1E[:, :], inp["w1exp"].ap())),
               P.dma("sync", dma16(W2R[:, :], inp["w2red"].ap())),
               P.dma("sync", dma16(LEX[:, :], inp["lexp"].ap())),
               P.dma("sync", dma16(bob[:, :], inp["bo1"].ap())),
               P.dma("sync", dma16(binb[:, :], inp["bin1"].ap())),
               P.dma("sync", dma16(b1b[:, :], inp["b1f"].ap())),
               P.dma("sync", dma16(b2b[:, :], inp["b2c"].ap())),
               P.dma("sync", dma16(boutb[:, :], inp["bout1"].ap()))]
        ring = [None, None]
        use_psp = [None, None]
        use_mid = None
        use_psE = None
        use_psF = None
        for i in range(NT):
            r = i % 2
            tt = slice(i * TTn, (i + 1) * TTn)
            d0 = [ring[r]] if ring[r] is not None else []
            ld_ar = P.dma("sync", dma16(ar[:, r, :, :], rearr(scr["attraw"])[:, :, tt]), d0)
            ld_li = P.dma("sync", dma16(li[:, r, :], scr["linv1"].ap()[:, tt]), d0)
            ld_xn1 = P.dma("sync", dma16(xn1[:, r, :, :], rearr(scr["xn1T"])[:, :, tt]), d0)
            ld_xn2 = P.dma("sync", dma16(xn2[:, r, :, :], rearr(scr["xn2T"])[:, :, tt]), d0)
            ld_x1 = P.dma("sync", dma16(x1b[:, r, :, :], rearr(scr["x1T"])[:, :, tt]), d0)

            att_steps = []
            for c in range(NCK):
                def f_le(t, c=c, r=r):
                    return t.matmul(psE[:, 0:TTn], LEX[:, c * 128:(c + 1) * 128], li[0:16, r, :],
                                    start=True, stop=True)
                d = [ld_li, wld[5]]
                if use_psE is not None:
                    d.append(use_psE)
                mle = P.step("tensor", f_le, d)

                def f_att(v, c=c, r=r):
                    return v.tensor_tensor(att[:, c, :], ar[:, r, c, :], psE[:, 0:TTn], OP.mult)
                d = [mle, ld_ar]
                if use_mid is not None:
                    d.append(use_mid)
                vat = P.step("vector", f_att, d)
                use_psE = vat
                att_steps.append(vat)

            def ev_s1(e, oc, r=r):
                e.tensor_scalar(s1[:, oc, :], psp[oc % 2][:, 0:TTn], bob[:, oc:oc + 1], None, OP.add)
                return e.tensor_tensor(s1[:, oc, :], s1[:, oc, :], xn2[:, r, oc, :], OP.add)
            evs1 = _proj_steps(P, WO, wld[0], lambda c: att[:, c, :], psp, use_psp,
                               ("vector", ev_s1), [wld[6], ld_xn2], att_steps, TTn)

            def f_win(t, r=r):
                last = None
                for c in range(NCK):
                    last = t.matmul(psF[0:64, 0:TTn], WIN[:, c, :], xn1[:, r, c, :],
                                    start=(c == 0), stop=(c == NCK - 1))
                return last
            d = [ld_xn1, wld[1]]
            if use_psF is not None:
                d.append(use_psF)
            mwin = P.step("tensor", f_win, d)
            ev_sm = P.step("vector", lambda v: v.tensor_scalar(sm[:, :], psF[0:64, 0:TTn],
                                                               binb[:, 0:1], None, OP.add),
                           [mwin, wld[7]])
            use_psF = ev_sm

            h_steps = []
            for c in range(NCK):
                def f_ex(t, c=c):
                    return t.matmul(psE[:, 0:TTn], W1E[:, c * 128:(c + 1) * 128], sm[:, :],
                                    start=True, stop=True)
                d = [ev_sm, wld[3]]
                if use_psE is not None:
                    d.append(use_psE)
                mex = P.step("tensor", f_ex, d)
                s_h = P.step("scalar", lambda s, c=c: s.activation(hb[:, c, :], psE[:, 0:TTn],
                                                                   AF.Relu, bias=b1b[:, c:c + 1]),
                             [mex, wld[8]])
                use_psE = s_h
                h_steps.append(s_h)

            def f_red(t):
                last = None
                for c in range(NCK):
                    last = t.matmul(psF[0:64, 0:TTn], W2R[:, c * 64:(c + 1) * 64], hb[:, c, :],
                                    start=(c == 0), stop=(c == NCK - 1))
                return last
            mred = P.step("tensor", f_red, h_steps + [wld[4], ev_sm])
            ev_gsm = P.step("vector", lambda v: v.tensor_scalar(gsm[:, :], psF[0:64, 0:TTn],
                                                                b2b[:, 0:1], None, OP.add),
                            [mred, wld[9]])
            use_psF = ev_gsm

            def ev_of(e, oc):
                return e.tensor_scalar(of[:, oc, :], psp[oc % 2][:, 0:TTn], boutb[:, oc:oc + 1],
                                       None, OP.add)
            def wout_mm():
                evs = []
                for oc in range(NCK):
                    def f_mm(t, oc=oc):
                        return t.matmul(psp[oc % 2][:, 0:TTn], WOUT[:, oc * 128:(oc + 1) * 128],
                                        gsm[:, :], start=True, stop=True)
                    d = [ev_gsm, wld[2]]
                    if use_psp[oc % 2] is not None:
                        d.append(use_psp[oc % 2])
                    mp = P.step("tensor", f_mm, d)
                    ev = P.step("vector", lambda e, oc=oc: ev_of(e, oc), [mp, wld[10]])
                    use_psp[oc % 2] = ev
                    evs.append(ev)
                return evs
            evs_of = wout_mm()

            s_sig = P.step("scalar", lambda s: s.activation(oft[:, :, :], of[:, :, :], AF.Sigmoid),
                           evs_of)

            def f_out(v, r=r):
                v.tensor_tensor(oft[:, :, :], oft[:, :, :], of[:, :, :], OP.mult)
                v.tensor_tensor(oft[:, :, :], oft[:, :, :], s1[:, :, :], OP.mult)
                return v.tensor_tensor(outs[:, r, :, :], x1b[:, r, :, :], oft[:, :, :], OP.add)
            v_out = P.step("vector", f_out, [s_sig, ld_x1] + evs1)
            use_mid = v_out

            st = P.dma("gpsimd", dma16(rearr(outT)[:, :, tt], outs[:, r, :, :]), [v_out])
            ring[r] = st
        P.play()


def build(debug=(), phases=(1, 2, 3, 4, 5, 6, 7), preload=()):
    nc = bass.Bass("TRN2", target_bir_lowering=False, detect_race_conditions=False)
    B = Builder(nc, debug, preload)

    B.dram_in("xT", [D, T])
    B.dram_in("wrT", [D, D]); B.dram_in("wkT", [D, D]); B.dram_in("wvT", [D, D])
    B.dram_in("wgT", [D, D]); B.dram_in("woT", [D, D])
    B.dram_in("maa_w1", [D, 160])
    B.dram_in("w2h", [32, 5 * D])
    B.dram_in("c5col", [128, 5 * NCK])
    B.dram_in("tmx", [128, NCK]); B.dram_in("tdec", [128, NCK])
    B.dram_in("dw1", [D, 64]); B.dram_in("dw2", [64, D])
    B.dram_in("lnw0", [128, NCK]); B.dram_in("lnb0", [128, NCK])
    B.dram_in("wrfT", [D, D]); B.dram_in("wkfT", [D, D]); B.dram_in("wvfT", [D, D])
    B.dram_in("mur", [128, NCK]); B.dram_in("muk", [128, NCK])
    B.dram_in("normw1", [128, NCK])
    B.dram_in("wqT1", [D, D]); B.dram_in("wkT1", [D, D]); B.dram_in("wvT1", [D, D]); B.dram_in("woT1", [D, D])
    B.dram_in("bq1", [128, NCK]); B.dram_in("bk1", [128, NCK]); B.dram_in("bv1", [128, NCK]); B.dram_in("bo1", [128, NCK])
    B.dram_in("win1", [D, 64]); B.dram_in("bin1", [64, 1])
    B.dram_in("w1exp", [64, NCK * 128]); B.dram_in("b1f", [128, NCK])
    B.dram_in("w2red", [128, NCK * 64]); B.dram_in("b2c", [64, 1])
    B.dram_in("wout1", [64, D]); B.dram_in("bout1", [128, NCK])
    B.dram_in("ones128", [128, 1]); B.dram_in("ones1x128", [1, 128])
    B.dram_in("gn2", [128, 2]); B.dram_in("gnexp", [2, 128]); B.dram_in("ident", [128, 128])
    B.dram_in("lexp", [16, NCK * 128])
    B.dram_in("trimask2", [128, 256], U8)
    B.dram_in("dmask", [128, 128], U8); B.dram_in("dmaski", [128, 128], U8)

    for nm, shp in [
        ("xnT", [D, T]), ("xxxT", [32, 5 * T]), ("rT", [D, T]), ("kT", [D, T]),
        ("gpreT", [D, T]), ("ewT", [D, T]), ("csT", [D, T]), ("vtok", [T, D]),
        ("ynT", [D, T]), ("x1T", [D, T]), ("xn1T", [D, T]), ("xn2T", [D, T]),
        ("q1T", [D, T]), ("k1T", [D, T]), ("v1tok", [T, D]), ("attraw", [D, T]),
        ("l1s", [16, T]), ("linv1", [16, T]), ("gn_a", [2, T]), ("gn_b", [2, T]),
        ("gT", [D, T]), ("s0T", [D, T]),
    ]:
        B.scratch(nm, shp)

    outT = nc.dram_tensor("outT", [D, T], F32, kind="ExternalOutput")
    B.out = outT

    if 1 in phases:
        phase_p1a(nc, B)
    if 2 in phases:
        phase_p1b(nc, B, 1)
        phase_p1b(nc, B, 2)
    if 3 in phases:
        phase_p2(nc, B)
    if 4 in phases:
        phase_p3a(nc, B)
        phase_p3b(nc, B)
    if 5 in phases:
        phase_p4(nc, B)
    if 6 in phases:
        phase_p5(nc, B)
    if 7 in phases:
        phase_p6(nc, B)
    return nc


def _host_consts(inputs):
    g = inputs
    d = {}
    d["wrT"] = g["L0_attn_W_r"].T
    d["wkT"] = g["L0_attn_W_k"].T
    d["wvT"] = g["L0_attn_W_v"].T
    d["wgT"] = g["L0_attn_W_g"].T
    d["woT"] = g["L0_attn_W_o"].T
    d["maa_w1"] = g["L0_attn_time_maa_w1"]
    w2 = g["L0_attn_time_maa_w2"]
    d["w2h"] = np.concatenate([w2[f] for f in range(5)], axis=1)
    c5 = np.stack([g["L0_attn_time_maa_w"].reshape(D), g["L0_attn_time_maa_k"].reshape(D),
                   g["L0_attn_time_maa_v"].reshape(D), g["L0_attn_time_maa_r"].reshape(D),
                   g["L0_attn_time_maa_g"].reshape(D)])
    col = lambda v: np.ascontiguousarray(np.asarray(v).reshape(NCK, 128).T)
    d["c5col"] = np.concatenate([col(c5[f]) for f in range(5)], axis=1)
    d["tmx"] = col(g["L0_attn_time_maa_x"])
    d["tdec"] = col(g["L0_attn_time_decay"])
    d["dw1"] = g["L0_attn_time_decay_w1"]
    d["dw2"] = g["L0_attn_time_decay_w2"]
    d["lnw0"] = col(g["L0_attn_ln_w"])
    d["lnb0"] = col(g["L0_attn_ln_b"])
    d["wrfT"] = g["L0_ffn_W_r"].T
    d["wkfT"] = g["L0_ffn_W_k"].T
    d["wvfT"] = g["L0_ffn_W_v"].T
    d["mur"] = col(g["L0_ffn_mu_r"])
    d["muk"] = col(g["L0_ffn_mu_k"])
    d["normw1"] = col(g["L1_attn_norm_w"])
    d["wqT1"] = g["L1_attn_Wq"].T
    d["wkT1"] = g["L1_attn_Wk"].T
    d["wvT1"] = g["L1_attn_Wv"].T
    d["woT1"] = g["L1_attn_Wo"].T
    d["bq1"] = col(g["L1_attn_bq"])
    d["bk1"] = col(g["L1_attn_bk"])
    d["bv1"] = col(g["L1_attn_bv"])
    d["bo1"] = col(g["L1_attn_bo"])
    d["win1"] = g["L1_ffn_Win"]
    d["bin1"] = np.asarray(g["L1_ffn_bin_"]).reshape(64, 1)
    W1 = np.asarray(g["L1_ffn_W1"])
    w1exp = np.zeros((64, NCK * 128), dtype=np.float32)
    for c in range(NCK):
        for p in range(128):
            m = 8 * c + p // 16
            w1exp[m, c * 128 + p] = W1[m, p % 16]
    d["w1exp"] = w1exp
    d["b1f"] = col(np.asarray(g["L1_ffn_b1"]).reshape(D))
    W2 = np.asarray(g["L1_ffn_W2"])
    w2red = np.zeros((128, NCK * 64), dtype=np.float32)
    for c in range(NCK):
        for p in range(128):
            m = 8 * c + p // 16
            w2red[p, c * 64 + m] = W2[m, p % 16]
    d["w2red"] = w2red
    d["b2c"] = np.asarray(g["L1_ffn_b2"]).reshape(64, 1)
    d["wout1"] = g["L1_ffn_Wout"]
    d["bout1"] = col(g["L1_ffn_bout"])
    d["ones128"] = np.ones((128, 1), dtype=np.float32)
    d["ones1x128"] = np.ones((1, 128), dtype=np.float32)
    gn2 = np.zeros((128, 2), dtype=np.float32)
    gn2[0:64, 0] = 1.0
    gn2[64:128, 1] = 1.0
    d["gn2"] = gn2
    gnexp = np.zeros((2, 128), dtype=np.float32)
    gnexp[0, 0:64] = 1.0
    gnexp[1, 64:128] = 1.0
    d["gnexp"] = gnexp
    d["ident"] = np.eye(128, dtype=np.float32)
    lexp = np.zeros((16, NCK * 128), dtype=np.float32)
    for c in range(NCK):
        for p in range(128):
            lexp[2 * c + p // 64, c * 128 + p] = 1.0
    d["lexp"] = lexp
    tri = (np.arange(128)[None, :] >= np.arange(128)[:, None]).astype(np.uint8)
    d["trimask2"] = np.tile(tri, (1, 2))
    d["dmask"] = tri
    d["dmaski"] = (1 - tri).astype(np.uint8)
    out = {}
    for k, v in d.items():
        v = np.asarray(v)
        if v.dtype == np.uint8:
            out[k] = np.ascontiguousarray(v)
        else:
            out[k] = np.ascontiguousarray(v.astype(np.float32))
    return out


def kernel(**inputs):
    consts = _host_consts(inputs)
    nc = build()
    x = np.asarray(inputs["L0_attn_x"])
    in_maps = []
    for b in range(8):
        m = dict(consts)
        m["xT"] = np.ascontiguousarray(x[b].T)
        in_maps.append(m)
    res = run_bass_kernel_spmd(nc, in_maps, core_ids=list(range(8)))
    out = np.stack([res.results[b]["outT"].T for b in range(8)], axis=0)
    return np.ascontiguousarray(out.astype(np.float32))


# version 08
# speedup vs baseline: 1.0263x; 1.0263x over previous
import numpy as np
import contextlib
import concourse.bass as bass
import concourse.mybir as mybir
from concourse.bass_utils import run_bass_kernel_spmd
from concourse.alu_op_type import AluOpType as OP

F32 = mybir.dt.float32
BF16 = mybir.dt.bfloat16
U8 = mybir.dt.uint8
AF = mybir.ActivationFunctionType

T = 4096
D = 1024
NCK = 8
H = 16
HD = 64
EXP_SHIFT = 12.0

LN_EPS = 1e-5
GN_EPS = 1e-5 * 64.0
RMS_EPS = 1e-6

ENGS = ("sync", "tensor", "vector", "scalar", "gpsimd")


class Prog:

    def __init__(self, nc, name):
        self.nc = nc
        self.name = name
        self.steps = []
        self.cum = []

    def step(self, engine, fn, deps=(), inc=1):
        assert engine in ENGS
        for d in deps:
            assert 0 <= d < len(self.steps), (d, len(self.steps))
        cum_prev = 0
        for i in range(len(self.steps) - 1, -1, -1):
            if self.steps[i][0] == engine:
                cum_prev = self.cum[i]
                break
        self.steps.append([engine, fn, inc, tuple(deps)])
        self.cum.append(cum_prev + inc)
        return len(self.steps) - 1

    def dma(self, engine, fn, deps=()):
        return self.step(engine, fn, deps, inc=16)

    def play(self):
        nc = self.nc
        with nc.Block() as blk:
            sems = {}
            ctx = []
            for e in ENGS:
                if any(s[0] == e for s in self.steps):
                    cm = nc.semaphore(f"{self.name}_{e}")
                    sems[e] = cm.__enter__()
                    ctx.append(cm)
            watermark = {}

            def play_engine(eng_name, eng):
                dma_cum = 0
                for idx, (e, fn, inc, deps) in enumerate(self.steps):
                    if e != eng_name:
                        continue
                    if inc == 16:
                        if dma_cum > 0:
                            eng.wait_ge(sems[eng_name], dma_cum)
                        dma_cum = self.cum[idx]
                    for d in deps:
                        de, _, _, _ = self.steps[d]
                        need = self.cum[d]
                        key = (eng_name, de)
                        if watermark.get(key, -1) >= need:
                            continue
                        watermark[key] = need
                        eng.wait_ge(sems[de], need)
                    ret = fn(eng)
                    if isinstance(ret, int):
                        assert ret == inc, (idx, ret, inc)
                    elif ret is None:
                        raise RuntimeError(f"step {idx} returned None; must return last instruction")
                    else:
                        ret.then_inc(sems[eng_name], inc)

            for e in ENGS:
                if e not in sems:
                    continue
                if e == "sync":
                    @blk.sync
                    def _(s, e=e):
                        play_engine(e, s)
                elif e == "tensor":
                    @blk.tensor
                    def _(s, e=e):
                        play_engine(e, s)
                elif e == "vector":
                    @blk.vector
                    def _(s, e=e):
                        play_engine(e, s)
                elif e == "scalar":
                    @blk.scalar
                    def _(s, e=e):
                        play_engine(e, s)
                elif e == "gpsimd":
                    @blk.gpsimd
                    def _(s, e=e):
                        play_engine(e, s)
            for cm in ctx:
                cm.__exit__(None, None, None)


def dma16(ap_out, ap_in):
    def fn(eng):
        return eng.dma_start(ap_out, ap_in)
    return fn


def rearr(dram, p=128):
    return dram.ap().rearrange("(c p) t -> p c t", p=p)


class Builder:
    def __init__(self, nc: bass.Bass, debug=(), preload=()):
        self.nc = nc
        self.debug = set(debug)
        self.preload = set(preload)
        self.inp = {}
        self.scr = {}

    def dram_in(self, name, shape, dtype=F32):
        h = self.nc.dram_tensor(name, shape, dtype, kind="ExternalInput")
        self.inp[name] = h
        return h

    def scratch(self, name, shape):
        kind = "ExternalOutput" if name in self.debug else "Internal"
        if name in self.preload:
            kind = "ExternalInput"
        h = self.nc.dram_tensor(name, shape, F32, kind=kind)
        self.scr[name] = h
        return h


def phase_p1a(nc, B):
    TT = 512
    NT = T // TT
    inp, scr = B.inp, B.scr
    with (
        nc.sbuf_tensor("p1a_x", [128, 2, NCK, TT], F32) as xb,
        nc.sbuf_tensor("p1a_sq", [128, NCK, TT], F32) as sqb,
        nc.sbuf_tensor("p1a_xn", [128, 2, NCK, TT], F32) as xnb,
        nc.sbuf_tensor("p1a_ones", [128, 1], F32) as onescol,
        nc.sbuf_tensor("p1a_ones1", [1, 128], F32) as onesrow,
        nc.sbuf_tensor("p1a_rows", [1, 6 * TT], F32) as rows,
        nc.psum_tensor("p1a_ps_s", [1, 2 * TT], F32) as ps_s,
        nc.psum_tensor("p1a_ps_q", [1, 2 * TT], F32) as ps_q,
        nc.psum_tensor("p1a_ps_mu", [128, TT], F32) as ps_mu,
        nc.psum_tensor("p1a_ps_rs", [128, TT], F32) as ps_rs,
    ):
        P = Prog(nc, "p1a")

        def R(i):
            return rows[0:1, i * TT:(i + 1) * TT]

        c_ones = P.dma("sync", dma16(onescol[:, :], inp["ones128"].ap()))
        c_ones1 = P.dma("sync", dma16(onesrow[:, :], inp["ones1x128"].ap()))

        last_use = [None, None]
        bc_use = None
        stat_use = None
        use_sq = None
        for i in range(NT):
            r = i % 2
            deps = [last_use[r]] if last_use[r] is not None else []
            ldx = P.dma("sync", dma16(xb[:, r, :, :], rearr(inp["xT"])[:, :, i * TT:(i + 1) * TT]), deps)
            sq = P.step("scalar", lambda s, r=r: s.activation(sqb[:, :, :], xb[:, r, :, :], AF.Square),
                        [ldx] + ([use_sq] if use_sq else []))

            def mm_stats(t, r=r, i=i):
                p = (i % 2) * TT
                for c in range(NCK):
                    t.matmul(ps_s[0:1, p:p + TT], onescol[:, :], xb[:, r, c, :],
                             start=(c == 0), stop=(c == NCK - 1))
                last = None
                for c in range(NCK):
                    last = t.matmul(ps_q[0:1, p:p + TT], onescol[:, :], sqb[:, c, :],
                                    start=(c == 0), stop=(c == NCK - 1))
                return last
            mst = P.step("tensor", mm_stats, [ldx, sq, c_ones] + ([stat_use] if stat_use else []))
            use_sq = mst

            def v_stats(v, i=i):
                p = (i % 2) * TT
                v.tensor_scalar(R(2), ps_s[0:1, p:p + TT], 1.0 / D, None, OP.mult)
                v.tensor_scalar(R(1), ps_q[0:1, p:p + TT], 1.0 / D, None, OP.mult)
                v.tensor_tensor(R(0), R(2), R(2), OP.mult)
                v.tensor_tensor(R(3), R(1), R(0), OP.subtract)
                return v.tensor_scalar(R(3), R(3), LN_EPS, None, OP.add)
            vst = P.step("vector", v_stats, [mst])
            stat_use = vst

            def s_rstd(s):
                s.activation(R(5), R(3), AF.Ln)
                return s.activation(R(4), R(5), AF.Exp, scale=-0.5)
            srs = P.step("scalar", s_rstd, [vst])

            def v_newton(v):
                v.tensor_tensor(R(5), R(4), R(4), OP.mult)
                v.tensor_tensor(R(5), R(5), R(3), OP.mult)
                v.tensor_scalar(R(5), R(5), -0.5, 1.5, OP.mult, OP.add)
                return v.tensor_tensor(R(4), R(4), R(5), OP.mult)
            vnw = P.step("vector", v_newton, [srs])

            def mm_bcast(t):
                t.matmul(ps_mu[:, :], onesrow[:, :], R(2), start=True, stop=True)
                return t.matmul(ps_rs[:, :], onesrow[:, :], R(4), start=True, stop=True)
            bc = P.step("tensor", mm_bcast, [vnw, c_ones1] + ([bc_use] if bc_use else []))

            def v_xn(v, r=r):
                last = None
                for c in range(NCK):
                    v.tensor_tensor(xnb[:, r, c, :], xb[:, r, c, :], ps_mu[:, :], OP.subtract)
                    last = v.tensor_tensor(xnb[:, r, c, :], xnb[:, r, c, :], ps_rs[:, :], OP.mult)
                return last
            vxn = P.step("vector", v_xn, [bc])
            bc_use = vxn

            st = P.dma("gpsimd", dma16(rearr(scr["xnT"])[:, :, i * TT:(i + 1) * TT], xnb[:, r, :, :]), [vxn])
            last_use[r] = st
        P.play()



def phase_p1b(nc, B, part):
    TT = 256
    NT = T // TT
    inp, scr = B.inp, B.scr
    projs = ["w", "k", "r"] if part == 1 else ["g", "v"]
    f_of = {"w": 0, "k": 1, "v": 2, "r": 3, "g": 4}
    out_of = {"k": scr["kT"], "r": scr["rT"], "g": scr["gpreT"]}
    pfx = f"p1b{part}"
    with contextlib.ExitStack() as _es:
        xnb = _es.enter_context(nc.sbuf_tensor(f"{pfx}_xn", [128, 2, NCK, TT + 2], F32))
        dxb = _es.enter_context(nc.sbuf_tensor(f"{pfx}_dx", [128, NCK, TT], F32))
        mixb = _es.enter_context(nc.sbuf_tensor(f"{pfx}_mix", [128, NCK, TT] if part == 1 else [1, 1], F32))
        xxxb = _es.enter_context(nc.sbuf_tensor(f"{pfx}_xxx", [32, 2, 5, TT], F32))
        xsb = _es.enter_context(nc.sbuf_tensor(f"{pfx}_xs", [128, 2, NCK, TT], F32))
        stage = _es.enter_context(nc.sbuf_tensor(f"{pfx}_stage", [128, 2, NCK, TT], F32))
        w1b = _es.enter_context(nc.sbuf_tensor(f"{pfx}_w1", [128, NCK, 160] if part == 1 else [1, 1], F32))
        w2b = _es.enter_context(nc.sbuf_tensor(f"{pfx}_w2h", [32, 5 * D], F32))
        c5b = _es.enter_context(nc.sbuf_tensor(f"{pfx}_c5", [128, 5 * NCK], F32))
        tmxb = _es.enter_context(nc.sbuf_tensor(f"{pfx}_tmx", [128, NCK] if part == 1 else [1, 1], F32))
        WA = _es.enter_context(nc.sbuf_tensor(f"{pfx}_WA", [128, NCK, D], F32))
        WB = _es.enter_context(nc.sbuf_tensor(f"{pfx}_WB", [128, NCK, D], F32))
        dw1b = _es.enter_context(nc.sbuf_tensor(f"{pfx}_dw1", [128, NCK, 64] if part == 1 else [1, 1], F32))
        dw2b = _es.enter_context(nc.sbuf_tensor(f"{pfx}_dw2", [64, D] if part == 1 else [1, 1], F32))
        tdecb = _es.enter_context(nc.sbuf_tensor(f"{pfx}_tdec", [128, NCK] if part == 1 else [1, 1], F32))
        tdb = _es.enter_context(nc.sbuf_tensor(f"{pfx}_td", [64, TT] if part == 1 else [1, 1], F32))
        ewb = _es.enter_context(nc.sbuf_tensor(f"{pfx}_ew", [128, NCK, TT] if part == 1 else [1, 1], F32))
        csb = _es.enter_context(nc.sbuf_tensor(f"{pfx}_cs", [128, 2, NCK, TT] if part == 1 else [1, 1], F32))
        psx1 = _es.enter_context(nc.psum_tensor(f"{pfx}_psx1", [96, 512], F32))
        psx2 = _es.enter_context(nc.psum_tensor(f"{pfx}_psx2", [64, 512], F32))
        psm0 = _es.enter_context(nc.psum_tensor(f"{pfx}_psm0", [128, 512], F32))
        psm1 = _es.enter_context(nc.psum_tensor(f"{pfx}_psm1", [128, 512], F32))
        psp0 = _es.enter_context(nc.psum_tensor(f"{pfx}_psp0", [128, 512], F32))
        psp1 = _es.enter_context(nc.psum_tensor(f"{pfx}_psp1", [128, 512], F32))
        psd = _es.enter_context(nc.psum_tensor(f"{pfx}_psd", [64, 512] if part == 1 else [1, 1], F32))
        psm = [psm0, psm1]
        psp = [psp0, psp1]
        P = Prog(nc, pfx)

        pre = {}
        pre["w1"] = P.dma("sync", dma16(w1b[:, :, :], inp["maa_w1"].ap().rearrange("(c p) o -> p c o", p=128))) if part == 1 else None
        pre["w2"] = P.dma("sync", dma16(w2b[:, :], inp["w2h"].ap()))
        pre["c5"] = P.dma("sync", dma16(c5b[:, :], inp["c5col"].ap()))
        pre["tmx"] = P.dma("sync", dma16(tmxb[:, :], inp["tmx"].ap())) if part == 1 else None
        WA_src = inp["wkT"] if part == 1 else inp["wgT"]
        WB_src = inp["wrT"] if part == 1 else inp["wvT"]
        pre["WA"] = P.dma("sync", dma16(WA[:, :, :], WA_src.ap().rearrange("(c p) o -> p c o", p=128)))
        pre["WB"] = P.dma("sync", dma16(WB[:, :, :], WB_src.ap().rearrange("(c p) o -> p c o", p=128)))
        if part == 1:
            pre["dw1"] = P.dma("sync", dma16(dw1b[:, :, :], inp["dw1"].ap().rearrange("(c p) o -> p c o", p=128)))
            pre["dw2"] = P.dma("sync", dma16(dw2b[:, :], inp["dw2"].ap()))
            pre["tdec"] = P.dma("sync", dma16(tdecb[:, :], inp["tdec"].ap()))
        ring_xn = [None, None]
        ring_ew = None
        ring_xs = [None, None]
        ring_stage = [None, None]
        ring_ewcs = [None, None]
        ring_xxx = [None, None]
        use_mix = None
        use_dx = None
        use_psx = None
        use_psm = [None, None]
        use_psp = [None, None]
        use_psd = None
        prev_scan = None

        for i in range(NT):
            r = i % 2
            lo, hi = i * TT - 1, (i + 1) * TT + 1
            s_lo, s_hi = max(lo, 0), min(hi, T)
            d_lo = s_lo - lo
            d_hi = d_lo + (s_hi - s_lo)
            deps = [ring_xn[r]] if ring_xn[r] is not None else []
            ldx = P.dma("sync", dma16(xnb[:, r, :, d_lo:d_hi], rearr(scr["xnT"])[:, :, s_lo:s_hi]), deps)
            if part == 2:
                xdeps = [ring_xxx[r]] if ring_xxx[r] is not None else []
                ldxxx = P.dma("sync", dma16(
                    xxxb[0:32, r, :, :],
                    scr["xxxT"].ap().rearrange("p (f t) -> p f t", f=5)[:, :, i * TT:(i + 1) * TT]), xdeps)

            def v_prep(v, r=r, i=i):
                bx = xnb[:, r, :, :]
                if i == 0:
                    v.memset(xnb[:, r, :, 0:1], 0.0)
                if i == NT - 1:
                    v.memset(xnb[:, r, :, TT + 1:TT + 2], 0.0)
                v.tensor_tensor(dxb[:, :, :], bx[:, :, 0:TT], bx[:, :, 2:TT + 2], OP.add)
                last = v.scalar_tensor_tensor(dxb[:, :, :], dxb[:, :, :], 0.5, bx[:, :, 1:TT + 1],
                                              OP.mult, OP.subtract)
                if part == 1:
                    for c in range(NCK):
                        last = v.scalar_tensor_tensor(mixb[:, c, :], dxb[:, c, :], tmxb[:, c:c + 1],
                                                      bx[:, c, 1:TT + 1], OP.mult, OP.add)
                return last
            vp_deps = [ldx]
            if part == 1:
                vp_deps.append(pre["tmx"])
                if use_mix is not None:
                    vp_deps.append(use_mix)
            if ring_xn[r] is not None:
                vp_deps.append(ring_xn[r])
            if use_dx is not None:
                vp_deps.append(use_dx)
            vprep = P.step("vector", v_prep, vp_deps)

            if part == 1:
                def mm_maa1(t):
                    for c in range(NCK):
                        t.matmul(psx1[:, 0:TT], w1b[:, c, 0:96], mixb[:, c, :],
                                 start=(c == 0), stop=(c == NCK - 1))
                    last = None
                    for c in range(NCK):
                        last = t.matmul(psx2[:, 0:TT], w1b[:, c, 96:160], mixb[:, c, :],
                                        start=(c == 0), stop=(c == NCK - 1))
                    return last
                mmx = P.step("tensor", mm_maa1, [vprep, pre["w1"]] + ([use_psx] if use_psx else []))
                use_mix = mmx

                def s_xxx(s, r=r):
                    s.activation(xxxb[0:32, r, 0, :], psx1[0:32, 0:TT], AF.Tanh)
                    s.activation(xxxb[0:32, r, 1, :], psx1[32:64, 0:TT], AF.Tanh)
                    s.activation(xxxb[0:32, r, 2, :], psx1[64:96, 0:TT], AF.Tanh)
                    s.activation(xxxb[0:32, r, 3, :], psx2[0:32, 0:TT], AF.Tanh)
                    return s.activation(xxxb[0:32, r, 4, :], psx2[32:64, 0:TT], AF.Tanh)
                sx_deps = [mmx]
                if ring_xxx[r] is not None:
                    sx_deps.append(ring_xxx[r])
                sxxx = P.step("scalar", s_xxx, sx_deps)
                use_psx = sxxx
                stxxx = P.dma("gpsimd", dma16(
                    scr["xxxT"].ap().rearrange("p (f t) -> p f t", f=5)[:, :, i * TT:(i + 1) * TT],
                    xxxb[0:32, r, :, :]), [sxxx])
                xxx_ready = sxxx
                xxx_consumers = [stxxx]
            else:
                xxx_ready = ldxxx
                xxx_consumers = []

            last_step_of_tile = None
            for pi, p in enumerate(projs):
                pr = pi % 2
                f = f_of[p]
                xs_steps = []
                for oc in range(NCK):
                    def mm_one(t, pr=pr, f=f, r=r, oc=oc):
                        return t.matmul(psm[pr][:, 0:TT], w2b[:, f * D + oc * 128: f * D + (oc + 1) * 128],
                                        xxxb[0:32, r, f, :], start=True, stop=True)
                    mdeps = [xxx_ready, vprep, pre["w2"]]
                    if use_psm[pr] is not None:
                        mdeps.append(use_psm[pr])
                    mt = P.step("tensor", mm_one, mdeps)

                    def v_xs(v, pr=pr, r=r, oc=oc, f=f):
                        v.scalar_tensor_tensor(xsb[:, pr, oc, :], psm[pr][:, 0:TT],
                                               c5b[:, f * NCK + oc:f * NCK + oc + 1],
                                               dxb[:, oc, :], OP.add, OP.mult)
                        return v.tensor_tensor(xsb[:, pr, oc, :], xsb[:, pr, oc, :],
                                               xnb[:, r, oc, 1:TT + 1], OP.add)
                    xdeps2 = [mt, pre["c5"]]
                    if ring_xs[pr] is not None:
                        xdeps2.append(ring_xs[pr])
                    vxs = P.step("vector", v_xs, xdeps2)
                    use_psm[pr] = vxs
                    xs_steps.append(vxs)
                xxx_consumers.append(xs_steps[-1])

                if p == "w":
                    def mm_d1(t, pr=pr):
                        last = None
                        for c in range(NCK):
                            last = t.matmul(psd[:, 0:TT], dw1b[:, c, :], xsb[:, pr, c, :],
                                            start=(c == 0), stop=(c == NCK - 1))
                        return last
                    d1deps = xs_steps + [pre["dw1"]]
                    if use_psd is not None:
                        d1deps.append(use_psd)
                    md1 = P.step("tensor", mm_d1, d1deps)
                    std = P.step("scalar", lambda s: s.activation(tdb[:, :], psd[:, 0:TT], AF.Tanh), [md1])
                    use_psd = std
                    ew_steps = []
                    for oc in range(NCK):
                        def mm_d2(t, oc=oc):
                            return t.matmul(psp[oc % 2][:, 0:TT], dw2b[:, oc * 128:(oc + 1) * 128],
                                            tdb[:, :], start=True, stop=True)
                        ddeps = [std, pre["dw2"]]
                        if use_psp[oc % 2] is not None:
                            ddeps.append(use_psp[oc % 2])
                        md2 = P.step("tensor", mm_d2, ddeps)

                        def s_ew(s, oc=oc):
                            return s.activation(ewb[:, oc, :], psp[oc % 2][:, 0:TT],
                                                AF.Exp, bias=tdecb[:, oc:oc + 1], scale=1.0)
                        ewdeps = [md2, pre["tdec"]]
                        if ring_ew is not None:
                            ewdeps.append(ring_ew)
                        if prev_scan is not None:
                            ewdeps.append(prev_scan)
                        sew = P.step("scalar", s_ew, ewdeps)
                        use_psp[oc % 2] = sew
                        ew_steps.append(sew)

                    def v_scan(v, r=r, i=i):
                        last = None
                        for oc in range(NCK):
                            init = 0.0 if i == 0 else csb[:, 1 - r, oc, TT - 1:TT]
                            last = v.tensor_tensor_scan(csb[:, r, oc, :], ewb[:, oc, :],
                                                        ewb[:, oc, :], init, OP.add, OP.bypass)
                        return last
                    sdeps = list(ew_steps) + ([prev_scan] if prev_scan else [])
                    if ring_ewcs[r] is not None:
                        sdeps.append(ring_ewcs[r])
                    vscan = P.step("vector", v_scan, sdeps)
                    prev_scan = vscan
                    stew = P.dma("gpsimd", dma16(rearr(scr["ewT"])[:, :, i * TT:(i + 1) * TT], ewb[:, :, :]),
                          list(ew_steps))
                    st2 = P.dma("gpsimd", dma16(rearr(scr["csT"])[:, :, i * TT:(i + 1) * TT], csb[:, r, :, :]),
                                [vscan])
                    ring_ewcs[r] = st2
                    ring_ew = stew
                    ring_xs[pr] = md1
                    last_step_of_tile = md1
                elif p != "v":
                    W = WA if (p in ("k", "g")) else WB
                    Wdep = pre["WA"] if (p in ("k", "g")) else pre["WB"]
                    evac_steps = []
                    for oc in range(NCK):
                        def mm_p(t, pr=pr, oc=oc, W=W):
                            last = None
                            for c in range(NCK):
                                last = t.matmul(psp[oc % 2][:, 0:TT], W[:, c, oc * 128:(oc + 1) * 128],
                                                xsb[:, pr, c, :], start=(c == 0), stop=(c == NCK - 1))
                            return last
                        pdeps = xs_steps + [Wdep]
                        if use_psp[oc % 2] is not None:
                            pdeps.append(use_psp[oc % 2])
                        mp = P.step("tensor", mm_p, pdeps)

                        def v_evac(v, oc=oc, r=r):
                            return v.tensor_copy(stage[:, r, oc, :], psp[oc % 2][:, 0:TT])
                        edeps = [mp]
                        if ring_stage[r] is not None:
                            edeps.append(ring_stage[r])
                        ev = P.step("vector", v_evac, edeps)
                        use_psp[oc % 2] = ev
                        evac_steps.append(ev)
                    stp = P.dma("gpsimd", dma16(rearr(out_of[p])[:, :, i * TT:(i + 1) * TT],
                                                stage[:, r, :, :]), evac_steps)
                    ring_stage[r] = stp
                    ring_xs[pr] = evac_steps[-1]
                    last_step_of_tile = evac_steps[-1]
                else:
                    evac_steps = []
                    stage_flat = stage[:, r, :, :].rearrange("p c t -> p (c t)")
                    for ts in range(TT // 128):
                        for half in range(2):
                            def mm_v(t, pr=pr, ts=ts, half=half):
                                last = None
                                for c in range(NCK):
                                    last = t.matmul(psp[half][:, :],
                                                    xsb[:, pr, c, ts * 128:(ts + 1) * 128],
                                                    WB[:, c, half * 512:(half + 1) * 512],
                                                    start=(c == 0), stop=(c == NCK - 1))
                                return last
                            vdeps = xs_steps + [pre["WB"]]
                            if use_psp[half] is not None:
                                vdeps.append(use_psp[half])
                            mv = P.step("tensor", mm_v, vdeps)

                            def v_evacv(v, ts=ts, half=half, stage_flat=stage_flat):
                                base = ts * 1024 + half * 512
                                return v.tensor_copy(stage_flat[:, base:base + 512], psp[half][:, :])
                            edeps = [mv]
                            if ring_stage[r] is not None:
                                edeps.append(ring_stage[r])
                            ev = P.step("vector", v_evacv, edeps)
                            use_psp[half] = ev
                            evac_steps.append(ev)
                    stv = P.dma("gpsimd", dma16(
                        scr["vtok"].ap().rearrange("(n p) d -> p n d", p=128)[:, 2 * i:2 * i + 2, :],
                        stage_flat.rearrange("p (n d) -> p n d", d=1024)),
                        evac_steps)
                    ring_stage[r] = stv
                    ring_xs[pr] = evac_steps[-1]
                    last_step_of_tile = evac_steps[-1]
            ring_xn[r] = last_step_of_tile
            use_dx = xs_steps[-1]
            ring_xxx[r] = xxx_consumers[-1] if xxx_consumers else None
        P.play()



P2_NPAIR = 8
P2_FWDLVL = 3
P2_FWD = True
P2_BWD = True
P2_GN = True


def phase_p2(nc, B):
    inp, scr = B.inp, B.scr
    CH = 128
    NCH = T // CH
    with contextlib.ExitStack() as _es:
        b_r = _es.enter_context(nc.sbuf_tensor("p2_r", [128, T], F32))
        b_k = _es.enter_context(nc.sbuf_tensor("p2_k", [128, T], F32))
        b_cs = _es.enter_context(nc.sbuf_tensor("p2_cs", [128, NCH, CH], F32))
        b_t0 = _es.enter_context(nc.sbuf_tensor("p2_t0", [128, NCH, CH], F32))
        b_t1 = _es.enter_context(nc.sbuf_tensor("p2_t1", [128, T], F32))
        b_rf = _es.enter_context(nc.sbuf_tensor("p2_rf", [128, T], F32))
        b_kf = _es.enter_context(nc.sbuf_tensor("p2_kf", [128, T], F32))
        b_rb = _es.enter_context(nc.sbuf_tensor("p2_rb", [128, T], F32))
        b_kb = _es.enter_context(nc.sbuf_tensor("p2_kb", [128, T], F32))
        b_v = _es.enter_context(nc.sbuf_tensor("p2_v", [128, NCH, CH], F32))
        selbuf = _es.enter_context(nc.sbuf_tensor("p2_sel", [128, 2, 256], F32))
        gstg = _es.enter_context(nc.sbuf_tensor("p2_gstg", [2, 2, 1024], F32))
        lstg = _es.enter_context(nc.sbuf_tensor("p2_lstg", [2, 2, 1024], F32))
        st1 = _es.enter_context(nc.sbuf_tensor("p2_st1", [128, 64], F32))
        st2 = _es.enter_context(nc.sbuf_tensor("p2_st2", [128, 64], F32))
        st3 = _es.enter_context(nc.sbuf_tensor("p2_st3", [128, 64], F32))
        st1x = _es.enter_context(nc.sbuf_tensor("p2_st4", [128, 64], F32))
        identb = _es.enter_context(nc.sbuf_tensor("p2_ident", [128, 128], F32))
        ktf_r = _es.enter_context(nc.sbuf_tensor("p2_ktfr", [128, 2, 128], F32))
        ktb_r = _es.enter_context(nc.sbuf_tensor("p2_ktbr", [128, 2, 128], F32))
        gn2b = _es.enter_context(nc.sbuf_tensor("p2_gn2", [128, 2], F32))
        gnexpb = _es.enter_context(nc.sbuf_tensor("p2_gnexp", [2, 128], F32))
        trib = _es.enter_context(nc.sbuf_tensor("p2_tri", [128, 256], U8))
        lnwb = _es.enter_context(nc.sbuf_tensor("p2_lnw", [128, NCK], F32))
        lnbb = _es.enter_context(nc.sbuf_tensor("p2_lnb", [128, NCK], F32))
        psA = [_es.enter_context(nc.psum_tensor(f"p2_psA{j}", [128, 512], F32)) for j in range(2)]
        psY = [_es.enter_context(nc.psum_tensor(f"p2_psY{j}", [128, 512], F32)) for j in range(2)]
        psT = [_es.enter_context(nc.psum_tensor(f"p2_psT{j}", [128, 512], F32)) for j in range(2)]
        U_sb = _es.enter_context(nc.sbuf_tensor("p2_U", [128, 32, 128], F32))

        y_sb = b_r
        ynb = b_k
        ysq = b_t1

        P = Prog(nc, "p2")
        c_id = P.dma("sync", dma16(identb[:, :], inp["ident"].ap()))
        c_gn2 = P.dma("sync", dma16(gn2b[:, :], inp["gn2"].ap()))
        c_gne = P.dma("sync", dma16(gnexpb[:, :], inp["gnexp"].ap()))
        c_tri = P.dma("sync", dma16(trib[:, :], inp["trimask2"].ap()))
        c_lnw = P.dma("sync", dma16(lnwb[:, :], inp["lnw0"].ap()))
        c_lnb = P.dma("sync", dma16(lnbb[:, :], inp["lnb0"].ap()))
        prev_pair = None

        for hp in range(P2_NPAIR):
            rows = slice(128 * hp, 128 * hp + 128)
            pdeps = [prev_pair] if prev_pair is not None else []
            ld_r = P.dma("sync", dma16(b_r[:, :], scr["rT"].ap()[rows, :]), pdeps)
            ld_k = P.dma("sync", dma16(b_k[:, :], scr["kT"].ap()[rows, :]), pdeps)
            ld_cs = P.dma("sync", dma16(b_cs[:, :, :].rearrange("p c t -> p (c t)"), scr["csT"].ap()[rows, :]), pdeps)
            ld_ew = P.dma("sync", dma16(b_t1[:, :], scr["ewT"].ap()[rows, :]), pdeps)
            ld_v = P.dma("sync", dma16(
                b_v[:, :, :], scr["vtok"].ap().rearrange("(n p) d -> p n d", p=128)[:, :, 128 * hp:128 * hp + 128]),
                pdeps)
            csf = b_cs[:, :, :].rearrange("p c t -> p (c t)")
            t0f = b_t0[:, :, :].rearrange("p c t -> p (c t)")

            def v_csb(v):
                return v.tensor_tensor(t0f, csf, b_t1[:, :], OP.subtract)
            s_csb = P.step("vector", v_csb, [ld_cs, ld_ew] + pdeps)

            def v_cfp(v):
                v.tensor_scalar(b_t1[:, :], csf, csf[:, 2048:2049], 60.0, OP.subtract, OP.min)
                return v.tensor_scalar(b_t1[:, :], b_t1[:, :], -60.0, None, OP.max)
            s_cfp = P.step("vector", v_cfp, [s_csb])

            def v_cbp(v):
                v.tensor_scalar(csf, t0f, t0f[:, 2048:2049], 60.0, OP.subtract, OP.min)
                return v.tensor_scalar(csf, csf, -60.0, None, OP.max)
            s_cbp = P.step("vector", v_cbp, [s_cfp])

            e1 = P.step("scalar", lambda s: s.activation(t0f, b_t1[:, :], AF.Exp, scale=-1.0), [s_cbp])
            m_rf = P.step("vector", lambda v: v.tensor_tensor(b_rf[:, :], b_r[:, :], t0f, OP.mult),
                          [e1, ld_r])
            e2 = P.step("scalar", lambda s: s.activation(t0f, b_t1[:, :], AF.Exp, scale=1.0), [m_rf])
            m_kf = P.step("vector", lambda v: v.tensor_tensor(b_kf[:, :], b_k[:, :], t0f, OP.mult),
                          [e2, ld_k])
            e3 = P.step("scalar", lambda s: s.activation(b_t1[:, :], csf, AF.Exp, scale=1.0), [m_kf])
            m_rb = P.step("vector", lambda v: v.tensor_tensor(b_rb[:, :], b_r[:, :], b_t1[:, :], OP.mult),
                          [e3])
            e4 = P.step("scalar", lambda s: s.activation(b_t1[:, :], csf, AF.Exp, scale=-1.0), [m_rb])
            m_kb = P.step("vector", lambda v: v.tensor_tensor(b_kb[:, :], b_k[:, :], b_t1[:, :], OP.mult),
                          [e4])

            rfh1 = csf
            kfh1 = t0f
            rbh1 = b_t1
            kbh1 = b_k
            cp1 = P.dma("gpsimd", dma16(rfh1[0:64, :], b_rf[64:128, :]), [m_rf, e3, e4])
            cp2 = P.dma("gpsimd", dma16(kfh1[0:64, :], b_kf[64:128, :]), [m_kf])
            cp3 = P.dma("gpsimd", dma16(rbh1[0:64, :], b_rb[64:128, :]), [m_rb])
            cp4 = P.dma("gpsimd", dma16(kbh1[0:64, :], b_kb[64:128, :]), [m_kb])
            prep = [m_rf, m_kf, m_rb, m_kb, ld_v, c_id, c_tri, cp1, cp2, cp3, cp4]

            if not P2_FWD:
                st_yn = P.dma("gpsimd", dma16(scr["ynT"].ap()[rows, :], b_rf[:, :]), prep)
                prev_pair = st_yn
                continue
            TA = [None] * NCH
            VA = [None] * NCH
            TB = [None] * NCH
            VB = [None] * NCH

            def rec_TA(c):
                cc = slice(c * CH, (c + 1) * CH)
                j = c % 2

                def f_TA(t, cc=cc, j=j):
                    t.matmul(psA[j][:, 0:128], b_kf[0:64, cc], b_rf[0:64, cc], start=True, stop=True)
                    t.matmul(psA[j][:, 128:256], kfh1[0:64, cc], rfh1[0:64, cc], start=True, stop=True)
                    t.matmul(psA[j][:, 256:384], b_kb[0:64, cc], b_rb[0:64, cc], start=True, stop=True)
                    t.matmul(psA[j][:, 384:512], kbh1[0:64, cc], rbh1[0:64, cc], start=True, stop=True)
                    t.transpose(psT[j][:, 0:128], b_kf[:, cc], identb[:, :])
                    return t.transpose(psT[j][:, 128:256], b_kb[:, cc], identb[:, :])
                d = list(prep)
                if c >= 2:
                    d.append(VB[c - 2])
                TA[c] = P.step("tensor", f_TA, d)

            def rec_VA(c):
                j = c % 2

                def f_VA(v, j=j, c=c):
                    v.select(selbuf[:, c % 2, :], trib[:, :], psA[j][:, 0:256], psA[j][:, 256:512])
                    v.tensor_copy(ktf_r[:, c % 2, :], psT[j][:, 0:128])
                    return v.tensor_copy(ktb_r[:, c % 2, :], psT[j][:, 128:256])
                d = [TA[c]]
                if c >= 2:
                    d.append(TB[c - 2])
                VA[c] = P.step("vector", f_VA, d)

            def rec_TB(c):
                cc = slice(c * CH, (c + 1) * CH)
                j = c % 2

                def f_TB(t, cc=cc, j=j, c=c):
                    t.matmul(psY[j][0:64, 0:128], b_v[:, c, 0:64], selbuf[:, c % 2, 0:128],
                             start=True, stop=True)
                    t.matmul(psY[j][64:128, 0:128], b_v[:, c, 64:128], selbuf[:, c % 2, 128:256],
                             start=True, stop=True, tile_position=(0, 64))
                    t.matmul(psT[j][0:64, 256:320], ktf_r[:, c % 2, 0:64], b_v[:, c, 0:64],
                             start=True, stop=True)
                    t.matmul(psT[j][64:128, 256:320], ktf_r[:, c % 2, 64:128], b_v[:, c, 64:128],
                             start=True, stop=True, tile_position=(0, 64))
                    t.matmul(psT[j][0:64, 320:384], ktb_r[:, c % 2, 0:64], b_v[:, c, 0:64],
                             start=True, stop=True)
                    return t.matmul(psT[j][64:128, 320:384], ktb_r[:, c % 2, 64:128], b_v[:, c, 64:128],
                                    start=True, stop=True, tile_position=(0, 64))
                d = [VA[c]]
                if c >= 2:
                    d.append(VB[c - 2])
                TB[c] = P.step("tensor", f_TB, d)

            def rec_VB(c):
                j = c % 2

                def f_VB(v, j=j, c=c):
                    v.tensor_copy(y_sb[:, c * CH:(c + 1) * CH], psY[j][:, 0:128])
                    return v.tensor_copy(U_sb[:, c, :], psT[j][:, 256:384])
                VB[c] = P.step("vector", f_VB, [TB[c]])

            for c in range(NCH + 1):
                if c < NCH:
                    rec_TA(c)
                    rec_VA(c)
                if c >= 1:
                    rec_TB(c - 1)
                    rec_VB(c - 1)

            def Spre_(c):
                return b_t1[:, c * 128:c * 128 + 64]

            def Ssuf_(c):
                return b_t1[:, c * 128 + 64:(c + 1) * 128]

            def f_scanS(v):
                v.memset(Spre_(0), 0.0)
                v.memset(Ssuf_(NCH - 1), 0.0)
                for c in range(1, NCH):
                    v.tensor_tensor(Spre_(c), Spre_(c - 1), U_sb[:, c - 1, 0:64], OP.add)
                last = None
                for c in range(NCH - 2, -1, -1):
                    last = v.tensor_tensor(Ssuf_(c), Ssuf_(c + 1), U_sb[:, c + 1, 64:128], OP.add)
                return last
            vscanS = P.step("vector", f_scanS, [VB[NCH - 1], VB[NCH - 2]])

            TC = [None] * NCH
            VC = [None] * NCH

            def rec_TC(c):
                cc = slice(c * CH, (c + 1) * CH)
                j = c % 2

                def f_TC(t, cc=cc, j=j, c=c):
                    t.matmul(psY[j][0:64, 0:128], Spre_(c)[0:64, :], b_rf[0:64, cc],
                             start=True, stop=False)
                    t.matmul(psY[j][0:64, 0:128], Ssuf_(c)[0:64, :], b_rb[0:64, cc],
                             start=False, stop=True)
                    t.matmul(psY[j][64:128, 0:128], Spre_(c)[64:128, :], b_rf[64:128, cc],
                             start=True, stop=False, tile_position=(64, 64))
                    return t.matmul(psY[j][64:128, 0:128], Ssuf_(c)[64:128, :], b_rb[64:128, cc],
                                    start=False, stop=True, tile_position=(64, 64))
                d = [vscanS]
                if c >= 2:
                    d.append(VC[c - 2])
                TC[c] = P.step("tensor", f_TC, d)

            def rec_VC(c):
                j = c % 2

                def f_VC(v, j=j, c=c):
                    return v.tensor_tensor(y_sb[:, c * CH:(c + 1) * CH], y_sb[:, c * CH:(c + 1) * CH],
                                           psY[j][:, 0:128], OP.add)
                VC[c] = P.step("vector", f_VC, [TC[c]])

            for c in range(NCH):
                rec_TC(c)
                rec_VC(c)
            scan_done = VC[NCH - 1]

            g_sq = P.step("scalar", lambda s: s.activation(ysq[:, :], y_sb[:, :], AF.Square), [scan_done])
            gst_list = []
            for jt in range(8):
                ts_ = slice(jt * 512, (jt + 1) * 512)
                jr = jt % 2

                def f_gmm(t, ts_=ts_):
                    t.matmul(psA[0][0:2, 0:512], gn2b[:, :], y_sb[:, ts_], start=True, stop=True)
                    return t.matmul(psT[0][0:2, 0:512], gn2b[:, :], ysq[:, ts_], start=True, stop=True)
                d = [g_sq, c_gn2, scan_done]
                if gst_list:
                    d.append(gst_list[-1])
                gmm = P.step("tensor", f_gmm, d)

                def f_gev(v, jr=jr):
                    v.tensor_copy(gstg[:, jr, 0:512], psA[0][0:2, 0:512])
                    return v.tensor_copy(gstg[:, jr, 512:1024], psT[0][0:2, 0:512])
                gev = P.step("vector", f_gev, [gmm] + ([gst_list[-2]] if len(gst_list) >= 2 else []))
                s1d = P.dma("gpsimd", dma16(scr["gn_a"].ap()[:, jt * 512:(jt + 1) * 512],
                                            gstg[:, jr, 0:512]), [gev])
                s2d = P.dma("gpsimd", dma16(scr["gn_b"].ap()[:, jt * 512:(jt + 1) * 512],
                                            gstg[:, jr, 512:1024]), [gev])
                gst_list.append(s2d)
            rs1 = P.dma("gpsimd", dma16(
                st1[:, :],
                scr["gn_a"].ap().rearrange("h (b c) -> (h b) c", c=64)), [gst_list[-1]])
            rs2 = P.dma("gpsimd", dma16(
                st2[:, :],
                scr["gn_b"].ap().rearrange("h (b c) -> (h b) c", c=64)), [gst_list[-1]])

            def f_var(v):
                v.tensor_scalar(st1[:, :], st1[:, :], 1.0 / 64.0, None, OP.mult)
                v.tensor_scalar(st2[:, :], st2[:, :], 1.0 / 64.0, None, OP.mult)
                v.tensor_tensor(st3[:, :], st1[:, :], st1[:, :], OP.mult)
                v.tensor_tensor(st2[:, :], st2[:, :], st3[:, :], OP.subtract)
                return v.tensor_scalar(st2[:, :], st2[:, :], GN_EPS, None, OP.add)
            g_var = P.step("vector", f_var, [rs1, rs2])

            def f_rs(s):
                s.activation(st3[:, :], st2[:, :], AF.Ln)
                return s.activation(st3[:, :], st3[:, :], AF.Exp, scale=-0.5)
            g_rs = P.step("scalar", f_rs, [g_var])

            def f_fix(v):
                v.tensor_tensor(st1x[:, :], st3[:, :], st3[:, :], OP.mult)
                v.tensor_tensor(st1x[:, :], st1x[:, :], st2[:, :], OP.mult)
                v.tensor_scalar(st1x[:, :], st1x[:, :], -0.5, 1.5, OP.mult, OP.add)
                v.tensor_tensor(st3[:, :], st3[:, :], st1x[:, :], OP.mult)
                v.tensor_tensor(st1[:, :], st1[:, :], st3[:, :], OP.mult)
                return v.tensor_scalar(st1[:, :], st1[:, :], -1.0, None, OP.mult)
            g_fix = P.step("vector", f_fix, [g_rs])

            bs1 = P.dma("gpsimd", dma16(
                scr["gn_a"].ap().rearrange("h (b c) -> (h b) c", c=64), st3[:, :]), [g_fix, rs1, rs2])
            bs2 = P.dma("gpsimd", dma16(
                scr["gn_b"].ap().rearrange("h (b c) -> (h b) c", c=64), st1[:, :]), [g_fix, rs1, rs2])

            app_list = []
            ldl_prev = None
            for jt in range(8):
                ts_ = slice(jt * 512, (jt + 1) * 512)
                jr = jt % 2

                def mkld(v=None):
                    pass
                dl = [bs1, bs2]
                if len(app_list) >= 2:
                    dl.append(app_list[-2])
                ld1 = P.dma("sync", dma16(lstg[:, jr, 0:512],
                                          scr["gn_a"].ap()[:, jt * 512:(jt + 1) * 512]), dl)
                ld2 = P.dma("sync", dma16(lstg[:, jr, 512:1024],
                                          scr["gn_b"].ap()[:, jt * 512:(jt + 1) * 512]), dl)

                def f_emm(t, jr=jr):
                    t.matmul(psA[1][:, 0:512], gnexpb[:, :], lstg[0:2, jr, 0:512], start=True, stop=True)
                    return t.matmul(psT[1][:, 0:512], gnexpb[:, :], lstg[0:2, jr, 512:1024], start=True, stop=True)
                d = [ld1, ld2, c_gne]
                if app_list:
                    d.append(app_list[-1])
                emm = P.step("tensor", f_emm, d)

                def f_app(v, ts_=ts_, hp=hp):
                    v.tensor_tensor(ynb[:, ts_], y_sb[:, ts_], psA[1][:, 0:512], OP.mult)
                    v.tensor_tensor(ynb[:, ts_], ynb[:, ts_], psT[1][:, 0:512], OP.add)
                    return v.tensor_scalar(ynb[:, ts_], ynb[:, ts_], lnwb[:, hp:hp + 1],
                                           lnbb[:, hp:hp + 1], OP.mult, OP.add)
                app_list.append(P.step("vector", f_app, [emm, c_lnw, c_lnb]))

            st_yn = P.dma("gpsimd", dma16(scr["ynT"].ap()[rows, :], ynb[:, :]), app_list)
            prev_pair = st_yn
        P.play()


def _proj_steps(P, W, Wdep, rhs_of, psp, use_psp, evac_fn, evac_deps_extra, mm_deps, TTn):
    evs = []
    for oc in range(NCK):
        def f_mm(t, oc=oc):
            last = None
            for c in range(NCK):
                last = t.matmul(psp[oc % 2][:, 0:TTn], W[:, c, oc * 128:(oc + 1) * 128],
                                rhs_of(c), start=(c == 0), stop=(c == NCK - 1))
            return last
        d = list(mm_deps) + [Wdep]
        if use_psp[oc % 2] is not None:
            d.append(use_psp[oc % 2])
        mp = P.step("tensor", f_mm, d)
        ev = P.step(evac_fn[0], lambda e, oc=oc: evac_fn[1](e, oc), [mp] + list(evac_deps_extra))
        use_psp[oc % 2] = ev
        evs.append(ev)
    return evs


def phase_p3a(nc, B):
    TTn = 256
    NT = T // TTn
    inp, scr = B.inp, B.scr
    with contextlib.ExitStack() as _es:
        yn = _es.enter_context(nc.sbuf_tensor("p3a_yn", [128, 2, NCK, TTn], F32))
        gp = _es.enter_context(nc.sbuf_tensor("p3a_gp", [128, 2, NCK, TTn], F32))
        gs = _es.enter_context(nc.sbuf_tensor("p3a_gs", [128, NCK, TTn], F32))
        yng = _es.enter_context(nc.sbuf_tensor("p3a_yng", [128, NCK, TTn], F32))
        stage = _es.enter_context(nc.sbuf_tensor("p3a_stage", [128, 2, NCK, TTn], F32))
        gout = _es.enter_context(nc.sbuf_tensor("p3a_gout", [128, 2, NCK, TTn], F32))
        WO = _es.enter_context(nc.sbuf_tensor("p3a_wo", [128, NCK, D], F32))
        psp0 = _es.enter_context(nc.psum_tensor("p3a_psp0", [128, 512], F32))
        psp1 = _es.enter_context(nc.psum_tensor("p3a_psp1", [128, 512], F32))
        psp = [psp0, psp1]
        P = Prog(nc, "p3a")
        wo_ld = P.dma("sync", dma16(WO[:, :, :], inp["woT"].ap().rearrange("(c p) o -> p c o", p=128)))
        ring_ld = [None, None]
        ring_st = [None, None]
        ring_g = [None, None]
        use_psp = [None, None]
        use_gs = None
        use_yng = None
        for i in range(NT):
            r = i % 2
            tt = slice(i * TTn, (i + 1) * TTn)
            d0 = [ring_ld[r]] if ring_ld[r] is not None else []
            ld_yn = P.dma("sync", dma16(yn[:, r, :, :], rearr(scr["ynT"])[:, :, tt]), d0)
            ld_gp = P.dma("sync", dma16(gp[:, r, :, :], rearr(scr["gpreT"])[:, :, tt]), d0)

            s_sig = P.step("scalar", lambda s, r=r: s.activation(gs[:, :, :], gp[:, r, :, :], AF.Sigmoid),
                           [ld_gp] + ([use_gs] if use_gs else []))

            def f_g(v, r=r):
                v.tensor_tensor(gout[:, r, :, :], gp[:, r, :, :], gs[:, :, :], OP.mult)
                return v.tensor_tensor(yng[:, :, :], yn[:, r, :, :], gout[:, r, :, :], OP.mult)
            d = [s_sig, ld_yn]
            if use_yng is not None:
                d.append(use_yng)
            if ring_g[r] is not None:
                d.append(ring_g[r])
            v_g = P.step("vector", f_g, d)
            use_gs = v_g

            st_g = P.dma("gpsimd", dma16(rearr(scr["gT"])[:, :, tt], gout[:, r, :, :]), [v_g])
            ring_g[r] = st_g

            def mk_evac(e, oc, r=r):
                return e.tensor_copy(stage[:, r, oc, :], psp[oc % 2][:, 0:TTn])
            evs = _proj_steps(P, WO, wo_ld, lambda c: yng[:, c, :], psp, use_psp,
                              ("vector", mk_evac), [ring_st[r]] if ring_st[r] is not None else [],
                              [v_g], TTn)
            use_yng = evs[-1]
            st_s0 = P.dma("gpsimd", dma16(rearr(scr["s0T"])[:, :, tt], stage[:, r, :, :]), evs)
            ring_st[r] = st_s0
            ring_ld[r] = evs[-1]
        P.play()


def phase_p3b(nc, B):
    TTn = 256
    NT = T // TTn
    inp, scr = B.inp, B.scr
    with contextlib.ExitStack() as _es:
        xn = _es.enter_context(nc.sbuf_tensor("p3b_xn", [128, 2, NCK, TTn + 1], F32))
        xb = _es.enter_context(nc.sbuf_tensor("p3b_x", [128, 2, NCK, TTn], F32))
        s0 = _es.enter_context(nc.sbuf_tensor("p3b_s0", [128, 2, NCK, TTn], F32))
        db = _es.enter_context(nc.sbuf_tensor("p3b_d", [128, NCK, TTn], F32))
        xr0 = _es.enter_context(nc.sbuf_tensor("p3b_xr0", [128, NCK, TTn], F32))
        xk0 = _es.enter_context(nc.sbuf_tensor("p3b_xk0", [128, NCK, TTn], F32))
        r0 = _es.enter_context(nc.sbuf_tensor("p3b_r0", [128, NCK, TTn], F32))
        k0 = _es.enter_context(nc.sbuf_tensor("p3b_k0", [128, NCK, TTn], F32))
        ffn = _es.enter_context(nc.sbuf_tensor("p3b_ffn", [128, NCK, TTn], F32))
        k0q = xr0
        fft = xk0
        WR = _es.enter_context(nc.sbuf_tensor("p3b_wr", [128, NCK, D], F32))
        WK = _es.enter_context(nc.sbuf_tensor("p3b_wk", [128, NCK, D], F32))
        WV = _es.enter_context(nc.sbuf_tensor("p3b_wv", [128, NCK, D], F32))
        murb = _es.enter_context(nc.sbuf_tensor("p3b_mur", [128, NCK], F32))
        mukb = _es.enter_context(nc.sbuf_tensor("p3b_muk", [128, NCK], F32))
        psp0 = _es.enter_context(nc.psum_tensor("p3b_psp0", [128, 512], F32))
        psp1 = _es.enter_context(nc.psum_tensor("p3b_psp1", [128, 512], F32))
        psp = [psp0, psp1]
        P = Prog(nc, "p3b")
        wld = [P.dma("sync", dma16(WR[:, :, :], inp["wrfT"].ap().rearrange("(c p) o -> p c o", p=128))),
               P.dma("sync", dma16(WK[:, :, :], inp["wkfT"].ap().rearrange("(c p) o -> p c o", p=128))),
               P.dma("sync", dma16(WV[:, :, :], inp["wvfT"].ap().rearrange("(c p) o -> p c o", p=128))),
               P.dma("sync", dma16(murb[:, :], inp["mur"].ap())),
               P.dma("sync", dma16(mukb[:, :], inp["muk"].ap()))]
        ring = [None, None]
        use_psp = [None, None]
        use_mid = None
        for i in range(NT):
            r = i % 2
            tt = slice(i * TTn, (i + 1) * TTn)
            lo = i * TTn - 1
            s_lo = max(lo, 0)
            d_lo = s_lo - lo
            d0 = [ring[r]] if ring[r] is not None else []
            ld_xn = P.dma("sync", dma16(xn[:, r, :, d_lo:TTn + 1],
                                        rearr(scr["xnT"])[:, :, s_lo:(i + 1) * TTn]), d0)
            ld_x = P.dma("sync", dma16(xb[:, r, :, :], rearr(inp["xT"])[:, :, tt]), d0)
            ld_s0 = P.dma("sync", dma16(s0[:, r, :, :], rearr(scr["s0T"])[:, :, tt]), d0)

            def f_mix(v, r=r, i=i):
                if i == 0:
                    v.memset(xn[:, r, :, 0:1], 0.0)
                bx = xn[:, r, :, :]
                v.tensor_tensor(db[:, :, :], bx[:, :, 1:TTn + 1], bx[:, :, 0:TTn], OP.subtract)
                for c in range(NCK):
                    v.scalar_tensor_tensor(xr0[:, c, :], db[:, c, :], murb[:, c:c + 1],
                                           bx[:, c, 0:TTn], OP.mult, OP.add)
                last = None
                for c in range(NCK):
                    last = v.scalar_tensor_tensor(xk0[:, c, :], db[:, c, :], mukb[:, c:c + 1],
                                                  bx[:, c, 0:TTn], OP.mult, OP.add)
                return last
            d = [ld_xn, wld[3], wld[4]]
            if use_mid is not None:
                d.append(use_mid)
            v_mix = P.step("vector", f_mix, d)

            def ev_r0(e, oc):
                return e.activation(r0[:, oc, :], psp[oc % 2][:, 0:TTn], AF.Sigmoid)
            evs_r = _proj_steps(P, WR, wld[0], lambda c: xr0[:, c, :], psp, use_psp,
                                ("scalar", ev_r0), [], [v_mix], TTn)

            def ev_k0(e, oc):
                return e.activation(k0[:, oc, :], psp[oc % 2][:, 0:TTn], AF.Relu)
            evs_k = _proj_steps(P, WK, wld[1], lambda c: xk0[:, c, :], psp, use_psp,
                                ("scalar", ev_k0), [], [v_mix], TTn)

            v_k0q = P.step("vector", lambda v: v.tensor_tensor(k0q[:, :, :], k0[:, :, :], k0[:, :, :], OP.mult),
                           evs_k + evs_r)

            def ev_ffn(e, oc):
                return e.tensor_tensor(ffn[:, oc, :], psp[oc % 2][:, 0:TTn], r0[:, oc, :], OP.mult)
            evs_v = _proj_steps(P, WV, wld[2], lambda c: k0q[:, c, :], psp, use_psp,
                                ("vector", ev_ffn), evs_r, [v_k0q], TTn)

            s_g0 = P.step("scalar", lambda s: s.activation(fft[:, :, :], ffn[:, :, :], AF.Sigmoid),
                          evs_v)

            def f_out(v, r=r):
                v.tensor_tensor(fft[:, :, :], fft[:, :, :], ffn[:, :, :], OP.mult)
                v.tensor_tensor(fft[:, :, :], fft[:, :, :], s0[:, r, :, :], OP.mult)
                return v.tensor_tensor(xn[:, r, :, 0:TTn], xb[:, r, :, :], fft[:, :, :], OP.add)
            v_out = P.step("vector", f_out, [s_g0, ld_x, ld_s0])
            use_mid = v_out

            st = P.dma("gpsimd", dma16(rearr(scr["x1T"])[:, :, tt], xn[:, r, :, 0:TTn]), [v_out])
            ring[r] = st
        P.play()


def phase_p4(nc, B):
    TTn = 256
    NT = T // TTn
    inp, scr = B.inp, B.scr
    with contextlib.ExitStack() as _es:
        xb = _es.enter_context(nc.sbuf_tensor("p4_x", [128, 2, NCK, TTn], F32))
        sq = _es.enter_context(nc.sbuf_tensor("p4_sq", [128, NCK, TTn], F32))
        xn1 = _es.enter_context(nc.sbuf_tensor("p4_xn1", [128, 2, NCK, TTn], F32))
        xn2 = _es.enter_context(nc.sbuf_tensor("p4_xn2", [128, 2, NCK, TTn], F32))
        stage = _es.enter_context(nc.sbuf_tensor("p4_stage", [128, 2, NCK, TTn], F32))
        vstage = _es.enter_context(nc.sbuf_tensor("p4_vstage", [128, 2, 2 * D], F32))
        rows = _es.enter_context(nc.sbuf_tensor("p4_rows", [1, 6 * TTn], F32))
        onescol = _es.enter_context(nc.sbuf_tensor("p4_ones", [128, 1], F32))
        onesrow = _es.enter_context(nc.sbuf_tensor("p4_ones1", [1, 128], F32))
        WQ = _es.enter_context(nc.sbuf_tensor("p4_wq", [128, NCK, D], F32))
        WK = _es.enter_context(nc.sbuf_tensor("p4_wk", [128, NCK, D], F32))
        WV = _es.enter_context(nc.sbuf_tensor("p4_wv", [128, NCK, D], F32))
        nwb = _es.enter_context(nc.sbuf_tensor("p4_nw", [128, NCK], F32))
        bqb = _es.enter_context(nc.sbuf_tensor("p4_bq", [128, NCK], F32))
        bkb = _es.enter_context(nc.sbuf_tensor("p4_bk", [128, NCK], F32))
        bvb = _es.enter_context(nc.sbuf_tensor("p4_bv", [128, NCK], F32))
        ps_s = _es.enter_context(nc.psum_tensor("p4_ps_s", [1, 512], F32))
        ps_q = _es.enter_context(nc.psum_tensor("p4_ps_q", [1, 512], F32))
        ps_mu = _es.enter_context(nc.psum_tensor("p4_ps_mu", [128, 512], F32))
        ps_rs = _es.enter_context(nc.psum_tensor("p4_ps_rs", [128, 512], F32))
        psp0 = _es.enter_context(nc.psum_tensor("p4_psp0", [128, 512], F32))
        psp1 = _es.enter_context(nc.psum_tensor("p4_psp1", [128, 512], F32))
        psp = [psp0, psp1]
        P = Prog(nc, "p4")

        def R(i):
            return rows[0:1, i * TTn:(i + 1) * TTn]

        wld = [P.dma("sync", dma16(WQ[:, :, :], inp["wqT1"].ap().rearrange("(c p) o -> p c o", p=128))),
               P.dma("sync", dma16(WK[:, :, :], inp["wkT1"].ap().rearrange("(c p) o -> p c o", p=128))),
               P.dma("sync", dma16(WV[:, :, :], inp["wvT1"].ap().rearrange("(c p) o -> p c o", p=128))),
               P.dma("sync", dma16(onescol[:, :], inp["ones128"].ap())),
               P.dma("sync", dma16(onesrow[:, :], inp["ones1x128"].ap())),
               P.dma("sync", dma16(nwb[:, :], inp["normw1"].ap())),
               P.dma("sync", dma16(bqb[:, :], inp["bq1"].ap())),
               P.dma("sync", dma16(bkb[:, :], inp["bk1"].ap())),
               P.dma("sync", dma16(bvb[:, :], inp["bv1"].ap()))]
        ring = [None, None]
        use_psp = [None, None]
        use_sq = None
        use_stats = None
        use_bc = None
        for i in range(NT):
            r = i % 2
            tt = slice(i * TTn, (i + 1) * TTn)
            d0 = [ring[r]] if ring[r] is not None else []
            ldx = P.dma("sync", dma16(xb[:, r, :, :], rearr(scr["x1T"])[:, :, tt]), d0)

            def norm_pass(src_ap, dst, eps, vr_extra, tag, i=i, r=r):
                sqst = P.step("scalar", lambda s: s.activation(sq[:, :, :], src_ap, AF.Square),
                              [ldx] + ([use_sq[tag]] if use_sq and use_sq.get(tag) else []))
                return sqst
            sq1 = P.step("scalar", lambda s, r=r: s.activation(sq[:, :, :], xb[:, r, :, :], AF.Square),
                         [ldx] + ([use_sq] if use_sq else []))

            def f_stats1(t, r=r):
                for c in range(NCK):
                    t.matmul(ps_s[0:1, 0:TTn], onescol[:, :], xb[:, r, c, :],
                             start=(c == 0), stop=(c == NCK - 1))
                last = None
                for c in range(NCK):
                    last = t.matmul(ps_q[0:1, 0:TTn], onescol[:, :], sq[:, c, :],
                                    start=(c == 0), stop=(c == NCK - 1))
                return last
            mst1 = P.step("tensor", f_stats1, [ldx, sq1, wld[3]] + ([use_stats] if use_stats else []))

            def f_vst1(v):
                v.tensor_scalar(R(2), ps_s[0:1, 0:TTn], 1.0 / D, None, OP.mult)
                v.tensor_scalar(R(1), ps_q[0:1, 0:TTn], 1.0 / D, None, OP.mult)
                v.tensor_tensor(R(0), R(2), R(2), OP.mult)
                v.tensor_tensor(R(3), R(1), R(0), OP.subtract)
                return v.tensor_scalar(R(3), R(3), LN_EPS, None, OP.add)
            vst1 = P.step("vector", f_vst1, [mst1])
            use_stats = vst1

            def f_rs1(s):
                s.activation(R(5), R(3), AF.Ln)
                return s.activation(R(4), R(5), AF.Exp, scale=-0.5)
            srs1 = P.step("scalar", f_rs1, [vst1])

            def f_nw1(v):
                v.tensor_tensor(R(5), R(4), R(4), OP.mult)
                v.tensor_tensor(R(5), R(5), R(3), OP.mult)
                v.tensor_scalar(R(5), R(5), -0.5, 1.5, OP.mult, OP.add)
                return v.tensor_tensor(R(4), R(4), R(5), OP.mult)
            vnw1 = P.step("vector", f_nw1, [srs1])

            def f_bc1(t):
                t.matmul(ps_mu[:, 0:TTn], onesrow[:, :], R(2), start=True, stop=True)
                return t.matmul(ps_rs[:, 0:TTn], onesrow[:, :], R(4), start=True, stop=True)
            bc1 = P.step("tensor", f_bc1, [vnw1, wld[4]] + ([use_bc] if use_bc else []))

            def f_xn1(v, r=r):
                last = None
                for c in range(NCK):
                    v.tensor_tensor(xn1[:, r, c, :], xb[:, r, c, :], ps_mu[:, 0:TTn], OP.subtract)
                    last = v.tensor_tensor(xn1[:, r, c, :], xn1[:, r, c, :], ps_rs[:, 0:TTn], OP.mult)
                return last
            vxn1 = P.step("vector", f_xn1, [bc1])
            st_xn1 = P.dma("gpsimd", dma16(rearr(scr["xn1T"])[:, :, tt], xn1[:, r, :, :]), [vxn1])

            sq2 = P.step("scalar", lambda s, r=r: s.activation(sq[:, :, :], xn1[:, r, :, :], AF.Square),
                         [vxn1, mst1])
            use_sq = sq2

            def f_stats2(t):
                last = None
                for c in range(NCK):
                    last = t.matmul(ps_q[0:1, 0:TTn], onescol[:, :], sq[:, c, :],
                                    start=(c == 0), stop=(c == NCK - 1))
                return last
            mst2 = P.step("tensor", f_stats2, [sq2, vst1])

            def f_vst2(v):
                v.tensor_scalar(R(3), ps_q[0:1, 0:TTn], 1.0 / D, RMS_EPS, OP.mult, OP.add)
                return v.tensor_scalar(R(3), R(3), 1.0, None, OP.mult)
            vst2 = P.step("vector", f_vst2, [mst2])

            def f_rs2(s):
                s.activation(R(5), R(3), AF.Ln)
                return s.activation(R(4), R(5), AF.Exp, scale=-0.5)
            srs2 = P.step("scalar", f_rs2, [vst2])

            def f_nw2(v):
                v.tensor_tensor(R(5), R(4), R(4), OP.mult)
                v.tensor_tensor(R(5), R(5), R(3), OP.mult)
                v.tensor_scalar(R(5), R(5), -0.5, 1.5, OP.mult, OP.add)
                return v.tensor_tensor(R(4), R(4), R(5), OP.mult)
            vnw2 = P.step("vector", f_nw2, [srs2])

            def f_bc2(t):
                return t.matmul(ps_rs[:, 0:TTn], onesrow[:, :], R(4), start=True, stop=True)
            bc2 = P.step("tensor", f_bc2, [vnw2, vxn1])
            use_bc = None

            def f_xn2(v, r=r):
                last = None
                for c in range(NCK):
                    v.tensor_tensor(xn2[:, r, c, :], xn1[:, r, c, :], ps_rs[:, 0:TTn], OP.mult)
                    last = v.tensor_scalar(xn2[:, r, c, :], xn2[:, r, c, :], nwb[:, c:c + 1],
                                           None, OP.mult)
                return last
            vxn2 = P.step("vector", f_xn2, [bc2, wld[5]])
            use_bc = vxn2
            st_xn2 = P.dma("gpsimd", dma16(rearr(scr["xn2T"])[:, :, tt], xn2[:, r, :, :]), [vxn2])

            def ev_q(e, oc, r=r):
                return e.tensor_scalar(stage[:, r, oc, :], psp[oc % 2][:, 0:TTn],
                                       bqb[:, oc:oc + 1], None, OP.add)
            evq = _proj_steps(P, WQ, wld[0], lambda c, r=r: xn2[:, r, c, :], psp, use_psp,
                              ("vector", ev_q), [wld[6]] + ([ring[r]] if ring[r] is not None else []),
                              [vxn2], TTn)
            st_q = P.dma("gpsimd", dma16(rearr(scr["q1T"])[:, :, tt], stage[:, r, :, :]), evq)

            def ev_k(e, oc, r=r):
                return e.tensor_scalar(stage[:, r, oc, :], psp[oc % 2][:, 0:TTn],
                                       bkb[:, oc:oc + 1], None, OP.add)
            evk = _proj_steps(P, WK, wld[1], lambda c, r=r: xn2[:, r, c, :], psp, use_psp,
                              ("vector", ev_k), [wld[7], st_q], [vxn2], TTn)
            st_k = P.dma("gpsimd", dma16(rearr(scr["k1T"])[:, :, tt], stage[:, r, :, :]), evk)

            evs_v = []
            vsf = vstage[:, r, :]
            for ts_ in range(TTn // 128):
                for half in range(2):
                    def f_mmv(t, ts_=ts_, half=half, r=r):
                        last = None
                        for c in range(NCK):
                            last = t.matmul(psp[half][:, :],
                                            xn2[:, r, c, ts_ * 128:(ts_ + 1) * 128],
                                            WV[:, c, half * 512:(half + 1) * 512],
                                            start=(c == 0), stop=(c == NCK - 1))
                        return last
                    d = [vxn2, wld[2]]
                    if use_psp[half] is not None:
                        d.append(use_psp[half])
                    mv = P.step("tensor", f_mmv, d)

                    def f_evv(v, ts_=ts_, half=half, vsf=vsf):
                        base = ts_ * 1024 + half * 512
                        return v.tensor_copy(vsf[:, base:base + 512], psp[half][:, :])
                    d = [mv, wld[8]]
                    if ring[r] is not None:
                        d.append(ring[r])
                    ev = P.step("vector", f_evv, d)
                    use_psp[half] = ev
                    evs_v.append(ev)
            st_v = P.dma("gpsimd", dma16(
                scr["v1tok"].ap().rearrange("(n p) d -> p n d", p=128)[:, 2 * i:2 * i + 2, :],
                vsf.rearrange("p (n d) -> p n d", d=1024)), evs_v)
            ring[r] = st_v
        P.play()


P5_NH = 16
P5_NI = 8


def phase_p5(nc, B):
    inp, scr = B.inp, B.scr
    NI = P5_NI
    with contextlib.ExitStack() as _es:
        qb = _es.enter_context(nc.sbuf_tensor("p5_q", [64, T], F32))
        kb = _es.enter_context(nc.sbuf_tensor("p5_k", [64, T], F32))
        vb = _es.enter_context(nc.sbuf_tensor("p5_v", [128, 32, 66], F32))
        pb = _es.enter_context(nc.sbuf_tensor("p5_p", [128, 4, 512], F32))
        zz = _es.enter_context(nc.sbuf_tensor("p5_z", [128, 128], F32))
        shb = _es.enter_context(nc.sbuf_tensor("p5_shift", [128, 1], F32))
        dmk = _es.enter_context(nc.sbuf_tensor("p5_dmk", [128, 128], U8))
        dmki = _es.enter_context(nc.sbuf_tensor("p5_dmki", [128, 128], U8))
        yst = _es.enter_context(nc.sbuf_tensor("p5_yst", [66, 3, 512], F32))
        lrow = _es.enter_context(nc.sbuf_tensor("p5_l", [16, T], F32))
        psS = [_es.enter_context(nc.psum_tensor(f"p5_psS{j}", [128, 512], F32)) for j in range(4)]
        psYl = [_es.enter_context(nc.psum_tensor(f"p5_psY{j}", [128, 512], F32)) for j in range(3)]
        P = Prog(nc, "p5")
        c_dm0 = P.dma("sync", dma16(dmk[:, :], inp["dmask"].ap()))
        c_dm = P.dma("sync", dma16(dmki[:, :], inp["dmaski"].ap()))
        c_z = P.step("vector", lambda v: (v.memset(zz[:, :], 0.0),
                                          v.memset(shb[:, :], -EXP_SHIFT))[1])
        prev_head = None
        for h in range(P5_NH):
            hrow = slice(64 * (h % 2), 64 * (h % 2) + 64) if False else None
            pdeps = [prev_head] if prev_head is not None else []
            ld_q = P.dma("sync", dma16(qb[:, :], scr["q1T"].ap()[64 * h:64 * h + 64, :]), pdeps)
            ld_k = P.dma("sync", dma16(kb[:, :], scr["k1T"].ap()[64 * h:64 * h + 64, :]), pdeps)
            ld_v = P.dma("sync", dma16(
                vb[:, :, 0:64],
                scr["v1tok"].ap().rearrange("(n p) d -> p n d", p=128)[:, :, 64 * h:64 * h + 64]), pdeps)
            vload = P.step("vector", lambda v: (v.memset(vb[:, :, 64:65], 1.0),
                                                v.memset(vb[:, :, 65:66], 0.0))[1], [ld_v] + pdeps)
            head_tail = []
            use_ps = [None, None, None, None]
            use_pb = [None, None, None, None]
            use_py = [None, None, None]
            LA = 4
            seq = [(ib, jc) for ib in range(NI) for jc in range(4 * ib + 4)]
            pready_of = {}
            pv_of = {}
            last_pv_of_ib = {}
            for ib in range(NI):
                last_pv_of_ib[ib] = 4 * ib + 3

            def rec_score(idx):
                ib, jc = seq[idx]
                i0 = ib * 512
                d_off = max(0, jc * 128 - i0)
                jj = idx % 4

                def f_sc(t, jc=jc, d_off=d_off, jj=jj, i0=i0):
                    return t.matmul(psS[jj][:, d_off:512], kb[:, jc * 128:(jc + 1) * 128],
                                    qb[:, i0 + d_off:i0 + 512], start=True, stop=True)
                d = [ld_q, ld_k]
                if use_ps[jj] is not None:
                    d.append(use_ps[jj])
                msc = P.step("tensor", f_sc, d)

                def f_exp(s, d_off=d_off, jj=jj):
                    return s.activation(pb[:, jj, d_off:512], psS[jj][:, d_off:512],
                                        AF.Exp, scale=0.125, bias=shb[:, 0:1])
                d = [msc, c_z]
                if use_pb[jj] is not None:
                    d.append(use_pb[jj])
                sexp = P.step("scalar", f_exp, d)
                use_ps[jj] = sexp
                pready = sexp
                if d_off > 0 or jc * 128 == i0:
                    dd = d_off

                    def f_msk(v, jj=jj, dd=dd):
                        return v.copy_predicated(pb[:, jj, dd:dd + 128], dmki[:, :], zz[:, :])
                    pready = P.step("vector", f_msk, [sexp, c_z, c_dm])
                pready_of[idx] = (sexp, pready)

            def rec_pv(idx):
                ib, jc = seq[idx]
                i0 = ib * 512
                d_off = max(0, jc * 128 - i0)
                jj = idx % 4
                njc = 4 * ib + 4
                psY = psYl[ib % 3]
                diag = (d_off > 0)
                exp_done, mask_done = pready_of[idx]
                if not diag:
                    def f_pv(t, jc=jc, d_off=d_off, jj=jj, first=(jc == 0), last=(jc == njc - 1), psY=psY):
                        return t.matmul(psY[0:66, d_off:512], vb[:, jc, :], pb[:, jj, d_off:512],
                                        start=first, stop=last)
                    d = [mask_done, vload]
                    if use_py[ib % 3] is not None:
                        d.append(use_py[ib % 3])
                    mpv = P.step("tensor", f_pv, d)
                else:
                    def f_pv_m(t, jc=jc, d_off=d_off, jj=jj, last=(jc == njc - 1), psY=psY):
                        return t.matmul(psY[0:66, d_off + 128:512], vb[:, jc, :],
                                        pb[:, jj, d_off + 128:512], start=False, stop=last)
                    def f_pv_s(t, jc=jc, d_off=d_off, jj=jj, last=(jc == njc - 1), psY=psY):
                        return t.matmul(psY[0:66, d_off:d_off + 128], vb[:, jc, :],
                                        pb[:, jj, d_off:d_off + 128], start=False, stop=last)
                    d = [exp_done, vload]
                    if use_py[ib % 3] is not None:
                        d.append(use_py[ib % 3])
                    if d_off + 128 < 512:
                        P.step("tensor", f_pv_m, d)
                    mpv = P.step("tensor", f_pv_s, [mask_done, vload] +
                                 ([use_py[ib % 3]] if use_py[ib % 3] is not None else []))
                use_pb[jj] = mpv
                pv_of[idx] = mpv
                if jc == njc - 1:
                    def f_ev(s, ib=ib, psY=psY):
                        s.copy(yst[0:64, ib % 3, :], psY[0:64, 0:512])
                        return s.copy(yst[64:65, ib % 3, :], psY[64:65, 0:512])
                    d = [mpv]
                    if len(head_tail) >= 3:
                        d.append(head_tail[-3])
                    evy = P.step("scalar", f_ev, d)
                    use_py[ib % 3] = evy
                    st_y = P.dma("gpsimd", dma16(scr["attraw"].ap()[64 * h:64 * h + 64, i0:i0 + 512],
                                                 yst[0:64, ib % 3, :]), [evy])
                    st_l = P.dma("gpsimd", dma16(scr["l1s"].ap()[h:h + 1, i0:i0 + 512],
                                                 yst[64:65, ib % 3, :]), [evy])
                    head_tail.append(st_l)

            for idx in range(len(seq) + LA):
                if idx >= LA:
                    rec_pv(idx - LA)
                if idx < len(seq):
                    rec_score(idx)
            prev_head = head_tail[-1]

        if P5_NH < 16:
            P.play()
            return
        ld_l = P.dma("sync", dma16(lrow[:, :], scr["l1s"].ap()), [prev_head])
        e_ln = P.step("scalar", lambda s: (s.activation(lrow[:, :], lrow[:, :], AF.Ln),
                                           s.activation(lrow[:, :], lrow[:, :], AF.Exp, scale=-1.0))[1],
                      [ld_l])
        st_li = P.dma("gpsimd", dma16(scr["linv1"].ap(), lrow[:, :]), [e_ln])
        P.play()


def phase_p6(nc, B):
    TTn = 256
    NT = T // TTn
    inp, scr = B.inp, B.scr
    outT = B.out
    with contextlib.ExitStack() as _es:
        ar = _es.enter_context(nc.sbuf_tensor("p6_ar", [128, 2, NCK, TTn], F32))
        li = _es.enter_context(nc.sbuf_tensor("p6_li", [16, 2, TTn], F32))
        att = _es.enter_context(nc.sbuf_tensor("p6_att", [128, NCK, TTn], F32))
        xn1 = _es.enter_context(nc.sbuf_tensor("p6_xn1", [128, 2, NCK, TTn], F32))
        xn2 = _es.enter_context(nc.sbuf_tensor("p6_xn2", [128, 2, NCK, TTn], F32))
        x1b = _es.enter_context(nc.sbuf_tensor("p6_x1", [128, 2, NCK, TTn], F32))
        s1 = _es.enter_context(nc.sbuf_tensor("p6_s1", [128, NCK, TTn], F32))
        sm = _es.enter_context(nc.sbuf_tensor("p6_sm", [64, TTn], F32))
        hb = _es.enter_context(nc.sbuf_tensor("p6_h", [128, NCK, TTn], F32))
        gsm = _es.enter_context(nc.sbuf_tensor("p6_gsm", [64, TTn], F32))
        of = _es.enter_context(nc.sbuf_tensor("p6_of", [128, NCK, TTn], F32))
        oft = _es.enter_context(nc.sbuf_tensor("p6_oft", [128, NCK, TTn], F32))
        outs = _es.enter_context(nc.sbuf_tensor("p6_out", [128, 2, NCK, TTn], F32))
        WO = _es.enter_context(nc.sbuf_tensor("p6_wo", [128, NCK, D], F32))
        WIN = _es.enter_context(nc.sbuf_tensor("p6_win", [128, NCK, 64], F32))
        WOUT = _es.enter_context(nc.sbuf_tensor("p6_wout", [64, D], F32))
        W1E = _es.enter_context(nc.sbuf_tensor("p6_w1e", [64, NCK * 128], F32))
        W2R = _es.enter_context(nc.sbuf_tensor("p6_w2r", [128, NCK * 64], F32))
        LEX = _es.enter_context(nc.sbuf_tensor("p6_lex", [16, NCK * 128], F32))
        bob = _es.enter_context(nc.sbuf_tensor("p6_bo", [128, NCK], F32))
        binb = _es.enter_context(nc.sbuf_tensor("p6_bin", [64, 1], F32))
        b1b = _es.enter_context(nc.sbuf_tensor("p6_b1", [128, NCK], F32))
        b2b = _es.enter_context(nc.sbuf_tensor("p6_b2", [64, 1], F32))
        boutb = _es.enter_context(nc.sbuf_tensor("p6_bout", [128, NCK], F32))
        psp0 = _es.enter_context(nc.psum_tensor("p6_psp0", [128, 512], F32))
        psp1 = _es.enter_context(nc.psum_tensor("p6_psp1", [128, 512], F32))
        psE = _es.enter_context(nc.psum_tensor("p6_psE", [128, 512], F32))
        psF = _es.enter_context(nc.psum_tensor("p6_psF", [128, 512], F32))
        psp = [psp0, psp1]
        P = Prog(nc, "p6")
        wld = [P.dma("sync", dma16(WO[:, :, :], inp["woT1"].ap().rearrange("(c p) o -> p c o", p=128))),
               P.dma("sync", dma16(WIN[:, :, :], inp["win1"].ap().rearrange("(c p) o -> p c o", p=128))),
               P.dma("sync", dma16(WOUT[:, :], inp["wout1"].ap())),
               P.dma("sync", dma16(W1E[:, :], inp["w1exp"].ap())),
               P.dma("sync", dma16(W2R[:, :], inp["w2red"].ap())),
               P.dma("sync", dma16(LEX[:, :], inp["lexp"].ap())),
               P.dma("sync", dma16(bob[:, :], inp["bo1"].ap())),
               P.dma("sync", dma16(binb[:, :], inp["bin1"].ap())),
               P.dma("sync", dma16(b1b[:, :], inp["b1f"].ap())),
               P.dma("sync", dma16(b2b[:, :], inp["b2c"].ap())),
               P.dma("sync", dma16(boutb[:, :], inp["bout1"].ap()))]
        ring = [None, None]
        use_psp = [None, None]
        use_mid = None
        use_psE = None
        use_psF = None
        for i in range(NT):
            r = i % 2
            tt = slice(i * TTn, (i + 1) * TTn)
            d0 = [ring[r]] if ring[r] is not None else []
            ld_ar = P.dma("sync", dma16(ar[:, r, :, :], rearr(scr["attraw"])[:, :, tt]), d0)
            ld_li = P.dma("sync", dma16(li[:, r, :], scr["linv1"].ap()[:, tt]), d0)
            ld_xn1 = P.dma("sync", dma16(xn1[:, r, :, :], rearr(scr["xn1T"])[:, :, tt]), d0)
            ld_xn2 = P.dma("sync", dma16(xn2[:, r, :, :], rearr(scr["xn2T"])[:, :, tt]), d0)
            ld_x1 = P.dma("sync", dma16(x1b[:, r, :, :], rearr(scr["x1T"])[:, :, tt]), d0)

            att_steps = []
            for c in range(NCK):
                def f_le(t, c=c, r=r):
                    return t.matmul(psE[:, 0:TTn], LEX[:, c * 128:(c + 1) * 128], li[0:16, r, :],
                                    start=True, stop=True)
                d = [ld_li, wld[5]]
                if use_psE is not None:
                    d.append(use_psE)
                mle = P.step("tensor", f_le, d)

                def f_att(v, c=c, r=r):
                    return v.tensor_tensor(att[:, c, :], ar[:, r, c, :], psE[:, 0:TTn], OP.mult)
                d = [mle, ld_ar]
                if use_mid is not None:
                    d.append(use_mid)
                vat = P.step("vector", f_att, d)
                use_psE = vat
                att_steps.append(vat)

            def ev_s1(e, oc, r=r):
                e.tensor_scalar(s1[:, oc, :], psp[oc % 2][:, 0:TTn], bob[:, oc:oc + 1], None, OP.add)
                return e.tensor_tensor(s1[:, oc, :], s1[:, oc, :], xn2[:, r, oc, :], OP.add)
            evs1 = _proj_steps(P, WO, wld[0], lambda c: att[:, c, :], psp, use_psp,
                               ("vector", ev_s1), [wld[6], ld_xn2], att_steps, TTn)

            def f_win(t, r=r):
                last = None
                for c in range(NCK):
                    last = t.matmul(psF[0:64, 0:TTn], WIN[:, c, :], xn1[:, r, c, :],
                                    start=(c == 0), stop=(c == NCK - 1))
                return last
            d = [ld_xn1, wld[1]]
            if use_psF is not None:
                d.append(use_psF)
            mwin = P.step("tensor", f_win, d)
            ev_sm = P.step("vector", lambda v: v.tensor_scalar(sm[:, :], psF[0:64, 0:TTn],
                                                               binb[:, 0:1], None, OP.add),
                           [mwin, wld[7]])
            use_psF = ev_sm

            h_steps = []
            for c in range(NCK):
                def f_ex(t, c=c):
                    return t.matmul(psE[:, 0:TTn], W1E[:, c * 128:(c + 1) * 128], sm[:, :],
                                    start=True, stop=True)
                d = [ev_sm, wld[3]]
                if use_psE is not None:
                    d.append(use_psE)
                mex = P.step("tensor", f_ex, d)
                s_h = P.step("scalar", lambda s, c=c: s.activation(hb[:, c, :], psE[:, 0:TTn],
                                                                   AF.Relu, bias=b1b[:, c:c + 1]),
                             [mex, wld[8]])
                use_psE = s_h
                h_steps.append(s_h)

            def f_red(t):
                last = None
                for c in range(NCK):
                    last = t.matmul(psF[0:64, 0:TTn], W2R[:, c * 64:(c + 1) * 64], hb[:, c, :],
                                    start=(c == 0), stop=(c == NCK - 1))
                return last
            mred = P.step("tensor", f_red, h_steps + [wld[4], ev_sm])
            ev_gsm = P.step("vector", lambda v: v.tensor_scalar(gsm[:, :], psF[0:64, 0:TTn],
                                                                b2b[:, 0:1], None, OP.add),
                            [mred, wld[9]])
            use_psF = ev_gsm

            def ev_of(e, oc):
                return e.tensor_scalar(of[:, oc, :], psp[oc % 2][:, 0:TTn], boutb[:, oc:oc + 1],
                                       None, OP.add)
            def wout_mm():
                evs = []
                for oc in range(NCK):
                    def f_mm(t, oc=oc):
                        return t.matmul(psp[oc % 2][:, 0:TTn], WOUT[:, oc * 128:(oc + 1) * 128],
                                        gsm[:, :], start=True, stop=True)
                    d = [ev_gsm, wld[2]]
                    if use_psp[oc % 2] is not None:
                        d.append(use_psp[oc % 2])
                    mp = P.step("tensor", f_mm, d)
                    ev = P.step("vector", lambda e, oc=oc: ev_of(e, oc), [mp, wld[10]])
                    use_psp[oc % 2] = ev
                    evs.append(ev)
                return evs
            evs_of = wout_mm()

            s_sig = P.step("scalar", lambda s: s.activation(oft[:, :, :], of[:, :, :], AF.Sigmoid),
                           evs_of)

            def f_out(v, r=r):
                v.tensor_tensor(oft[:, :, :], oft[:, :, :], of[:, :, :], OP.mult)
                v.tensor_tensor(oft[:, :, :], oft[:, :, :], s1[:, :, :], OP.mult)
                return v.tensor_tensor(outs[:, r, :, :], x1b[:, r, :, :], oft[:, :, :], OP.add)
            v_out = P.step("vector", f_out, [s_sig, ld_x1] + evs1)
            use_mid = v_out

            st = P.dma("gpsimd", dma16(rearr(outT)[:, :, tt], outs[:, r, :, :]), [v_out])
            ring[r] = st
        P.play()


def build(debug=(), phases=(1, 2, 3, 4, 5, 6, 7), preload=()):
    nc = bass.Bass("TRN2", target_bir_lowering=False, detect_race_conditions=False)
    B = Builder(nc, debug, preload)

    B.dram_in("xT", [D, T])
    B.dram_in("wrT", [D, D]); B.dram_in("wkT", [D, D]); B.dram_in("wvT", [D, D])
    B.dram_in("wgT", [D, D]); B.dram_in("woT", [D, D])
    B.dram_in("maa_w1", [D, 160])
    B.dram_in("w2h", [32, 5 * D])
    B.dram_in("c5col", [128, 5 * NCK])
    B.dram_in("tmx", [128, NCK]); B.dram_in("tdec", [128, NCK])
    B.dram_in("dw1", [D, 64]); B.dram_in("dw2", [64, D])
    B.dram_in("lnw0", [128, NCK]); B.dram_in("lnb0", [128, NCK])
    B.dram_in("wrfT", [D, D]); B.dram_in("wkfT", [D, D]); B.dram_in("wvfT", [D, D])
    B.dram_in("mur", [128, NCK]); B.dram_in("muk", [128, NCK])
    B.dram_in("normw1", [128, NCK])
    B.dram_in("wqT1", [D, D]); B.dram_in("wkT1", [D, D]); B.dram_in("wvT1", [D, D]); B.dram_in("woT1", [D, D])
    B.dram_in("bq1", [128, NCK]); B.dram_in("bk1", [128, NCK]); B.dram_in("bv1", [128, NCK]); B.dram_in("bo1", [128, NCK])
    B.dram_in("win1", [D, 64]); B.dram_in("bin1", [64, 1])
    B.dram_in("w1exp", [64, NCK * 128]); B.dram_in("b1f", [128, NCK])
    B.dram_in("w2red", [128, NCK * 64]); B.dram_in("b2c", [64, 1])
    B.dram_in("wout1", [64, D]); B.dram_in("bout1", [128, NCK])
    B.dram_in("ones128", [128, 1]); B.dram_in("ones1x128", [1, 128])
    B.dram_in("gn2", [128, 2]); B.dram_in("gnexp", [2, 128]); B.dram_in("ident", [128, 128])
    B.dram_in("lexp", [16, NCK * 128])
    B.dram_in("trimask2", [128, 256], U8)
    B.dram_in("dmask", [128, 128], U8); B.dram_in("dmaski", [128, 128], U8)

    for nm, shp in [
        ("xnT", [D, T]), ("xxxT", [32, 5 * T]), ("rT", [D, T]), ("kT", [D, T]),
        ("gpreT", [D, T]), ("ewT", [D, T]), ("csT", [D, T]), ("vtok", [T, D]),
        ("ynT", [D, T]), ("x1T", [D, T]), ("xn1T", [D, T]), ("xn2T", [D, T]),
        ("q1T", [D, T]), ("k1T", [D, T]), ("v1tok", [T, D]), ("attraw", [D, T]),
        ("l1s", [16, T]), ("linv1", [16, T]), ("gn_a", [2, T]), ("gn_b", [2, T]),
        ("gT", [D, T]), ("s0T", [D, T]),
    ]:
        B.scratch(nm, shp)

    outT = nc.dram_tensor("outT", [D, T], F32, kind="ExternalOutput")
    B.out = outT

    if 1 in phases:
        phase_p1a(nc, B)
    if 2 in phases:
        phase_p1b(nc, B, 1)
        phase_p1b(nc, B, 2)
    if 3 in phases:
        phase_p2(nc, B)
    if 4 in phases:
        phase_p3a(nc, B)
        phase_p3b(nc, B)
    if 5 in phases:
        phase_p4(nc, B)
    if 6 in phases:
        phase_p5(nc, B)
    if 7 in phases:
        phase_p6(nc, B)
    return nc


def _host_consts(inputs):
    g = inputs
    d = {}
    d["wrT"] = g["L0_attn_W_r"].T
    d["wkT"] = g["L0_attn_W_k"].T
    d["wvT"] = g["L0_attn_W_v"].T
    d["wgT"] = g["L0_attn_W_g"].T
    d["woT"] = g["L0_attn_W_o"].T
    d["maa_w1"] = g["L0_attn_time_maa_w1"]
    w2 = g["L0_attn_time_maa_w2"]
    d["w2h"] = np.concatenate([w2[f] for f in range(5)], axis=1)
    c5 = np.stack([g["L0_attn_time_maa_w"].reshape(D), g["L0_attn_time_maa_k"].reshape(D),
                   g["L0_attn_time_maa_v"].reshape(D), g["L0_attn_time_maa_r"].reshape(D),
                   g["L0_attn_time_maa_g"].reshape(D)])
    col = lambda v: np.ascontiguousarray(np.asarray(v).reshape(NCK, 128).T)
    d["c5col"] = np.concatenate([col(c5[f]) for f in range(5)], axis=1)
    d["tmx"] = col(g["L0_attn_time_maa_x"])
    d["tdec"] = col(g["L0_attn_time_decay"])
    d["dw1"] = g["L0_attn_time_decay_w1"]
    d["dw2"] = g["L0_attn_time_decay_w2"]
    d["lnw0"] = col(g["L0_attn_ln_w"])
    d["lnb0"] = col(g["L0_attn_ln_b"])
    d["wrfT"] = g["L0_ffn_W_r"].T
    d["wkfT"] = g["L0_ffn_W_k"].T
    d["wvfT"] = g["L0_ffn_W_v"].T
    d["mur"] = col(g["L0_ffn_mu_r"])
    d["muk"] = col(g["L0_ffn_mu_k"])
    d["normw1"] = col(g["L1_attn_norm_w"])
    d["wqT1"] = g["L1_attn_Wq"].T
    d["wkT1"] = g["L1_attn_Wk"].T
    d["wvT1"] = g["L1_attn_Wv"].T
    d["woT1"] = g["L1_attn_Wo"].T
    d["bq1"] = col(g["L1_attn_bq"])
    d["bk1"] = col(g["L1_attn_bk"])
    d["bv1"] = col(g["L1_attn_bv"])
    d["bo1"] = col(g["L1_attn_bo"])
    d["win1"] = g["L1_ffn_Win"]
    d["bin1"] = np.asarray(g["L1_ffn_bin_"]).reshape(64, 1)
    W1 = np.asarray(g["L1_ffn_W1"])
    w1exp = np.zeros((64, NCK * 128), dtype=np.float32)
    for c in range(NCK):
        for p in range(128):
            m = 8 * c + p // 16
            w1exp[m, c * 128 + p] = W1[m, p % 16]
    d["w1exp"] = w1exp
    d["b1f"] = col(np.asarray(g["L1_ffn_b1"]).reshape(D))
    W2 = np.asarray(g["L1_ffn_W2"])
    w2red = np.zeros((128, NCK * 64), dtype=np.float32)
    for c in range(NCK):
        for p in range(128):
            m = 8 * c + p // 16
            w2red[p, c * 64 + m] = W2[m, p % 16]
    d["w2red"] = w2red
    d["b2c"] = np.asarray(g["L1_ffn_b2"]).reshape(64, 1)
    d["wout1"] = g["L1_ffn_Wout"]
    d["bout1"] = col(g["L1_ffn_bout"])
    d["ones128"] = np.ones((128, 1), dtype=np.float32)
    d["ones1x128"] = np.ones((1, 128), dtype=np.float32)
    gn2 = np.zeros((128, 2), dtype=np.float32)
    gn2[0:64, 0] = 1.0
    gn2[64:128, 1] = 1.0
    d["gn2"] = gn2
    gnexp = np.zeros((2, 128), dtype=np.float32)
    gnexp[0, 0:64] = 1.0
    gnexp[1, 64:128] = 1.0
    d["gnexp"] = gnexp
    d["ident"] = np.eye(128, dtype=np.float32)
    lexp = np.zeros((16, NCK * 128), dtype=np.float32)
    for c in range(NCK):
        for p in range(128):
            lexp[2 * c + p // 64, c * 128 + p] = 1.0
    d["lexp"] = lexp
    tri = (np.arange(128)[None, :] >= np.arange(128)[:, None]).astype(np.uint8)
    d["trimask2"] = np.tile(tri, (1, 2))
    d["dmask"] = tri
    d["dmaski"] = (1 - tri).astype(np.uint8)
    out = {}
    for k, v in d.items():
        v = np.asarray(v)
        if v.dtype == np.uint8:
            out[k] = np.ascontiguousarray(v)
        else:
            out[k] = np.ascontiguousarray(v.astype(np.float32))
    return out


def kernel(**inputs):
    consts = _host_consts(inputs)
    nc = build()
    x = np.asarray(inputs["L0_attn_x"])
    in_maps = []
    for b in range(8):
        m = dict(consts)
        m["xT"] = np.ascontiguousarray(x[b].T)
        in_maps.append(m)
    res = run_bass_kernel_spmd(nc, in_maps, core_ids=list(range(8)))
    out = np.stack([res.results[b]["outT"].T for b in range(8)], axis=0)
    return np.ascontiguousarray(out.astype(np.float32))


# version 10
# speedup vs baseline: 1.0705x; 1.0430x over previous
import numpy as np
import contextlib
import concourse.bass as bass
import concourse.mybir as mybir
from concourse.bass_utils import run_bass_kernel_spmd
from concourse.alu_op_type import AluOpType as OP

F32 = mybir.dt.float32
BF16 = mybir.dt.bfloat16
U8 = mybir.dt.uint8
AF = mybir.ActivationFunctionType

T = 4096
D = 1024
NCK = 8
H = 16
HD = 64
EXP_SHIFT = 12.0

LN_EPS = 1e-5
GN_EPS = 1e-5 * 64.0
RMS_EPS = 1e-6

ENGS = ("sync", "tensor", "vector", "scalar", "gpsimd")


class Prog:

    def __init__(self, nc, name):
        self.nc = nc
        self.name = name
        self.steps = []
        self.cum = []

    def step(self, engine, fn, deps=(), inc=1):
        assert engine in ENGS
        for d in deps:
            assert 0 <= d < len(self.steps), (d, len(self.steps))
        cum_prev = 0
        for i in range(len(self.steps) - 1, -1, -1):
            if self.steps[i][0] == engine:
                cum_prev = self.cum[i]
                break
        self.steps.append([engine, fn, inc, tuple(deps)])
        self.cum.append(cum_prev + inc)
        return len(self.steps) - 1

    def dma(self, engine, fn, deps=()):
        return self.step(engine, fn, deps, inc=16)

    def play(self):
        nc = self.nc
        with nc.Block() as blk:
            sems = {}
            ctx = []
            for e in ENGS:
                if any(s[0] == e for s in self.steps):
                    cm = nc.semaphore(f"{self.name}_{e}")
                    sems[e] = cm.__enter__()
                    ctx.append(cm)
            watermark = {}

            def play_engine(eng_name, eng):
                dma_cum = 0
                for idx, (e, fn, inc, deps) in enumerate(self.steps):
                    if e != eng_name:
                        continue
                    if inc == 16:
                        if dma_cum > 0:
                            eng.wait_ge(sems[eng_name], dma_cum)
                        dma_cum = self.cum[idx]
                    for d in deps:
                        de, _, _, _ = self.steps[d]
                        need = self.cum[d]
                        key = (eng_name, de)
                        if watermark.get(key, -1) >= need:
                            continue
                        watermark[key] = need
                        eng.wait_ge(sems[de], need)
                    ret = fn(eng)
                    if isinstance(ret, int):
                        assert ret == inc, (idx, ret, inc)
                    elif ret is None:
                        raise RuntimeError(f"step {idx} returned None; must return last instruction")
                    else:
                        ret.then_inc(sems[eng_name], inc)

            for e in ENGS:
                if e not in sems:
                    continue
                if e == "sync":
                    @blk.sync
                    def _(s, e=e):
                        play_engine(e, s)
                elif e == "tensor":
                    @blk.tensor
                    def _(s, e=e):
                        play_engine(e, s)
                elif e == "vector":
                    @blk.vector
                    def _(s, e=e):
                        play_engine(e, s)
                elif e == "scalar":
                    @blk.scalar
                    def _(s, e=e):
                        play_engine(e, s)
                elif e == "gpsimd":
                    @blk.gpsimd
                    def _(s, e=e):
                        play_engine(e, s)
            for cm in ctx:
                cm.__exit__(None, None, None)


def dma16(ap_out, ap_in):
    def fn(eng):
        return eng.dma_start(ap_out, ap_in)
    return fn


def rearr(dram, p=128):
    return dram.ap().rearrange("(c p) t -> p c t", p=p)


class Builder:
    def __init__(self, nc: bass.Bass, debug=(), preload=()):
        self.nc = nc
        self.debug = set(debug)
        self.preload = set(preload)
        self.inp = {}
        self.scr = {}

    def dram_in(self, name, shape, dtype=F32):
        h = self.nc.dram_tensor(name, shape, dtype, kind="ExternalInput")
        self.inp[name] = h
        return h

    def scratch(self, name, shape, dtype=F32):
        kind = "ExternalOutput" if name in self.debug else "Internal"
        if name in self.preload:
            kind = "ExternalInput"
        h = self.nc.dram_tensor(name, shape, dtype, kind=kind)
        self.scr[name] = h
        return h


def phase_p1a(nc, B):
    TT = 512
    NT = T // TT
    inp, scr = B.inp, B.scr
    with (
        nc.sbuf_tensor("p1a_x", [128, 2, NCK, TT], F32) as xb,
        nc.sbuf_tensor("p1a_sq", [128, NCK, TT], F32) as sqb,
        nc.sbuf_tensor("p1a_xn", [128, 2, NCK, TT], F32) as xnb,
        nc.sbuf_tensor("p1a_ones", [128, 1], F32) as onescol,
        nc.sbuf_tensor("p1a_ones1", [1, 128], F32) as onesrow,
        nc.sbuf_tensor("p1a_rows", [1, 6 * TT], F32) as rows,
        nc.psum_tensor("p1a_ps_s", [1, 2 * TT], F32) as ps_s,
        nc.psum_tensor("p1a_ps_q", [1, 2 * TT], F32) as ps_q,
        nc.psum_tensor("p1a_ps_mu", [128, TT], F32) as ps_mu,
        nc.psum_tensor("p1a_ps_rs", [128, TT], F32) as ps_rs,
    ):
        P = Prog(nc, "p1a")

        def R(i):
            return rows[0:1, i * TT:(i + 1) * TT]

        c_ones = P.dma("sync", dma16(onescol[:, :], inp["ones128"].ap()))
        c_ones1 = P.dma("sync", dma16(onesrow[:, :], inp["ones1x128"].ap()))

        last_use = [None, None]
        bc_use = None
        stat_use = None
        use_sq = None
        for i in range(NT):
            r = i % 2
            deps = [last_use[r]] if last_use[r] is not None else []
            ldx = P.dma("sync", dma16(xb[:, r, :, :], rearr(inp["xT"])[:, :, i * TT:(i + 1) * TT]), deps)
            sq = P.step("scalar", lambda s, r=r: s.activation(sqb[:, :, :], xb[:, r, :, :], AF.Square),
                        [ldx] + ([use_sq] if use_sq else []))

            def mm_stats(t, r=r, i=i):
                p = (i % 2) * TT
                for c in range(NCK):
                    t.matmul(ps_s[0:1, p:p + TT], onescol[:, :], xb[:, r, c, :],
                             start=(c == 0), stop=(c == NCK - 1))
                last = None
                for c in range(NCK):
                    last = t.matmul(ps_q[0:1, p:p + TT], onescol[:, :], sqb[:, c, :],
                                    start=(c == 0), stop=(c == NCK - 1))
                return last
            mst = P.step("tensor", mm_stats, [ldx, sq, c_ones] + ([stat_use] if stat_use else []))
            use_sq = mst

            def v_stats(v, i=i):
                p = (i % 2) * TT
                v.tensor_scalar(R(2), ps_s[0:1, p:p + TT], 1.0 / D, None, OP.mult)
                v.tensor_scalar(R(1), ps_q[0:1, p:p + TT], 1.0 / D, None, OP.mult)
                v.tensor_tensor(R(0), R(2), R(2), OP.mult)
                v.tensor_tensor(R(3), R(1), R(0), OP.subtract)
                return v.tensor_scalar(R(3), R(3), LN_EPS, None, OP.add)
            vst = P.step("vector", v_stats, [mst])
            stat_use = vst

            def s_rstd(s):
                s.activation(R(5), R(3), AF.Ln)
                return s.activation(R(4), R(5), AF.Exp, scale=-0.5)
            srs = P.step("scalar", s_rstd, [vst])

            def v_newton(v):
                v.tensor_tensor(R(5), R(4), R(4), OP.mult)
                v.tensor_tensor(R(5), R(5), R(3), OP.mult)
                v.tensor_scalar(R(5), R(5), -0.5, 1.5, OP.mult, OP.add)
                return v.tensor_tensor(R(4), R(4), R(5), OP.mult)
            vnw = P.step("vector", v_newton, [srs])

            def mm_bcast(t):
                t.matmul(ps_mu[:, :], onesrow[:, :], R(2), start=True, stop=True)
                return t.matmul(ps_rs[:, :], onesrow[:, :], R(4), start=True, stop=True)
            bc = P.step("tensor", mm_bcast, [vnw, c_ones1] + ([bc_use] if bc_use else []))

            def v_xn(v, r=r):
                last = None
                for c in range(NCK):
                    v.tensor_tensor(xnb[:, r, c, :], xb[:, r, c, :], ps_mu[:, :], OP.subtract)
                    last = v.tensor_tensor(xnb[:, r, c, :], xnb[:, r, c, :], ps_rs[:, :], OP.mult)
                return last
            vxn = P.step("vector", v_xn, [bc])
            bc_use = vxn

            st = P.dma("gpsimd", dma16(rearr(scr["xnT"])[:, :, i * TT:(i + 1) * TT], xnb[:, r, :, :]), [vxn])
            last_use[r] = st
        P.play()



def phase_p1b(nc, B, part):
    TT = 256
    NT = T // TT
    inp, scr = B.inp, B.scr
    projs = ["w", "k", "r"] if part == 1 else ["g", "v"]
    f_of = {"w": 0, "k": 1, "v": 2, "r": 3, "g": 4}
    out_of = {"k": scr["kT"], "r": scr["rT"], "g": scr["gpreT"]}
    pfx = f"p1b{part}"
    with contextlib.ExitStack() as _es:
        xnb = _es.enter_context(nc.sbuf_tensor(f"{pfx}_xn", [128, 2, NCK, TT + 2], F32))
        dxb = _es.enter_context(nc.sbuf_tensor(f"{pfx}_dx", [128, NCK, TT], F32))
        mixb = _es.enter_context(nc.sbuf_tensor(f"{pfx}_mix", [128, NCK, TT] if part == 1 else [1, 1], F32))
        xxxb = _es.enter_context(nc.sbuf_tensor(f"{pfx}_xxx", [32, 2, 5, TT], F32))
        xsb = _es.enter_context(nc.sbuf_tensor(f"{pfx}_xs", [128, 2, NCK, TT], F32))
        stage = _es.enter_context(nc.sbuf_tensor(f"{pfx}_stage", [128, 2, NCK, TT], F32))
        w1b = _es.enter_context(nc.sbuf_tensor(f"{pfx}_w1", [128, NCK, 160] if part == 1 else [1, 1], F32))
        w2b = _es.enter_context(nc.sbuf_tensor(f"{pfx}_w2h", [32, 5 * D], F32))
        c5b = _es.enter_context(nc.sbuf_tensor(f"{pfx}_c5", [128, 5 * NCK], F32))
        tmxb = _es.enter_context(nc.sbuf_tensor(f"{pfx}_tmx", [128, NCK] if part == 1 else [1, 1], F32))
        WA = _es.enter_context(nc.sbuf_tensor(f"{pfx}_WA", [128, NCK, D], F32))
        WB = _es.enter_context(nc.sbuf_tensor(f"{pfx}_WB", [128, NCK, D], F32))
        dw1b = _es.enter_context(nc.sbuf_tensor(f"{pfx}_dw1", [128, NCK, 64] if part == 1 else [1, 1], F32))
        dw2b = _es.enter_context(nc.sbuf_tensor(f"{pfx}_dw2", [64, D] if part == 1 else [1, 1], F32))
        tdecb = _es.enter_context(nc.sbuf_tensor(f"{pfx}_tdec", [128, NCK] if part == 1 else [1, 1], F32))
        tdb = _es.enter_context(nc.sbuf_tensor(f"{pfx}_td", [64, TT] if part == 1 else [1, 1], F32))
        ewb = _es.enter_context(nc.sbuf_tensor(f"{pfx}_ew", [128, NCK, TT] if part == 1 else [1, 1], F32))
        csb = _es.enter_context(nc.sbuf_tensor(f"{pfx}_cs", [128, 2, NCK, TT] if part == 1 else [1, 1], F32))
        psx1 = _es.enter_context(nc.psum_tensor(f"{pfx}_psx1", [96, 512], F32))
        psx2 = _es.enter_context(nc.psum_tensor(f"{pfx}_psx2", [64, 512], F32))
        psm0 = _es.enter_context(nc.psum_tensor(f"{pfx}_psm0", [128, 512], F32))
        psm1 = _es.enter_context(nc.psum_tensor(f"{pfx}_psm1", [128, 512], F32))
        psp0 = _es.enter_context(nc.psum_tensor(f"{pfx}_psp0", [128, 512], F32))
        psp1 = _es.enter_context(nc.psum_tensor(f"{pfx}_psp1", [128, 512], F32))
        psd = _es.enter_context(nc.psum_tensor(f"{pfx}_psd", [64, 512] if part == 1 else [1, 1], F32))
        psm = [psm0, psm1]
        psp = [psp0, psp1]
        P = Prog(nc, pfx)

        pre = {}
        pre["w1"] = P.dma("sync", dma16(w1b[:, :, :], inp["maa_w1"].ap().rearrange("(c p) o -> p c o", p=128))) if part == 1 else None
        pre["w2"] = P.dma("sync", dma16(w2b[:, :], inp["w2h"].ap()))
        pre["c5"] = P.dma("sync", dma16(c5b[:, :], inp["c5col"].ap()))
        pre["tmx"] = P.dma("sync", dma16(tmxb[:, :], inp["tmx"].ap())) if part == 1 else None
        WA_src = inp["wkT"] if part == 1 else inp["wgT"]
        WB_src = inp["wrT"] if part == 1 else inp["wvT"]
        pre["WA"] = P.dma("sync", dma16(WA[:, :, :], WA_src.ap().rearrange("(c p) o -> p c o", p=128)))
        pre["WB"] = P.dma("sync", dma16(WB[:, :, :], WB_src.ap().rearrange("(c p) o -> p c o", p=128)))
        if part == 1:
            pre["dw1"] = P.dma("sync", dma16(dw1b[:, :, :], inp["dw1"].ap().rearrange("(c p) o -> p c o", p=128)))
            pre["dw2"] = P.dma("sync", dma16(dw2b[:, :], inp["dw2"].ap()))
            pre["tdec"] = P.dma("sync", dma16(tdecb[:, :], inp["tdec"].ap()))
        ring_xn = [None, None]
        ring_ew = None
        ring_xs = [None, None]
        ring_stage = [None, None]
        ring_ewcs = [None, None]
        ring_xxx = [None, None]
        use_mix = None
        use_dx = None
        use_psx = None
        use_psm = [None, None]
        use_psp = [None, None]
        use_psd = None
        prev_scan = None

        for i in range(NT):
            r = i % 2
            lo, hi = i * TT - 1, (i + 1) * TT + 1
            s_lo, s_hi = max(lo, 0), min(hi, T)
            d_lo = s_lo - lo
            d_hi = d_lo + (s_hi - s_lo)
            deps = [ring_xn[r]] if ring_xn[r] is not None else []
            ldx = P.dma("sync", dma16(xnb[:, r, :, d_lo:d_hi], rearr(scr["xnT"])[:, :, s_lo:s_hi]), deps)
            if part == 2:
                xdeps = [ring_xxx[r]] if ring_xxx[r] is not None else []
                ldxxx = P.dma("sync", dma16(
                    xxxb[0:32, r, :, :],
                    scr["xxxT"].ap().rearrange("p (f t) -> p f t", f=5)[:, :, i * TT:(i + 1) * TT]), xdeps)

            def v_prep(v, r=r, i=i):
                bx = xnb[:, r, :, :]
                if i == 0:
                    v.memset(xnb[:, r, :, 0:1], 0.0)
                if i == NT - 1:
                    v.memset(xnb[:, r, :, TT + 1:TT + 2], 0.0)
                v.tensor_tensor(dxb[:, :, :], bx[:, :, 0:TT], bx[:, :, 2:TT + 2], OP.add)
                last = v.scalar_tensor_tensor(dxb[:, :, :], dxb[:, :, :], 0.5, bx[:, :, 1:TT + 1],
                                              OP.mult, OP.subtract)
                if part == 1:
                    for c in range(NCK):
                        last = v.scalar_tensor_tensor(mixb[:, c, :], dxb[:, c, :], tmxb[:, c:c + 1],
                                                      bx[:, c, 1:TT + 1], OP.mult, OP.add)
                return last
            vp_deps = [ldx]
            if part == 1:
                vp_deps.append(pre["tmx"])
                if use_mix is not None:
                    vp_deps.append(use_mix)
            if ring_xn[r] is not None:
                vp_deps.append(ring_xn[r])
            if use_dx is not None:
                vp_deps.append(use_dx)
            vprep = P.step("vector", v_prep, vp_deps)

            if part == 1:
                def mm_maa1(t):
                    for c in range(NCK):
                        t.matmul(psx1[:, 0:TT], w1b[:, c, 0:96], mixb[:, c, :],
                                 start=(c == 0), stop=(c == NCK - 1))
                    last = None
                    for c in range(NCK):
                        last = t.matmul(psx2[:, 0:TT], w1b[:, c, 96:160], mixb[:, c, :],
                                        start=(c == 0), stop=(c == NCK - 1))
                    return last
                mmx = P.step("tensor", mm_maa1, [vprep, pre["w1"]] + ([use_psx] if use_psx else []))
                use_mix = mmx

                def s_xxx(s, r=r):
                    s.activation(xxxb[0:32, r, 0, :], psx1[0:32, 0:TT], AF.Tanh)
                    s.activation(xxxb[0:32, r, 1, :], psx1[32:64, 0:TT], AF.Tanh)
                    s.activation(xxxb[0:32, r, 2, :], psx1[64:96, 0:TT], AF.Tanh)
                    s.activation(xxxb[0:32, r, 3, :], psx2[0:32, 0:TT], AF.Tanh)
                    return s.activation(xxxb[0:32, r, 4, :], psx2[32:64, 0:TT], AF.Tanh)
                sx_deps = [mmx]
                if ring_xxx[r] is not None:
                    sx_deps.append(ring_xxx[r])
                sxxx = P.step("scalar", s_xxx, sx_deps)
                use_psx = sxxx
                stxxx = P.dma("gpsimd", dma16(
                    scr["xxxT"].ap().rearrange("p (f t) -> p f t", f=5)[:, :, i * TT:(i + 1) * TT],
                    xxxb[0:32, r, :, :]), [sxxx])
                xxx_ready = sxxx
                xxx_consumers = [stxxx]
            else:
                xxx_ready = ldxxx
                xxx_consumers = []

            last_step_of_tile = None
            for pi, p in enumerate(projs):
                pr = pi % 2
                f = f_of[p]
                xs_steps = []
                for oc in range(NCK):
                    def mm_one(t, pr=pr, f=f, r=r, oc=oc):
                        return t.matmul(psm[pr][:, 0:TT], w2b[:, f * D + oc * 128: f * D + (oc + 1) * 128],
                                        xxxb[0:32, r, f, :], start=True, stop=True)
                    mdeps = [xxx_ready, vprep, pre["w2"]]
                    if use_psm[pr] is not None:
                        mdeps.append(use_psm[pr])
                    mt = P.step("tensor", mm_one, mdeps)

                    def v_xs(v, pr=pr, r=r, oc=oc, f=f):
                        v.scalar_tensor_tensor(xsb[:, pr, oc, :], psm[pr][:, 0:TT],
                                               c5b[:, f * NCK + oc:f * NCK + oc + 1],
                                               dxb[:, oc, :], OP.add, OP.mult)
                        return v.tensor_tensor(xsb[:, pr, oc, :], xsb[:, pr, oc, :],
                                               xnb[:, r, oc, 1:TT + 1], OP.add)
                    xdeps2 = [mt, pre["c5"]]
                    if ring_xs[pr] is not None:
                        xdeps2.append(ring_xs[pr])
                    vxs = P.step("vector", v_xs, xdeps2)
                    use_psm[pr] = vxs
                    xs_steps.append(vxs)
                xxx_consumers.append(xs_steps[-1])

                if p == "w":
                    def mm_d1(t, pr=pr):
                        last = None
                        for c in range(NCK):
                            last = t.matmul(psd[:, 0:TT], dw1b[:, c, :], xsb[:, pr, c, :],
                                            start=(c == 0), stop=(c == NCK - 1))
                        return last
                    d1deps = xs_steps + [pre["dw1"]]
                    if use_psd is not None:
                        d1deps.append(use_psd)
                    md1 = P.step("tensor", mm_d1, d1deps)
                    std = P.step("scalar", lambda s: s.activation(tdb[:, :], psd[:, 0:TT], AF.Tanh), [md1])
                    use_psd = std
                    ew_steps = []
                    for oc in range(NCK):
                        def mm_d2(t, oc=oc):
                            return t.matmul(psp[oc % 2][:, 0:TT], dw2b[:, oc * 128:(oc + 1) * 128],
                                            tdb[:, :], start=True, stop=True)
                        ddeps = [std, pre["dw2"]]
                        if use_psp[oc % 2] is not None:
                            ddeps.append(use_psp[oc % 2])
                        md2 = P.step("tensor", mm_d2, ddeps)

                        def s_ew(s, oc=oc):
                            return s.activation(ewb[:, oc, :], psp[oc % 2][:, 0:TT],
                                                AF.Exp, bias=tdecb[:, oc:oc + 1], scale=1.0)
                        ewdeps = [md2, pre["tdec"]]
                        if ring_ew is not None:
                            ewdeps.append(ring_ew)
                        if prev_scan is not None:
                            ewdeps.append(prev_scan)
                        sew = P.step("scalar", s_ew, ewdeps)
                        use_psp[oc % 2] = sew
                        ew_steps.append(sew)

                    def v_scan(v, r=r, i=i):
                        last = None
                        for oc in range(NCK):
                            init = 0.0 if i == 0 else csb[:, 1 - r, oc, TT - 1:TT]
                            last = v.tensor_tensor_scan(csb[:, r, oc, :], ewb[:, oc, :],
                                                        ewb[:, oc, :], init, OP.add, OP.bypass)
                        return last
                    sdeps = list(ew_steps) + ([prev_scan] if prev_scan else [])
                    if ring_ewcs[r] is not None:
                        sdeps.append(ring_ewcs[r])
                    vscan = P.step("vector", v_scan, sdeps)
                    prev_scan = vscan
                    stew = P.dma("gpsimd", dma16(rearr(scr["ewT"])[:, :, i * TT:(i + 1) * TT], ewb[:, :, :]),
                          list(ew_steps))
                    st2 = P.dma("gpsimd", dma16(rearr(scr["csT"])[:, :, i * TT:(i + 1) * TT], csb[:, r, :, :]),
                                [vscan])
                    ring_ewcs[r] = st2
                    ring_ew = stew
                    ring_xs[pr] = md1
                    last_step_of_tile = md1
                elif p != "v":
                    W = WA if (p in ("k", "g")) else WB
                    Wdep = pre["WA"] if (p in ("k", "g")) else pre["WB"]
                    evac_steps = []
                    for oc in range(NCK):
                        def mm_p(t, pr=pr, oc=oc, W=W):
                            last = None
                            for c in range(NCK):
                                last = t.matmul(psp[oc % 2][:, 0:TT], W[:, c, oc * 128:(oc + 1) * 128],
                                                xsb[:, pr, c, :], start=(c == 0), stop=(c == NCK - 1))
                            return last
                        pdeps = xs_steps + [Wdep]
                        if use_psp[oc % 2] is not None:
                            pdeps.append(use_psp[oc % 2])
                        mp = P.step("tensor", mm_p, pdeps)

                        def v_evac(v, oc=oc, r=r):
                            return v.tensor_copy(stage[:, r, oc, :], psp[oc % 2][:, 0:TT])
                        edeps = [mp]
                        if ring_stage[r] is not None:
                            edeps.append(ring_stage[r])
                        ev = P.step("vector", v_evac, edeps)
                        use_psp[oc % 2] = ev
                        evac_steps.append(ev)
                    stp = P.dma("gpsimd", dma16(rearr(out_of[p])[:, :, i * TT:(i + 1) * TT],
                                                stage[:, r, :, :]), evac_steps)
                    ring_stage[r] = stp
                    ring_xs[pr] = evac_steps[-1]
                    last_step_of_tile = evac_steps[-1]
                else:
                    evac_steps = []
                    stage_flat = stage[:, r, :, :].rearrange("p c t -> p (c t)")
                    for ts in range(TT // 128):
                        for half in range(2):
                            def mm_v(t, pr=pr, ts=ts, half=half):
                                last = None
                                for c in range(NCK):
                                    last = t.matmul(psp[half][:, :],
                                                    xsb[:, pr, c, ts * 128:(ts + 1) * 128],
                                                    WB[:, c, half * 512:(half + 1) * 512],
                                                    start=(c == 0), stop=(c == NCK - 1))
                                return last
                            vdeps = xs_steps + [pre["WB"]]
                            if use_psp[half] is not None:
                                vdeps.append(use_psp[half])
                            mv = P.step("tensor", mm_v, vdeps)

                            def v_evacv(v, ts=ts, half=half, stage_flat=stage_flat):
                                base = ts * 1024 + half * 512
                                return v.tensor_copy(stage_flat[:, base:base + 512], psp[half][:, :])
                            edeps = [mv]
                            if ring_stage[r] is not None:
                                edeps.append(ring_stage[r])
                            ev = P.step("vector", v_evacv, edeps)
                            use_psp[half] = ev
                            evac_steps.append(ev)
                    stv = P.dma("gpsimd", dma16(
                        scr["vtok"].ap().rearrange("(n p) d -> p n d", p=128)[:, 2 * i:2 * i + 2, :],
                        stage_flat.rearrange("p (n d) -> p n d", d=1024)),
                        evac_steps)
                    ring_stage[r] = stv
                    ring_xs[pr] = evac_steps[-1]
                    last_step_of_tile = evac_steps[-1]
            ring_xn[r] = last_step_of_tile
            use_dx = xs_steps[-1]
            ring_xxx[r] = xxx_consumers[-1] if xxx_consumers else None
        P.play()



P2_NPAIR = 8
P2_FWDLVL = 3
P2_FWD = True
P2_BWD = True
P2_GN = True


def phase_p2(nc, B):
    inp, scr = B.inp, B.scr
    CH = 128
    NCH = T // CH
    with contextlib.ExitStack() as _es:
        b_r = _es.enter_context(nc.sbuf_tensor("p2_r", [128, T], F32))
        b_k = _es.enter_context(nc.sbuf_tensor("p2_k", [128, T], F32))
        b_cs = _es.enter_context(nc.sbuf_tensor("p2_cs", [128, NCH, CH], F32))
        b_t0 = _es.enter_context(nc.sbuf_tensor("p2_t0", [128, NCH, CH], F32))
        b_t1 = _es.enter_context(nc.sbuf_tensor("p2_t1", [128, T], F32))
        b_rf = _es.enter_context(nc.sbuf_tensor("p2_rf", [128, T], F32))
        b_kf = _es.enter_context(nc.sbuf_tensor("p2_kf", [128, T], F32))
        b_rb = _es.enter_context(nc.sbuf_tensor("p2_rb", [128, T], F32))
        b_kb = _es.enter_context(nc.sbuf_tensor("p2_kb", [128, T], F32))
        b_v = _es.enter_context(nc.sbuf_tensor("p2_v", [128, NCH, CH], F32))
        selbuf = _es.enter_context(nc.sbuf_tensor("p2_sel", [128, 2, 256], F32))
        gstg = _es.enter_context(nc.sbuf_tensor("p2_gstg", [2, 2, 1024], F32))
        lstg = _es.enter_context(nc.sbuf_tensor("p2_lstg", [2, 2, 1024], F32))
        st1 = _es.enter_context(nc.sbuf_tensor("p2_st1", [128, 64], F32))
        st2 = _es.enter_context(nc.sbuf_tensor("p2_st2", [128, 64], F32))
        st3 = _es.enter_context(nc.sbuf_tensor("p2_st3", [128, 64], F32))
        st1x = _es.enter_context(nc.sbuf_tensor("p2_st4", [128, 64], F32))
        identb = _es.enter_context(nc.sbuf_tensor("p2_ident", [128, 128], F32))
        ktf_r = _es.enter_context(nc.sbuf_tensor("p2_ktfr", [128, 2, 128], F32))
        ktb_r = _es.enter_context(nc.sbuf_tensor("p2_ktbr", [128, 2, 128], F32))
        gn2b = _es.enter_context(nc.sbuf_tensor("p2_gn2", [128, 2], F32))
        gnexpb = _es.enter_context(nc.sbuf_tensor("p2_gnexp", [2, 128], F32))
        trib = _es.enter_context(nc.sbuf_tensor("p2_tri", [128, 256], U8))
        lnwb = _es.enter_context(nc.sbuf_tensor("p2_lnw", [128, NCK], F32))
        lnbb = _es.enter_context(nc.sbuf_tensor("p2_lnb", [128, NCK], F32))
        psA = [_es.enter_context(nc.psum_tensor(f"p2_psA{j}", [128, 512], F32)) for j in range(2)]
        psY = [_es.enter_context(nc.psum_tensor(f"p2_psY{j}", [128, 512], F32)) for j in range(2)]
        psT = [_es.enter_context(nc.psum_tensor(f"p2_psT{j}", [128, 512], F32)) for j in range(2)]
        U_sb = _es.enter_context(nc.sbuf_tensor("p2_U", [128, 32, 128], F32))

        y_sb = b_r
        ynb = b_k
        ysq = b_t1

        P = Prog(nc, "p2")
        c_id = P.dma("sync", dma16(identb[:, :], inp["ident"].ap()))
        c_gn2 = P.dma("sync", dma16(gn2b[:, :], inp["gn2"].ap()))
        c_gne = P.dma("sync", dma16(gnexpb[:, :], inp["gnexp"].ap()))
        c_tri = P.dma("sync", dma16(trib[:, :], inp["trimask2"].ap()))
        c_lnw = P.dma("sync", dma16(lnwb[:, :], inp["lnw0"].ap()))
        c_lnb = P.dma("sync", dma16(lnbb[:, :], inp["lnb0"].ap()))
        prev_pair = None

        for hp in range(P2_NPAIR):
            rows = slice(128 * hp, 128 * hp + 128)
            pdeps = [prev_pair] if prev_pair is not None else []
            ld_r = P.dma("sync", dma16(b_r[:, :], scr["rT"].ap()[rows, :]), pdeps)
            ld_k = P.dma("sync", dma16(b_k[:, :], scr["kT"].ap()[rows, :]), pdeps)
            ld_cs = P.dma("sync", dma16(b_cs[:, :, :].rearrange("p c t -> p (c t)"), scr["csT"].ap()[rows, :]), pdeps)
            ld_ew = P.dma("sync", dma16(b_t1[:, :], scr["ewT"].ap()[rows, :]), pdeps)
            ld_v = P.dma("sync", dma16(
                b_v[:, :, :], scr["vtok"].ap().rearrange("(n p) d -> p n d", p=128)[:, :, 128 * hp:128 * hp + 128]),
                pdeps)
            csf = b_cs[:, :, :].rearrange("p c t -> p (c t)")
            t0f = b_t0[:, :, :].rearrange("p c t -> p (c t)")

            def v_csb(v):
                return v.tensor_tensor(t0f, csf, b_t1[:, :], OP.subtract)
            s_csb = P.step("vector", v_csb, [ld_cs, ld_ew] + pdeps)

            def v_cfp(v):
                v.tensor_scalar(b_t1[:, :], csf, csf[:, 2048:2049], 60.0, OP.subtract, OP.min)
                return v.tensor_scalar(b_t1[:, :], b_t1[:, :], -60.0, None, OP.max)
            s_cfp = P.step("vector", v_cfp, [s_csb])

            def v_cbp(v):
                v.tensor_scalar(csf, t0f, t0f[:, 2048:2049], 60.0, OP.subtract, OP.min)
                return v.tensor_scalar(csf, csf, -60.0, None, OP.max)
            s_cbp = P.step("vector", v_cbp, [s_cfp])

            e1 = P.step("scalar", lambda s: s.activation(t0f, b_t1[:, :], AF.Exp, scale=-1.0), [s_cbp])
            m_rf = P.step("vector", lambda v: v.tensor_tensor(b_rf[:, :], b_r[:, :], t0f, OP.mult),
                          [e1, ld_r])
            e2 = P.step("scalar", lambda s: s.activation(t0f, b_t1[:, :], AF.Exp, scale=1.0), [m_rf])
            m_kf = P.step("vector", lambda v: v.tensor_tensor(b_kf[:, :], b_k[:, :], t0f, OP.mult),
                          [e2, ld_k])
            e3 = P.step("scalar", lambda s: s.activation(b_t1[:, :], csf, AF.Exp, scale=1.0), [m_kf])
            m_rb = P.step("vector", lambda v: v.tensor_tensor(b_rb[:, :], b_r[:, :], b_t1[:, :], OP.mult),
                          [e3])
            e4 = P.step("scalar", lambda s: s.activation(b_t1[:, :], csf, AF.Exp, scale=-1.0), [m_rb])
            m_kb = P.step("vector", lambda v: v.tensor_tensor(b_kb[:, :], b_k[:, :], b_t1[:, :], OP.mult),
                          [e4])

            rfh1 = csf
            kfh1 = t0f
            rbh1 = b_t1
            kbh1 = b_k
            cp1 = P.dma("gpsimd", dma16(rfh1[0:64, :], b_rf[64:128, :]), [m_rf, e3, e4])
            cp2 = P.dma("gpsimd", dma16(kfh1[0:64, :], b_kf[64:128, :]), [m_kf])
            cp3 = P.dma("gpsimd", dma16(rbh1[0:64, :], b_rb[64:128, :]), [m_rb])
            cp4 = P.dma("gpsimd", dma16(kbh1[0:64, :], b_kb[64:128, :]), [m_kb])
            prep = [m_rf, m_kf, m_rb, m_kb, ld_v, c_id, c_tri, cp1, cp2, cp3, cp4]

            if not P2_FWD:
                st_yn = P.dma("gpsimd", dma16(scr["ynT"].ap()[rows, :], b_rf[:, :]), prep)
                prev_pair = st_yn
                continue
            TA = [None] * NCH
            VA = [None] * NCH
            TB = [None] * NCH
            VB = [None] * NCH

            def rec_TA(c):
                cc = slice(c * CH, (c + 1) * CH)
                j = c % 2

                def f_TA(t, cc=cc, j=j):
                    t.matmul(psA[j][:, 0:128], b_kf[0:64, cc], b_rf[0:64, cc], start=True, stop=True)
                    t.matmul(psA[j][:, 128:256], kfh1[0:64, cc], rfh1[0:64, cc], start=True, stop=True)
                    t.matmul(psA[j][:, 256:384], b_kb[0:64, cc], b_rb[0:64, cc], start=True, stop=True)
                    t.matmul(psA[j][:, 384:512], kbh1[0:64, cc], rbh1[0:64, cc], start=True, stop=True)
                    t.transpose(psT[j][:, 0:128], b_kf[:, cc], identb[:, :])
                    return t.transpose(psT[j][:, 128:256], b_kb[:, cc], identb[:, :])
                d = list(prep)
                if c >= 2:
                    d.append(VB[c - 2])
                TA[c] = P.step("tensor", f_TA, d)

            def rec_VA(c):
                j = c % 2

                def f_VA(v, j=j, c=c):
                    v.select(selbuf[:, c % 2, :], trib[:, :], psA[j][:, 0:256], psA[j][:, 256:512])
                    v.tensor_copy(ktf_r[:, c % 2, :], psT[j][:, 0:128])
                    return v.tensor_copy(ktb_r[:, c % 2, :], psT[j][:, 128:256])
                d = [TA[c]]
                if c >= 2:
                    d.append(TB[c - 2])
                VA[c] = P.step("vector", f_VA, d)

            def rec_TB(c):
                cc = slice(c * CH, (c + 1) * CH)
                j = c % 2

                def f_TB(t, cc=cc, j=j, c=c):
                    t.matmul(psY[j][0:64, 0:128], b_v[:, c, 0:64], selbuf[:, c % 2, 0:128],
                             start=True, stop=True)
                    t.matmul(psY[j][64:128, 0:128], b_v[:, c, 64:128], selbuf[:, c % 2, 128:256],
                             start=True, stop=True, tile_position=(0, 64))
                    t.matmul(psT[j][0:64, 256:320], ktf_r[:, c % 2, 0:64], b_v[:, c, 0:64],
                             start=True, stop=True)
                    t.matmul(psT[j][64:128, 256:320], ktf_r[:, c % 2, 64:128], b_v[:, c, 64:128],
                             start=True, stop=True, tile_position=(0, 64))
                    t.matmul(psT[j][0:64, 320:384], ktb_r[:, c % 2, 0:64], b_v[:, c, 0:64],
                             start=True, stop=True)
                    return t.matmul(psT[j][64:128, 320:384], ktb_r[:, c % 2, 64:128], b_v[:, c, 64:128],
                                    start=True, stop=True, tile_position=(0, 64))
                d = [VA[c]]
                if c >= 2:
                    d.append(VB[c - 2])
                TB[c] = P.step("tensor", f_TB, d)

            def rec_VB(c):
                j = c % 2

                def f_VB(v, j=j, c=c):
                    v.tensor_copy(y_sb[:, c * CH:(c + 1) * CH], psY[j][:, 0:128])
                    return v.tensor_copy(U_sb[:, c, :], psT[j][:, 256:384])
                VB[c] = P.step("vector", f_VB, [TB[c]])

            for c in range(NCH + 1):
                if c < NCH:
                    rec_TA(c)
                    rec_VA(c)
                if c >= 1:
                    rec_TB(c - 1)
                    rec_VB(c - 1)

            def Spre_(c):
                return b_t1[:, c * 128:c * 128 + 64]

            def Ssuf_(c):
                return b_t1[:, c * 128 + 64:(c + 1) * 128]

            def f_scanS(v):
                v.memset(Spre_(0), 0.0)
                v.memset(Ssuf_(NCH - 1), 0.0)
                for c in range(1, NCH):
                    v.tensor_tensor(Spre_(c), Spre_(c - 1), U_sb[:, c - 1, 0:64], OP.add)
                last = None
                for c in range(NCH - 2, -1, -1):
                    last = v.tensor_tensor(Ssuf_(c), Ssuf_(c + 1), U_sb[:, c + 1, 64:128], OP.add)
                return last
            vscanS = P.step("vector", f_scanS, [VB[NCH - 1], VB[NCH - 2]])

            TC = [None] * NCH
            VC = [None] * NCH

            def rec_TC(c):
                cc = slice(c * CH, (c + 1) * CH)
                j = c % 2

                def f_TC(t, cc=cc, j=j, c=c):
                    t.matmul(psY[j][0:64, 0:128], Spre_(c)[0:64, :], b_rf[0:64, cc],
                             start=True, stop=False)
                    t.matmul(psY[j][0:64, 0:128], Ssuf_(c)[0:64, :], b_rb[0:64, cc],
                             start=False, stop=True)
                    t.matmul(psY[j][64:128, 0:128], Spre_(c)[64:128, :], b_rf[64:128, cc],
                             start=True, stop=False, tile_position=(64, 64))
                    return t.matmul(psY[j][64:128, 0:128], Ssuf_(c)[64:128, :], b_rb[64:128, cc],
                                    start=False, stop=True, tile_position=(64, 64))
                d = [vscanS]
                if c >= 2:
                    d.append(VC[c - 2])
                TC[c] = P.step("tensor", f_TC, d)

            def rec_VC(c):
                j = c % 2

                def f_VC(v, j=j, c=c):
                    return v.tensor_tensor(y_sb[:, c * CH:(c + 1) * CH], y_sb[:, c * CH:(c + 1) * CH],
                                           psY[j][:, 0:128], OP.add)
                VC[c] = P.step("vector", f_VC, [TC[c]])

            for c in range(NCH):
                rec_TC(c)
                rec_VC(c)
            scan_done = VC[NCH - 1]

            g_sq = P.step("scalar", lambda s: s.activation(ysq[:, :], y_sb[:, :], AF.Square), [scan_done])
            gst_list = []
            for jt in range(8):
                ts_ = slice(jt * 512, (jt + 1) * 512)
                jr = jt % 2

                def f_gmm(t, ts_=ts_):
                    t.matmul(psA[0][0:2, 0:512], gn2b[:, :], y_sb[:, ts_], start=True, stop=True)
                    return t.matmul(psT[0][0:2, 0:512], gn2b[:, :], ysq[:, ts_], start=True, stop=True)
                d = [g_sq, c_gn2, scan_done]
                if gst_list:
                    d.append(gst_list[-1])
                gmm = P.step("tensor", f_gmm, d)

                def f_gev(v, jr=jr):
                    v.tensor_copy(gstg[:, jr, 0:512], psA[0][0:2, 0:512])
                    return v.tensor_copy(gstg[:, jr, 512:1024], psT[0][0:2, 0:512])
                gev = P.step("vector", f_gev, [gmm] + ([gst_list[-2]] if len(gst_list) >= 2 else []))
                s1d = P.dma("gpsimd", dma16(scr["gn_a"].ap()[:, jt * 512:(jt + 1) * 512],
                                            gstg[:, jr, 0:512]), [gev])
                s2d = P.dma("gpsimd", dma16(scr["gn_b"].ap()[:, jt * 512:(jt + 1) * 512],
                                            gstg[:, jr, 512:1024]), [gev])
                gst_list.append(s2d)
            rs1 = P.dma("gpsimd", dma16(
                st1[:, :],
                scr["gn_a"].ap().rearrange("h (b c) -> (h b) c", c=64)), [gst_list[-1]])
            rs2 = P.dma("gpsimd", dma16(
                st2[:, :],
                scr["gn_b"].ap().rearrange("h (b c) -> (h b) c", c=64)), [gst_list[-1]])

            def f_var(v):
                v.tensor_scalar(st1[:, :], st1[:, :], 1.0 / 64.0, None, OP.mult)
                v.tensor_scalar(st2[:, :], st2[:, :], 1.0 / 64.0, None, OP.mult)
                v.tensor_tensor(st3[:, :], st1[:, :], st1[:, :], OP.mult)
                v.tensor_tensor(st2[:, :], st2[:, :], st3[:, :], OP.subtract)
                return v.tensor_scalar(st2[:, :], st2[:, :], GN_EPS, None, OP.add)
            g_var = P.step("vector", f_var, [rs1, rs2])

            def f_rs(s):
                s.activation(st3[:, :], st2[:, :], AF.Ln)
                return s.activation(st3[:, :], st3[:, :], AF.Exp, scale=-0.5)
            g_rs = P.step("scalar", f_rs, [g_var])

            def f_fix(v):
                v.tensor_tensor(st1x[:, :], st3[:, :], st3[:, :], OP.mult)
                v.tensor_tensor(st1x[:, :], st1x[:, :], st2[:, :], OP.mult)
                v.tensor_scalar(st1x[:, :], st1x[:, :], -0.5, 1.5, OP.mult, OP.add)
                v.tensor_tensor(st3[:, :], st3[:, :], st1x[:, :], OP.mult)
                v.tensor_tensor(st1[:, :], st1[:, :], st3[:, :], OP.mult)
                return v.tensor_scalar(st1[:, :], st1[:, :], -1.0, None, OP.mult)
            g_fix = P.step("vector", f_fix, [g_rs])

            bs1 = P.dma("gpsimd", dma16(
                scr["gn_a"].ap().rearrange("h (b c) -> (h b) c", c=64), st3[:, :]), [g_fix, rs1, rs2])
            bs2 = P.dma("gpsimd", dma16(
                scr["gn_b"].ap().rearrange("h (b c) -> (h b) c", c=64), st1[:, :]), [g_fix, rs1, rs2])

            app_list = []
            ldl_prev = None
            for jt in range(8):
                ts_ = slice(jt * 512, (jt + 1) * 512)
                jr = jt % 2

                def mkld(v=None):
                    pass
                dl = [bs1, bs2]
                if len(app_list) >= 2:
                    dl.append(app_list[-2])
                ld1 = P.dma("sync", dma16(lstg[:, jr, 0:512],
                                          scr["gn_a"].ap()[:, jt * 512:(jt + 1) * 512]), dl)
                ld2 = P.dma("sync", dma16(lstg[:, jr, 512:1024],
                                          scr["gn_b"].ap()[:, jt * 512:(jt + 1) * 512]), dl)

                def f_emm(t, jr=jr):
                    t.matmul(psA[1][:, 0:512], gnexpb[:, :], lstg[0:2, jr, 0:512], start=True, stop=True)
                    return t.matmul(psT[1][:, 0:512], gnexpb[:, :], lstg[0:2, jr, 512:1024], start=True, stop=True)
                d = [ld1, ld2, c_gne]
                if app_list:
                    d.append(app_list[-1])
                emm = P.step("tensor", f_emm, d)

                def f_app(v, ts_=ts_, hp=hp):
                    v.tensor_tensor(ynb[:, ts_], y_sb[:, ts_], psA[1][:, 0:512], OP.mult)
                    v.tensor_tensor(ynb[:, ts_], ynb[:, ts_], psT[1][:, 0:512], OP.add)
                    return v.tensor_scalar(ynb[:, ts_], ynb[:, ts_], lnwb[:, hp:hp + 1],
                                           lnbb[:, hp:hp + 1], OP.mult, OP.add)
                app_list.append(P.step("vector", f_app, [emm, c_lnw, c_lnb]))

            st_yn = P.dma("gpsimd", dma16(scr["ynT"].ap()[rows, :], ynb[:, :]), app_list)
            prev_pair = st_yn
        P.play()


def _proj_steps(P, W, Wdep, rhs_of, psp, use_psp, evac_fn, evac_deps_extra, mm_deps, TTn):
    evs = []
    for oc in range(NCK):
        def f_mm(t, oc=oc):
            last = None
            for c in range(NCK):
                last = t.matmul(psp[oc % 2][:, 0:TTn], W[:, c, oc * 128:(oc + 1) * 128],
                                rhs_of(c), start=(c == 0), stop=(c == NCK - 1))
            return last
        d = list(mm_deps) + [Wdep]
        if use_psp[oc % 2] is not None:
            d.append(use_psp[oc % 2])
        mp = P.step("tensor", f_mm, d)
        ev = P.step(evac_fn[0], lambda e, oc=oc: evac_fn[1](e, oc), [mp] + list(evac_deps_extra))
        use_psp[oc % 2] = ev
        evs.append(ev)
    return evs


def phase_p3a(nc, B):
    TTn = 256
    NT = T // TTn
    inp, scr = B.inp, B.scr
    with contextlib.ExitStack() as _es:
        yn = _es.enter_context(nc.sbuf_tensor("p3a_yn", [128, 2, NCK, TTn], F32))
        gp = _es.enter_context(nc.sbuf_tensor("p3a_gp", [128, 2, NCK, TTn], F32))
        gs = _es.enter_context(nc.sbuf_tensor("p3a_gs", [128, NCK, TTn], F32))
        yng = _es.enter_context(nc.sbuf_tensor("p3a_yng", [128, NCK, TTn], F32))
        stage = _es.enter_context(nc.sbuf_tensor("p3a_stage", [128, 2, NCK, TTn], F32))
        gout = _es.enter_context(nc.sbuf_tensor("p3a_gout", [128, 2, NCK, TTn], F32))
        WO = _es.enter_context(nc.sbuf_tensor("p3a_wo", [128, NCK, D], F32))
        psp0 = _es.enter_context(nc.psum_tensor("p3a_psp0", [128, 512], F32))
        psp1 = _es.enter_context(nc.psum_tensor("p3a_psp1", [128, 512], F32))
        psp = [psp0, psp1]
        P = Prog(nc, "p3a")
        wo_ld = P.dma("sync", dma16(WO[:, :, :], inp["woT"].ap().rearrange("(c p) o -> p c o", p=128)))
        ring_ld = [None, None]
        ring_st = [None, None]
        ring_g = [None, None]
        use_psp = [None, None]
        use_gs = None
        use_yng = None
        for i in range(NT):
            r = i % 2
            tt = slice(i * TTn, (i + 1) * TTn)
            d0 = [ring_ld[r]] if ring_ld[r] is not None else []
            ld_yn = P.dma("sync", dma16(yn[:, r, :, :], rearr(scr["ynT"])[:, :, tt]), d0)
            ld_gp = P.dma("sync", dma16(gp[:, r, :, :], rearr(scr["gpreT"])[:, :, tt]), d0)

            s_sig = P.step("scalar", lambda s, r=r: s.activation(gs[:, :, :], gp[:, r, :, :], AF.Sigmoid),
                           [ld_gp] + ([use_gs] if use_gs else []))

            def f_g(v, r=r):
                v.tensor_tensor(gout[:, r, :, :], gp[:, r, :, :], gs[:, :, :], OP.mult)
                return v.tensor_tensor(yng[:, :, :], yn[:, r, :, :], gout[:, r, :, :], OP.mult)
            d = [s_sig, ld_yn]
            if use_yng is not None:
                d.append(use_yng)
            if ring_g[r] is not None:
                d.append(ring_g[r])
            v_g = P.step("vector", f_g, d)
            use_gs = v_g

            st_g = P.dma("gpsimd", dma16(rearr(scr["gT"])[:, :, tt], gout[:, r, :, :]), [v_g])
            ring_g[r] = st_g

            def mk_evac(e, oc, r=r):
                return e.tensor_copy(stage[:, r, oc, :], psp[oc % 2][:, 0:TTn])
            evs = _proj_steps(P, WO, wo_ld, lambda c: yng[:, c, :], psp, use_psp,
                              ("vector", mk_evac), [ring_st[r]] if ring_st[r] is not None else [],
                              [v_g], TTn)
            use_yng = evs[-1]
            st_s0 = P.dma("gpsimd", dma16(rearr(scr["s0T"])[:, :, tt], stage[:, r, :, :]), evs)
            ring_st[r] = st_s0
            ring_ld[r] = evs[-1]
        P.play()


def phase_p3b(nc, B):
    TTn = 256
    NT = T // TTn
    inp, scr = B.inp, B.scr
    with contextlib.ExitStack() as _es:
        xn = _es.enter_context(nc.sbuf_tensor("p3b_xn", [128, 2, NCK, TTn + 1], F32))
        xb = _es.enter_context(nc.sbuf_tensor("p3b_x", [128, 2, NCK, TTn], F32))
        s0 = _es.enter_context(nc.sbuf_tensor("p3b_s0", [128, 2, NCK, TTn], F32))
        db = _es.enter_context(nc.sbuf_tensor("p3b_d", [128, NCK, TTn], F32))
        xr0 = _es.enter_context(nc.sbuf_tensor("p3b_xr0", [128, NCK, TTn], F32))
        xk0 = _es.enter_context(nc.sbuf_tensor("p3b_xk0", [128, NCK, TTn], F32))
        r0 = _es.enter_context(nc.sbuf_tensor("p3b_r0", [128, NCK, TTn], F32))
        k0 = _es.enter_context(nc.sbuf_tensor("p3b_k0", [128, NCK, TTn], F32))
        ffn = _es.enter_context(nc.sbuf_tensor("p3b_ffn", [128, NCK, TTn], F32))
        k0q = xr0
        fft = xk0
        WR = _es.enter_context(nc.sbuf_tensor("p3b_wr", [128, NCK, D], F32))
        WK = _es.enter_context(nc.sbuf_tensor("p3b_wk", [128, NCK, D], F32))
        WV = _es.enter_context(nc.sbuf_tensor("p3b_wv", [128, NCK, D], F32))
        murb = _es.enter_context(nc.sbuf_tensor("p3b_mur", [128, NCK], F32))
        mukb = _es.enter_context(nc.sbuf_tensor("p3b_muk", [128, NCK], F32))
        psp0 = _es.enter_context(nc.psum_tensor("p3b_psp0", [128, 512], F32))
        psp1 = _es.enter_context(nc.psum_tensor("p3b_psp1", [128, 512], F32))
        psp = [psp0, psp1]
        P = Prog(nc, "p3b")
        wld = [P.dma("sync", dma16(WR[:, :, :], inp["wrfT"].ap().rearrange("(c p) o -> p c o", p=128))),
               P.dma("sync", dma16(WK[:, :, :], inp["wkfT"].ap().rearrange("(c p) o -> p c o", p=128))),
               P.dma("sync", dma16(WV[:, :, :], inp["wvfT"].ap().rearrange("(c p) o -> p c o", p=128))),
               P.dma("sync", dma16(murb[:, :], inp["mur"].ap())),
               P.dma("sync", dma16(mukb[:, :], inp["muk"].ap()))]
        ring = [None, None]
        use_psp = [None, None]
        use_mid = None
        for i in range(NT):
            r = i % 2
            tt = slice(i * TTn, (i + 1) * TTn)
            lo = i * TTn - 1
            s_lo = max(lo, 0)
            d_lo = s_lo - lo
            d0 = [ring[r]] if ring[r] is not None else []
            ld_xn = P.dma("sync", dma16(xn[:, r, :, d_lo:TTn + 1],
                                        rearr(scr["xnT"])[:, :, s_lo:(i + 1) * TTn]), d0)
            ld_x = P.dma("sync", dma16(xb[:, r, :, :], rearr(inp["xT"])[:, :, tt]), d0)
            ld_s0 = P.dma("sync", dma16(s0[:, r, :, :], rearr(scr["s0T"])[:, :, tt]), d0)

            def f_mix(v, r=r, i=i):
                if i == 0:
                    v.memset(xn[:, r, :, 0:1], 0.0)
                bx = xn[:, r, :, :]
                v.tensor_tensor(db[:, :, :], bx[:, :, 1:TTn + 1], bx[:, :, 0:TTn], OP.subtract)
                for c in range(NCK):
                    v.scalar_tensor_tensor(xr0[:, c, :], db[:, c, :], murb[:, c:c + 1],
                                           bx[:, c, 0:TTn], OP.mult, OP.add)
                last = None
                for c in range(NCK):
                    last = v.scalar_tensor_tensor(xk0[:, c, :], db[:, c, :], mukb[:, c:c + 1],
                                                  bx[:, c, 0:TTn], OP.mult, OP.add)
                return last
            d = [ld_xn, wld[3], wld[4]]
            if use_mid is not None:
                d.append(use_mid)
            v_mix = P.step("vector", f_mix, d)

            def ev_r0(e, oc):
                return e.activation(r0[:, oc, :], psp[oc % 2][:, 0:TTn], AF.Sigmoid)
            evs_r = _proj_steps(P, WR, wld[0], lambda c: xr0[:, c, :], psp, use_psp,
                                ("scalar", ev_r0), [], [v_mix], TTn)

            def ev_k0(e, oc):
                return e.activation(k0[:, oc, :], psp[oc % 2][:, 0:TTn], AF.Relu)
            evs_k = _proj_steps(P, WK, wld[1], lambda c: xk0[:, c, :], psp, use_psp,
                                ("scalar", ev_k0), [], [v_mix], TTn)

            v_k0q = P.step("vector", lambda v: v.tensor_tensor(k0q[:, :, :], k0[:, :, :], k0[:, :, :], OP.mult),
                           evs_k + evs_r)

            def ev_ffn(e, oc):
                return e.tensor_tensor(ffn[:, oc, :], psp[oc % 2][:, 0:TTn], r0[:, oc, :], OP.mult)
            evs_v = _proj_steps(P, WV, wld[2], lambda c: k0q[:, c, :], psp, use_psp,
                                ("vector", ev_ffn), evs_r, [v_k0q], TTn)

            s_g0 = P.step("scalar", lambda s: s.activation(fft[:, :, :], ffn[:, :, :], AF.Sigmoid),
                          evs_v)

            def f_out(v, r=r):
                v.tensor_tensor(fft[:, :, :], fft[:, :, :], ffn[:, :, :], OP.mult)
                v.tensor_tensor(fft[:, :, :], fft[:, :, :], s0[:, r, :, :], OP.mult)
                return v.tensor_tensor(xn[:, r, :, 0:TTn], xb[:, r, :, :], fft[:, :, :], OP.add)
            v_out = P.step("vector", f_out, [s_g0, ld_x, ld_s0])
            use_mid = v_out

            st = P.dma("gpsimd", dma16(rearr(scr["x1T"])[:, :, tt], xn[:, r, :, 0:TTn]), [v_out])
            ring[r] = st
        P.play()


def phase_p4(nc, B):
    TTn = 256
    NT = T // TTn
    inp, scr = B.inp, B.scr
    with contextlib.ExitStack() as _es:
        xb = _es.enter_context(nc.sbuf_tensor("p4_x", [128, 2, NCK, TTn], F32))
        sq = _es.enter_context(nc.sbuf_tensor("p4_sq", [128, NCK, TTn], F32))
        xn1 = _es.enter_context(nc.sbuf_tensor("p4_xn1", [128, 2, NCK, TTn], F32))
        xn2 = _es.enter_context(nc.sbuf_tensor("p4_xn2", [128, 2, NCK, TTn], F32))
        stage = _es.enter_context(nc.sbuf_tensor("p4_stage", [128, 2, NCK, TTn], F32))
        stgh = _es.enter_context(nc.sbuf_tensor("p4_stgh", [128, 2, NCK, TTn], BF16))
        stgl = _es.enter_context(nc.sbuf_tensor("p4_stgl", [128, 2, NCK, TTn], BF16))
        vstage = _es.enter_context(nc.sbuf_tensor("p4_vstage", [128, 2, 2 * D], F32))
        rows = _es.enter_context(nc.sbuf_tensor("p4_rows", [1, 6 * TTn], F32))
        onescol = _es.enter_context(nc.sbuf_tensor("p4_ones", [128, 1], F32))
        onesrow = _es.enter_context(nc.sbuf_tensor("p4_ones1", [1, 128], F32))
        WQ = _es.enter_context(nc.sbuf_tensor("p4_wq", [128, NCK, D], F32))
        WK = _es.enter_context(nc.sbuf_tensor("p4_wk", [128, NCK, D], F32))
        WV = _es.enter_context(nc.sbuf_tensor("p4_wv", [128, NCK, D], F32))
        nwb = _es.enter_context(nc.sbuf_tensor("p4_nw", [128, NCK], F32))
        bqb = _es.enter_context(nc.sbuf_tensor("p4_bq", [128, NCK], F32))
        bkb = _es.enter_context(nc.sbuf_tensor("p4_bk", [128, NCK], F32))
        bvb = _es.enter_context(nc.sbuf_tensor("p4_bv", [128, NCK], F32))
        ps_s = _es.enter_context(nc.psum_tensor("p4_ps_s", [1, 512], F32))
        ps_q = _es.enter_context(nc.psum_tensor("p4_ps_q", [1, 512], F32))
        ps_mu = _es.enter_context(nc.psum_tensor("p4_ps_mu", [128, 512], F32))
        ps_rs = _es.enter_context(nc.psum_tensor("p4_ps_rs", [128, 512], F32))
        psp0 = _es.enter_context(nc.psum_tensor("p4_psp0", [128, 512], F32))
        psp1 = _es.enter_context(nc.psum_tensor("p4_psp1", [128, 512], F32))
        psp = [psp0, psp1]
        P = Prog(nc, "p4")

        def R(i):
            return rows[0:1, i * TTn:(i + 1) * TTn]

        wld = [P.dma("sync", dma16(WQ[:, :, :], inp["wqT1"].ap().rearrange("(c p) o -> p c o", p=128))),
               P.dma("sync", dma16(WK[:, :, :], inp["wkT1"].ap().rearrange("(c p) o -> p c o", p=128))),
               P.dma("sync", dma16(WV[:, :, :], inp["wvT1"].ap().rearrange("(c p) o -> p c o", p=128))),
               P.dma("sync", dma16(onescol[:, :], inp["ones128"].ap())),
               P.dma("sync", dma16(onesrow[:, :], inp["ones1x128"].ap())),
               P.dma("sync", dma16(nwb[:, :], inp["normw1"].ap())),
               P.dma("sync", dma16(bqb[:, :], inp["bq1"].ap())),
               P.dma("sync", dma16(bkb[:, :], inp["bk1"].ap())),
               P.dma("sync", dma16(bvb[:, :], inp["bv1"].ap()))]
        ring = [None, None]
        use_psp = [None, None]
        use_sq = None
        use_stats = None
        use_bc = None
        for i in range(NT):
            r = i % 2
            tt = slice(i * TTn, (i + 1) * TTn)
            d0 = [ring[r]] if ring[r] is not None else []
            ldx = P.dma("sync", dma16(xb[:, r, :, :], rearr(scr["x1T"])[:, :, tt]), d0)

            def norm_pass(src_ap, dst, eps, vr_extra, tag, i=i, r=r):
                sqst = P.step("scalar", lambda s: s.activation(sq[:, :, :], src_ap, AF.Square),
                              [ldx] + ([use_sq[tag]] if use_sq and use_sq.get(tag) else []))
                return sqst
            sq1 = P.step("scalar", lambda s, r=r: s.activation(sq[:, :, :], xb[:, r, :, :], AF.Square),
                         [ldx] + ([use_sq] if use_sq else []))

            def f_stats1(t, r=r):
                for c in range(NCK):
                    t.matmul(ps_s[0:1, 0:TTn], onescol[:, :], xb[:, r, c, :],
                             start=(c == 0), stop=(c == NCK - 1))
                last = None
                for c in range(NCK):
                    last = t.matmul(ps_q[0:1, 0:TTn], onescol[:, :], sq[:, c, :],
                                    start=(c == 0), stop=(c == NCK - 1))
                return last
            mst1 = P.step("tensor", f_stats1, [ldx, sq1, wld[3]] + ([use_stats] if use_stats else []))

            def f_vst1(v):
                v.tensor_scalar(R(2), ps_s[0:1, 0:TTn], 1.0 / D, None, OP.mult)
                v.tensor_scalar(R(1), ps_q[0:1, 0:TTn], 1.0 / D, None, OP.mult)
                v.tensor_tensor(R(0), R(2), R(2), OP.mult)
                v.tensor_tensor(R(3), R(1), R(0), OP.subtract)
                return v.tensor_scalar(R(3), R(3), LN_EPS, None, OP.add)
            vst1 = P.step("vector", f_vst1, [mst1])
            use_stats = vst1

            def f_rs1(s):
                s.activation(R(5), R(3), AF.Ln)
                return s.activation(R(4), R(5), AF.Exp, scale=-0.5)
            srs1 = P.step("scalar", f_rs1, [vst1])

            def f_nw1(v):
                v.tensor_tensor(R(5), R(4), R(4), OP.mult)
                v.tensor_tensor(R(5), R(5), R(3), OP.mult)
                v.tensor_scalar(R(5), R(5), -0.5, 1.5, OP.mult, OP.add)
                return v.tensor_tensor(R(4), R(4), R(5), OP.mult)
            vnw1 = P.step("vector", f_nw1, [srs1])

            def f_bc1(t):
                t.matmul(ps_mu[:, 0:TTn], onesrow[:, :], R(2), start=True, stop=True)
                return t.matmul(ps_rs[:, 0:TTn], onesrow[:, :], R(4), start=True, stop=True)
            bc1 = P.step("tensor", f_bc1, [vnw1, wld[4]] + ([use_bc] if use_bc else []))

            def f_xn1(v, r=r):
                last = None
                for c in range(NCK):
                    v.tensor_tensor(xn1[:, r, c, :], xb[:, r, c, :], ps_mu[:, 0:TTn], OP.subtract)
                    last = v.tensor_tensor(xn1[:, r, c, :], xn1[:, r, c, :], ps_rs[:, 0:TTn], OP.mult)
                return last
            vxn1 = P.step("vector", f_xn1, [bc1])
            st_xn1 = P.dma("gpsimd", dma16(rearr(scr["xn1T"])[:, :, tt], xn1[:, r, :, :]), [vxn1])

            sq2 = P.step("scalar", lambda s, r=r: s.activation(sq[:, :, :], xn1[:, r, :, :], AF.Square),
                         [vxn1, mst1])
            use_sq = sq2

            def f_stats2(t):
                last = None
                for c in range(NCK):
                    last = t.matmul(ps_q[0:1, 0:TTn], onescol[:, :], sq[:, c, :],
                                    start=(c == 0), stop=(c == NCK - 1))
                return last
            mst2 = P.step("tensor", f_stats2, [sq2, vst1])

            def f_vst2(v):
                v.tensor_scalar(R(3), ps_q[0:1, 0:TTn], 1.0 / D, RMS_EPS, OP.mult, OP.add)
                return v.tensor_scalar(R(3), R(3), 1.0, None, OP.mult)
            vst2 = P.step("vector", f_vst2, [mst2])

            def f_rs2(s):
                s.activation(R(5), R(3), AF.Ln)
                return s.activation(R(4), R(5), AF.Exp, scale=-0.5)
            srs2 = P.step("scalar", f_rs2, [vst2])

            def f_nw2(v):
                v.tensor_tensor(R(5), R(4), R(4), OP.mult)
                v.tensor_tensor(R(5), R(5), R(3), OP.mult)
                v.tensor_scalar(R(5), R(5), -0.5, 1.5, OP.mult, OP.add)
                return v.tensor_tensor(R(4), R(4), R(5), OP.mult)
            vnw2 = P.step("vector", f_nw2, [srs2])

            def f_bc2(t):
                return t.matmul(ps_rs[:, 0:TTn], onesrow[:, :], R(4), start=True, stop=True)
            bc2 = P.step("tensor", f_bc2, [vnw2, vxn1])
            use_bc = None

            def f_xn2(v, r=r):
                last = None
                for c in range(NCK):
                    v.tensor_tensor(xn2[:, r, c, :], xn1[:, r, c, :], ps_rs[:, 0:TTn], OP.mult)
                    last = v.tensor_scalar(xn2[:, r, c, :], xn2[:, r, c, :], nwb[:, c:c + 1],
                                           None, OP.mult)
                return last
            vxn2 = P.step("vector", f_xn2, [bc2, wld[5]])
            use_bc = vxn2
            st_xn2 = P.dma("gpsimd", dma16(rearr(scr["xn2T"])[:, :, tt], xn2[:, r, :, :]), [vxn2])

            def ev_q(e, oc, r=r):
                return e.tensor_scalar(stage[:, r, oc, :], psp[oc % 2][:, 0:TTn],
                                       bqb[:, oc:oc + 1], None, OP.add)
            evq = _proj_steps(P, WQ, wld[0], lambda c, r=r: xn2[:, r, c, :], psp, use_psp,
                              ("vector", ev_q), [wld[6]] + ([ring[r]] if ring[r] is not None else []),
                              [vxn2], TTn)

            def f_hl(v, r=r):
                v.tensor_copy(stgh[:, r, :, :], stage[:, r, :, :])
                return v.tensor_tensor(stgl[:, r, :, :], stage[:, r, :, :], stgh[:, r, :, :],
                                       OP.subtract)
            hl_q = P.step("vector", f_hl, evq + ([ring[r]] if ring[r] is not None else []))
            st_qh = P.dma("gpsimd", dma16(rearr(scr["q1hT"])[:, :, tt], stgh[:, r, :, :]), [hl_q])
            st_ql = P.dma("gpsimd", dma16(rearr(scr["q1lT"])[:, :, tt], stgl[:, r, :, :]), [hl_q])

            def ev_k(e, oc, r=r):
                return e.tensor_scalar(stage[:, r, oc, :], psp[oc % 2][:, 0:TTn],
                                       bkb[:, oc:oc + 1], None, OP.add)
            evk = _proj_steps(P, WK, wld[1], lambda c, r=r: xn2[:, r, c, :], psp, use_psp,
                              ("vector", ev_k), [wld[7], hl_q], [vxn2], TTn)
            hl_k = P.step("vector", f_hl, evk + [st_qh, st_ql])
            st_kh = P.dma("gpsimd", dma16(rearr(scr["k1hT"])[:, :, tt], stgh[:, r, :, :]), [hl_k])
            st_k = P.dma("gpsimd", dma16(rearr(scr["k1lT"])[:, :, tt], stgl[:, r, :, :]), [hl_k])

            evs_v = []
            vsf = vstage[:, r, :]
            for ts_ in range(TTn // 128):
                for half in range(2):
                    def f_mmv(t, ts_=ts_, half=half, r=r):
                        last = None
                        for c in range(NCK):
                            last = t.matmul(psp[half][:, :],
                                            xn2[:, r, c, ts_ * 128:(ts_ + 1) * 128],
                                            WV[:, c, half * 512:(half + 1) * 512],
                                            start=(c == 0), stop=(c == NCK - 1))
                        return last
                    d = [vxn2, wld[2]]
                    if use_psp[half] is not None:
                        d.append(use_psp[half])
                    mv = P.step("tensor", f_mmv, d)

                    def f_evv(v, ts_=ts_, half=half, vsf=vsf):
                        base = ts_ * 1024 + half * 512
                        return v.tensor_copy(vsf[:, base:base + 512], psp[half][:, :])
                    d = [mv, wld[8]]
                    if ring[r] is not None:
                        d.append(ring[r])
                    ev = P.step("vector", f_evv, d)
                    use_psp[half] = ev
                    evs_v.append(ev)
            st_v = P.dma("gpsimd", dma16(
                scr["v1tok"].ap().rearrange("(n p) d -> p n d", p=128)[:, 2 * i:2 * i + 2, :],
                vsf.rearrange("p (n d) -> p n d", d=1024)), evs_v)
            ring[r] = st_v
        P.play()


P5_NH = 16
P5_NI = 8


def phase_p5(nc, B):
    inp, scr = B.inp, B.scr
    NI = P5_NI
    with contextlib.ExitStack() as _es:
        qh = _es.enter_context(nc.sbuf_tensor("p5_qh", [64, T], BF16))
        qs2 = _es.enter_context(nc.sbuf_tensor("p5_qs2", [128, T], BF16))
        ks2 = _es.enter_context(nc.sbuf_tensor("p5_ks2", [128, T], BF16))
        kl = _es.enter_context(nc.sbuf_tensor("p5_kl", [64, T], BF16))
        vb = _es.enter_context(nc.sbuf_tensor("p5_v", [128, 32, 66], F32))
        pb = _es.enter_context(nc.sbuf_tensor("p5_p", [128, 4, 512], F32))
        zz = _es.enter_context(nc.sbuf_tensor("p5_z", [128, 128], F32))
        shb = _es.enter_context(nc.sbuf_tensor("p5_shift", [128, 1], F32))
        dmk = _es.enter_context(nc.sbuf_tensor("p5_dmk", [128, 128], U8))
        dmki = _es.enter_context(nc.sbuf_tensor("p5_dmki", [128, 128], U8))
        yst = _es.enter_context(nc.sbuf_tensor("p5_yst", [66, 3, 512], F32))
        lrow = _es.enter_context(nc.sbuf_tensor("p5_l", [16, T], F32))
        psS = [_es.enter_context(nc.psum_tensor(f"p5_psS{j}", [128, 512], F32)) for j in range(4)]
        psYl = [_es.enter_context(nc.psum_tensor(f"p5_psY{j}", [128, 512], F32)) for j in range(3)]
        P = Prog(nc, "p5")
        c_dm0 = P.dma("sync", dma16(dmk[:, :], inp["dmask"].ap()))
        c_dm = P.dma("sync", dma16(dmki[:, :], inp["dmaski"].ap()))
        c_z = P.step("vector", lambda v: (v.memset(zz[:, :], 0.0),
                                          v.memset(shb[:, :], -EXP_SHIFT))[1])
        prev_head = None
        for h in range(P5_NH):
            hrow = slice(64 * (h % 2), 64 * (h % 2) + 64) if False else None
            pdeps = [prev_head] if prev_head is not None else []
            hrows = slice(64 * h, 64 * h + 64)
            ld_q1 = P.dma("sync", dma16(qh[:, :], scr["q1hT"].ap()[hrows, :]), pdeps)
            ld_q2 = P.dma("sync", dma16(qs2[0:64, :], scr["q1lT"].ap()[hrows, :]), pdeps)
            ld_q3 = P.dma("sync", dma16(qs2[64:128, :], scr["q1hT"].ap()[hrows, :]), pdeps)
            ld_k1 = P.dma("sync", dma16(ks2[0:64, :], scr["k1hT"].ap()[hrows, :]), pdeps)
            ld_k2 = P.dma("sync", dma16(ks2[64:128, :], scr["k1lT"].ap()[hrows, :]), pdeps)
            ld_k3 = P.dma("sync", dma16(kl[:, :], scr["k1lT"].ap()[hrows, :]), pdeps)
            ld_q = ld_q3
            ld_k = ld_k3
            ld_v = P.dma("sync", dma16(
                vb[:, :, 0:64],
                scr["v1tok"].ap().rearrange("(n p) d -> p n d", p=128)[:, :, 64 * h:64 * h + 64]), pdeps)
            vload = P.step("vector", lambda v: (v.memset(vb[:, :, 64:65], 1.0),
                                                v.memset(vb[:, :, 65:66], 0.0))[1], [ld_v] + pdeps)
            head_tail = []
            use_ps = [None, None, None, None]
            use_pb = [None, None, None, None]
            use_py = [None, None, None]
            LA = 4
            seq = [(ib, jc) for ib in range(NI) for jc in range(4 * ib + 4)]
            pready_of = {}
            pv_of = {}
            last_pv_of_ib = {}
            for ib in range(NI):
                last_pv_of_ib[ib] = 4 * ib + 3

            def rec_score(idx):
                ib, jc = seq[idx]
                i0 = ib * 512
                d_off = max(0, jc * 128 - i0)
                jj = idx % 4

                def f_sc(t, jc=jc, d_off=d_off, jj=jj, i0=i0):
                    t.matmul(psS[jj][:, d_off:512], ks2[0:64, jc * 128:(jc + 1) * 128],
                             qh[:, i0 + d_off:i0 + 512], start=True, stop=False)
                    t.matmul(psS[jj][:, d_off:512], kl[:, jc * 128:(jc + 1) * 128],
                             qs2[0:64, i0 + d_off:i0 + 512], start=False, stop=False)
                    return t.matmul(psS[jj][:, d_off:512], ks2[:, jc * 128:(jc + 1) * 128],
                                    qs2[:, i0 + d_off:i0 + 512], start=False, stop=True)
                d = [ld_q1, ld_q2, ld_q, ld_k1, ld_k2, ld_k]
                if use_ps[jj] is not None:
                    d.append(use_ps[jj])
                msc = P.step("tensor", f_sc, d)

                def f_exp(s, d_off=d_off, jj=jj):
                    return s.activation(pb[:, jj, d_off:512], psS[jj][:, d_off:512],
                                        AF.Exp, scale=0.125, bias=shb[:, 0:1])
                d = [msc, c_z]
                if use_pb[jj] is not None:
                    d.append(use_pb[jj])
                sexp = P.step("scalar", f_exp, d)
                use_ps[jj] = sexp
                pready = sexp
                if d_off > 0 or jc * 128 == i0:
                    dd = d_off

                    def f_msk(v, jj=jj, dd=dd):
                        return v.copy_predicated(pb[:, jj, dd:dd + 128], dmki[:, :], zz[:, :])
                    pready = P.step("vector", f_msk, [sexp, c_z, c_dm])
                pready_of[idx] = (sexp, pready)

            def rec_pv(idx):
                ib, jc = seq[idx]
                i0 = ib * 512
                d_off = max(0, jc * 128 - i0)
                jj = idx % 4
                njc = 4 * ib + 4
                psY = psYl[ib % 3]
                diag = (d_off > 0)
                exp_done, mask_done = pready_of[idx]
                if not diag:
                    def f_pv(t, jc=jc, d_off=d_off, jj=jj, first=(jc == 0), last=(jc == njc - 1), psY=psY):
                        return t.matmul(psY[0:66, d_off:512], vb[:, jc, :], pb[:, jj, d_off:512],
                                        start=first, stop=last)
                    d = [mask_done, vload]
                    if use_py[ib % 3] is not None:
                        d.append(use_py[ib % 3])
                    mpv = P.step("tensor", f_pv, d)
                else:
                    def f_pv_m(t, jc=jc, d_off=d_off, jj=jj, last=(jc == njc - 1), psY=psY):
                        return t.matmul(psY[0:66, d_off + 128:512], vb[:, jc, :],
                                        pb[:, jj, d_off + 128:512], start=False, stop=last)
                    def f_pv_s(t, jc=jc, d_off=d_off, jj=jj, last=(jc == njc - 1), psY=psY):
                        return t.matmul(psY[0:66, d_off:d_off + 128], vb[:, jc, :],
                                        pb[:, jj, d_off:d_off + 128], start=False, stop=last)
                    d = [exp_done, vload]
                    if use_py[ib % 3] is not None:
                        d.append(use_py[ib % 3])
                    if d_off + 128 < 512:
                        P.step("tensor", f_pv_m, d)
                    mpv = P.step("tensor", f_pv_s, [mask_done, vload] +
                                 ([use_py[ib % 3]] if use_py[ib % 3] is not None else []))
                use_pb[jj] = mpv
                pv_of[idx] = mpv
                if jc == njc - 1:
                    def f_ev(s, ib=ib, psY=psY):
                        s.copy(yst[0:64, ib % 3, :], psY[0:64, 0:512])
                        return s.copy(yst[64:65, ib % 3, :], psY[64:65, 0:512])
                    d = [mpv]
                    if len(head_tail) >= 3:
                        d.append(head_tail[-3])
                    evy = P.step("scalar", f_ev, d)
                    use_py[ib % 3] = evy
                    st_y = P.dma("gpsimd", dma16(scr["attraw"].ap()[64 * h:64 * h + 64, i0:i0 + 512],
                                                 yst[0:64, ib % 3, :]), [evy])
                    st_l = P.dma("gpsimd", dma16(scr["l1s"].ap()[h:h + 1, i0:i0 + 512],
                                                 yst[64:65, ib % 3, :]), [evy])
                    head_tail.append(st_l)

            for idx in range(len(seq) + LA):
                if idx >= LA:
                    rec_pv(idx - LA)
                if idx < len(seq):
                    rec_score(idx)
            prev_head = head_tail[-1]

        if P5_NH < 16:
            P.play()
            return
        ld_l = P.dma("sync", dma16(lrow[:, :], scr["l1s"].ap()), [prev_head])
        e_ln = P.step("scalar", lambda s: (s.activation(lrow[:, :], lrow[:, :], AF.Ln),
                                           s.activation(lrow[:, :], lrow[:, :], AF.Exp, scale=-1.0))[1],
                      [ld_l])
        st_li = P.dma("gpsimd", dma16(scr["linv1"].ap(), lrow[:, :]), [e_ln])
        P.play()


def phase_p6(nc, B):
    TTn = 256
    NT = T // TTn
    inp, scr = B.inp, B.scr
    outT = B.out
    with contextlib.ExitStack() as _es:
        ar = _es.enter_context(nc.sbuf_tensor("p6_ar", [128, 2, NCK, TTn], F32))
        li = _es.enter_context(nc.sbuf_tensor("p6_li", [16, 2, TTn], F32))
        att = _es.enter_context(nc.sbuf_tensor("p6_att", [128, NCK, TTn], F32))
        xn1 = _es.enter_context(nc.sbuf_tensor("p6_xn1", [128, 2, NCK, TTn], F32))
        xn2 = _es.enter_context(nc.sbuf_tensor("p6_xn2", [128, 2, NCK, TTn], F32))
        x1b = _es.enter_context(nc.sbuf_tensor("p6_x1", [128, 2, NCK, TTn], F32))
        s1 = _es.enter_context(nc.sbuf_tensor("p6_s1", [128, NCK, TTn], F32))
        sm = _es.enter_context(nc.sbuf_tensor("p6_sm", [64, TTn], F32))
        hb = _es.enter_context(nc.sbuf_tensor("p6_h", [128, NCK, TTn], F32))
        gsm = _es.enter_context(nc.sbuf_tensor("p6_gsm", [64, TTn], F32))
        of = _es.enter_context(nc.sbuf_tensor("p6_of", [128, NCK, TTn], F32))
        oft = _es.enter_context(nc.sbuf_tensor("p6_oft", [128, NCK, TTn], F32))
        outs = _es.enter_context(nc.sbuf_tensor("p6_out", [128, 2, NCK, TTn], F32))
        WO = _es.enter_context(nc.sbuf_tensor("p6_wo", [128, NCK, D], F32))
        WIN = _es.enter_context(nc.sbuf_tensor("p6_win", [128, NCK, 64], F32))
        WOUT = _es.enter_context(nc.sbuf_tensor("p6_wout", [64, D], F32))
        W1E = _es.enter_context(nc.sbuf_tensor("p6_w1e", [64, NCK * 128], F32))
        W2R = _es.enter_context(nc.sbuf_tensor("p6_w2r", [128, NCK * 64], F32))
        LEX = _es.enter_context(nc.sbuf_tensor("p6_lex", [16, NCK * 128], F32))
        bob = _es.enter_context(nc.sbuf_tensor("p6_bo", [128, NCK], F32))
        binb = _es.enter_context(nc.sbuf_tensor("p6_bin", [64, 1], F32))
        b1b = _es.enter_context(nc.sbuf_tensor("p6_b1", [128, NCK], F32))
        b2b = _es.enter_context(nc.sbuf_tensor("p6_b2", [64, 1], F32))
        boutb = _es.enter_context(nc.sbuf_tensor("p6_bout", [128, NCK], F32))
        psp0 = _es.enter_context(nc.psum_tensor("p6_psp0", [128, 512], F32))
        psp1 = _es.enter_context(nc.psum_tensor("p6_psp1", [128, 512], F32))
        psE = _es.enter_context(nc.psum_tensor("p6_psE", [128, 512], F32))
        psF = _es.enter_context(nc.psum_tensor("p6_psF", [128, 512], F32))
        psp = [psp0, psp1]
        P = Prog(nc, "p6")
        wld = [P.dma("sync", dma16(WO[:, :, :], inp["woT1"].ap().rearrange("(c p) o -> p c o", p=128))),
               P.dma("sync", dma16(WIN[:, :, :], inp["win1"].ap().rearrange("(c p) o -> p c o", p=128))),
               P.dma("sync", dma16(WOUT[:, :], inp["wout1"].ap())),
               P.dma("sync", dma16(W1E[:, :], inp["w1exp"].ap())),
               P.dma("sync", dma16(W2R[:, :], inp["w2red"].ap())),
               P.dma("sync", dma16(LEX[:, :], inp["lexp"].ap())),
               P.dma("sync", dma16(bob[:, :], inp["bo1"].ap())),
               P.dma("sync", dma16(binb[:, :], inp["bin1"].ap())),
               P.dma("sync", dma16(b1b[:, :], inp["b1f"].ap())),
               P.dma("sync", dma16(b2b[:, :], inp["b2c"].ap())),
               P.dma("sync", dma16(boutb[:, :], inp["bout1"].ap()))]
        ring = [None, None]
        use_psp = [None, None]
        use_mid = None
        use_psE = None
        use_psF = None
        for i in range(NT):
            r = i % 2
            tt = slice(i * TTn, (i + 1) * TTn)
            d0 = [ring[r]] if ring[r] is not None else []
            ld_ar = P.dma("sync", dma16(ar[:, r, :, :], rearr(scr["attraw"])[:, :, tt]), d0)
            ld_li = P.dma("sync", dma16(li[:, r, :], scr["linv1"].ap()[:, tt]), d0)
            ld_xn1 = P.dma("sync", dma16(xn1[:, r, :, :], rearr(scr["xn1T"])[:, :, tt]), d0)
            ld_xn2 = P.dma("sync", dma16(xn2[:, r, :, :], rearr(scr["xn2T"])[:, :, tt]), d0)
            ld_x1 = P.dma("sync", dma16(x1b[:, r, :, :], rearr(scr["x1T"])[:, :, tt]), d0)

            att_steps = []
            for c in range(NCK):
                def f_le(t, c=c, r=r):
                    return t.matmul(psE[:, 0:TTn], LEX[:, c * 128:(c + 1) * 128], li[0:16, r, :],
                                    start=True, stop=True)
                d = [ld_li, wld[5]]
                if use_psE is not None:
                    d.append(use_psE)
                mle = P.step("tensor", f_le, d)

                def f_att(v, c=c, r=r):
                    return v.tensor_tensor(att[:, c, :], ar[:, r, c, :], psE[:, 0:TTn], OP.mult)
                d = [mle, ld_ar]
                if use_mid is not None:
                    d.append(use_mid)
                vat = P.step("vector", f_att, d)
                use_psE = vat
                att_steps.append(vat)

            def ev_s1(e, oc, r=r):
                e.tensor_scalar(s1[:, oc, :], psp[oc % 2][:, 0:TTn], bob[:, oc:oc + 1], None, OP.add)
                return e.tensor_tensor(s1[:, oc, :], s1[:, oc, :], xn2[:, r, oc, :], OP.add)
            evs1 = _proj_steps(P, WO, wld[0], lambda c: att[:, c, :], psp, use_psp,
                               ("vector", ev_s1), [wld[6], ld_xn2], att_steps, TTn)

            def f_win(t, r=r):
                last = None
                for c in range(NCK):
                    last = t.matmul(psF[0:64, 0:TTn], WIN[:, c, :], xn1[:, r, c, :],
                                    start=(c == 0), stop=(c == NCK - 1))
                return last
            d = [ld_xn1, wld[1]]
            if use_psF is not None:
                d.append(use_psF)
            mwin = P.step("tensor", f_win, d)
            ev_sm = P.step("vector", lambda v: v.tensor_scalar(sm[:, :], psF[0:64, 0:TTn],
                                                               binb[:, 0:1], None, OP.add),
                           [mwin, wld[7]])
            use_psF = ev_sm

            h_steps = []
            for c in range(NCK):
                def f_ex(t, c=c):
                    return t.matmul(psE[:, 0:TTn], W1E[:, c * 128:(c + 1) * 128], sm[:, :],
                                    start=True, stop=True)
                d = [ev_sm, wld[3]]
                if use_psE is not None:
                    d.append(use_psE)
                mex = P.step("tensor", f_ex, d)
                s_h = P.step("scalar", lambda s, c=c: s.activation(hb[:, c, :], psE[:, 0:TTn],
                                                                   AF.Relu, bias=b1b[:, c:c + 1]),
                             [mex, wld[8]])
                use_psE = s_h
                h_steps.append(s_h)

            def f_red(t):
                last = None
                for c in range(NCK):
                    last = t.matmul(psF[0:64, 0:TTn], W2R[:, c * 64:(c + 1) * 64], hb[:, c, :],
                                    start=(c == 0), stop=(c == NCK - 1))
                return last
            mred = P.step("tensor", f_red, h_steps + [wld[4], ev_sm])
            ev_gsm = P.step("vector", lambda v: v.tensor_scalar(gsm[:, :], psF[0:64, 0:TTn],
                                                                b2b[:, 0:1], None, OP.add),
                            [mred, wld[9]])
            use_psF = ev_gsm

            def ev_of(e, oc):
                return e.tensor_scalar(of[:, oc, :], psp[oc % 2][:, 0:TTn], boutb[:, oc:oc + 1],
                                       None, OP.add)
            def wout_mm():
                evs = []
                for oc in range(NCK):
                    def f_mm(t, oc=oc):
                        return t.matmul(psp[oc % 2][:, 0:TTn], WOUT[:, oc * 128:(oc + 1) * 128],
                                        gsm[:, :], start=True, stop=True)
                    d = [ev_gsm, wld[2]]
                    if use_psp[oc % 2] is not None:
                        d.append(use_psp[oc % 2])
                    mp = P.step("tensor", f_mm, d)
                    ev = P.step("vector", lambda e, oc=oc: ev_of(e, oc), [mp, wld[10]])
                    use_psp[oc % 2] = ev
                    evs.append(ev)
                return evs
            evs_of = wout_mm()

            s_sig = P.step("scalar", lambda s: s.activation(oft[:, :, :], of[:, :, :], AF.Sigmoid),
                           evs_of)

            def f_out(v, r=r):
                v.tensor_tensor(oft[:, :, :], oft[:, :, :], of[:, :, :], OP.mult)
                v.tensor_tensor(oft[:, :, :], oft[:, :, :], s1[:, :, :], OP.mult)
                return v.tensor_tensor(outs[:, r, :, :], x1b[:, r, :, :], oft[:, :, :], OP.add)
            v_out = P.step("vector", f_out, [s_sig, ld_x1] + evs1)
            use_mid = v_out

            st = P.dma("gpsimd", dma16(rearr(outT)[:, :, tt], outs[:, r, :, :]), [v_out])
            ring[r] = st
        P.play()


def build(debug=(), phases=(1, 2, 3, 4, 5, 6, 7), preload=()):
    nc = bass.Bass("TRN2", target_bir_lowering=False, detect_race_conditions=False)
    B = Builder(nc, debug, preload)

    B.dram_in("xT", [D, T])
    B.dram_in("wrT", [D, D]); B.dram_in("wkT", [D, D]); B.dram_in("wvT", [D, D])
    B.dram_in("wgT", [D, D]); B.dram_in("woT", [D, D])
    B.dram_in("maa_w1", [D, 160])
    B.dram_in("w2h", [32, 5 * D])
    B.dram_in("c5col", [128, 5 * NCK])
    B.dram_in("tmx", [128, NCK]); B.dram_in("tdec", [128, NCK])
    B.dram_in("dw1", [D, 64]); B.dram_in("dw2", [64, D])
    B.dram_in("lnw0", [128, NCK]); B.dram_in("lnb0", [128, NCK])
    B.dram_in("wrfT", [D, D]); B.dram_in("wkfT", [D, D]); B.dram_in("wvfT", [D, D])
    B.dram_in("mur", [128, NCK]); B.dram_in("muk", [128, NCK])
    B.dram_in("normw1", [128, NCK])
    B.dram_in("wqT1", [D, D]); B.dram_in("wkT1", [D, D]); B.dram_in("wvT1", [D, D]); B.dram_in("woT1", [D, D])
    B.dram_in("bq1", [128, NCK]); B.dram_in("bk1", [128, NCK]); B.dram_in("bv1", [128, NCK]); B.dram_in("bo1", [128, NCK])
    B.dram_in("win1", [D, 64]); B.dram_in("bin1", [64, 1])
    B.dram_in("w1exp", [64, NCK * 128]); B.dram_in("b1f", [128, NCK])
    B.dram_in("w2red", [128, NCK * 64]); B.dram_in("b2c", [64, 1])
    B.dram_in("wout1", [64, D]); B.dram_in("bout1", [128, NCK])
    B.dram_in("ones128", [128, 1]); B.dram_in("ones1x128", [1, 128])
    B.dram_in("gn2", [128, 2]); B.dram_in("gnexp", [2, 128]); B.dram_in("ident", [128, 128])
    B.dram_in("lexp", [16, NCK * 128])
    B.dram_in("trimask2", [128, 256], U8)
    B.dram_in("dmask", [128, 128], U8); B.dram_in("dmaski", [128, 128], U8)

    for nm, shp in [
        ("xnT", [D, T]), ("xxxT", [32, 5 * T]), ("rT", [D, T]), ("kT", [D, T]),
        ("gpreT", [D, T]), ("ewT", [D, T]), ("csT", [D, T]), ("vtok", [T, D]),
        ("ynT", [D, T]), ("x1T", [D, T]), ("xn1T", [D, T]), ("xn2T", [D, T]),
        ("v1tok", [T, D]), ("attraw", [D, T]),
        ("l1s", [16, T]), ("linv1", [16, T]), ("gn_a", [2, T]), ("gn_b", [2, T]),
        ("gT", [D, T]), ("s0T", [D, T]),
    ]:
        B.scratch(nm, shp)

    for nm in ["q1hT", "q1lT", "k1hT", "k1lT"]:
        B.scratch(nm, [D, T], BF16)
    outT = nc.dram_tensor("outT", [D, T], F32, kind="ExternalOutput")
    B.out = outT

    if 1 in phases:
        phase_p1a(nc, B)
    if 2 in phases:
        phase_p1b(nc, B, 1)
        phase_p1b(nc, B, 2)
    if 3 in phases:
        phase_p2(nc, B)
    if 4 in phases:
        phase_p3a(nc, B)
        phase_p3b(nc, B)
    if 5 in phases:
        phase_p4(nc, B)
    if 6 in phases:
        phase_p5(nc, B)
    if 7 in phases:
        phase_p6(nc, B)
    return nc


def _host_consts(inputs):
    g = inputs
    d = {}
    d["wrT"] = g["L0_attn_W_r"].T
    d["wkT"] = g["L0_attn_W_k"].T
    d["wvT"] = g["L0_attn_W_v"].T
    d["wgT"] = g["L0_attn_W_g"].T
    d["woT"] = g["L0_attn_W_o"].T
    d["maa_w1"] = g["L0_attn_time_maa_w1"]
    w2 = g["L0_attn_time_maa_w2"]
    d["w2h"] = np.concatenate([w2[f] for f in range(5)], axis=1)
    c5 = np.stack([g["L0_attn_time_maa_w"].reshape(D), g["L0_attn_time_maa_k"].reshape(D),
                   g["L0_attn_time_maa_v"].reshape(D), g["L0_attn_time_maa_r"].reshape(D),
                   g["L0_attn_time_maa_g"].reshape(D)])
    col = lambda v: np.ascontiguousarray(np.asarray(v).reshape(NCK, 128).T)
    d["c5col"] = np.concatenate([col(c5[f]) for f in range(5)], axis=1)
    d["tmx"] = col(g["L0_attn_time_maa_x"])
    d["tdec"] = col(g["L0_attn_time_decay"])
    d["dw1"] = g["L0_attn_time_decay_w1"]
    d["dw2"] = g["L0_attn_time_decay_w2"]
    d["lnw0"] = col(g["L0_attn_ln_w"])
    d["lnb0"] = col(g["L0_attn_ln_b"])
    d["wrfT"] = g["L0_ffn_W_r"].T
    d["wkfT"] = g["L0_ffn_W_k"].T
    d["wvfT"] = g["L0_ffn_W_v"].T
    d["mur"] = col(g["L0_ffn_mu_r"])
    d["muk"] = col(g["L0_ffn_mu_k"])
    d["normw1"] = col(g["L1_attn_norm_w"])
    d["wqT1"] = g["L1_attn_Wq"].T
    d["wkT1"] = g["L1_attn_Wk"].T
    d["wvT1"] = g["L1_attn_Wv"].T
    d["woT1"] = g["L1_attn_Wo"].T
    d["bq1"] = col(g["L1_attn_bq"])
    d["bk1"] = col(g["L1_attn_bk"])
    d["bv1"] = col(g["L1_attn_bv"])
    d["bo1"] = col(g["L1_attn_bo"])
    d["win1"] = g["L1_ffn_Win"]
    d["bin1"] = np.asarray(g["L1_ffn_bin_"]).reshape(64, 1)
    W1 = np.asarray(g["L1_ffn_W1"])
    w1exp = np.zeros((64, NCK * 128), dtype=np.float32)
    for c in range(NCK):
        for p in range(128):
            m = 8 * c + p // 16
            w1exp[m, c * 128 + p] = W1[m, p % 16]
    d["w1exp"] = w1exp
    d["b1f"] = col(np.asarray(g["L1_ffn_b1"]).reshape(D))
    W2 = np.asarray(g["L1_ffn_W2"])
    w2red = np.zeros((128, NCK * 64), dtype=np.float32)
    for c in range(NCK):
        for p in range(128):
            m = 8 * c + p // 16
            w2red[p, c * 64 + m] = W2[m, p % 16]
    d["w2red"] = w2red
    d["b2c"] = np.asarray(g["L1_ffn_b2"]).reshape(64, 1)
    d["wout1"] = g["L1_ffn_Wout"]
    d["bout1"] = col(g["L1_ffn_bout"])
    d["ones128"] = np.ones((128, 1), dtype=np.float32)
    d["ones1x128"] = np.ones((1, 128), dtype=np.float32)
    gn2 = np.zeros((128, 2), dtype=np.float32)
    gn2[0:64, 0] = 1.0
    gn2[64:128, 1] = 1.0
    d["gn2"] = gn2
    gnexp = np.zeros((2, 128), dtype=np.float32)
    gnexp[0, 0:64] = 1.0
    gnexp[1, 64:128] = 1.0
    d["gnexp"] = gnexp
    d["ident"] = np.eye(128, dtype=np.float32)
    lexp = np.zeros((16, NCK * 128), dtype=np.float32)
    for c in range(NCK):
        for p in range(128):
            lexp[2 * c + p // 64, c * 128 + p] = 1.0
    d["lexp"] = lexp
    tri = (np.arange(128)[None, :] >= np.arange(128)[:, None]).astype(np.uint8)
    d["trimask2"] = np.tile(tri, (1, 2))
    d["dmask"] = tri
    d["dmaski"] = (1 - tri).astype(np.uint8)
    out = {}
    for k, v in d.items():
        v = np.asarray(v)
        if v.dtype == np.uint8:
            out[k] = np.ascontiguousarray(v)
        else:
            out[k] = np.ascontiguousarray(v.astype(np.float32))
    return out


def kernel(**inputs):
    consts = _host_consts(inputs)
    nc = build()
    x = np.asarray(inputs["L0_attn_x"])
    in_maps = []
    for b in range(8):
        m = dict(consts)
        m["xT"] = np.ascontiguousarray(x[b].T)
        in_maps.append(m)
    res = run_bass_kernel_spmd(nc, in_maps, core_ids=list(range(8)))
    out = np.stack([res.results[b]["outT"].T for b in range(8)], axis=0)
    return np.ascontiguousarray(out.astype(np.float32))


# version 11
# speedup vs baseline: 1.0880x; 1.0163x over previous
import numpy as np
import contextlib
import concourse.bass as bass
import concourse.mybir as mybir
from concourse.bass_utils import run_bass_kernel_spmd
from concourse.alu_op_type import AluOpType as OP

F32 = mybir.dt.float32
BF16 = mybir.dt.bfloat16
U8 = mybir.dt.uint8
AF = mybir.ActivationFunctionType

T = 4096
D = 1024
NCK = 8
H = 16
HD = 64
EXP_SHIFT = 12.0

LN_EPS = 1e-5
GN_EPS = 1e-5 * 64.0
RMS_EPS = 1e-6

ENGS = ("sync", "tensor", "vector", "scalar", "gpsimd")


class Prog:

    def __init__(self, nc, name):
        self.nc = nc
        self.name = name
        self.steps = []
        self.cum = []

    def step(self, engine, fn, deps=(), inc=1):
        assert engine in ENGS
        for d in deps:
            assert 0 <= d < len(self.steps), (d, len(self.steps))
        cum_prev = 0
        for i in range(len(self.steps) - 1, -1, -1):
            if self.steps[i][0] == engine:
                cum_prev = self.cum[i]
                break
        self.steps.append([engine, fn, inc, tuple(deps)])
        self.cum.append(cum_prev + inc)
        return len(self.steps) - 1

    def dma(self, engine, fn, deps=()):
        return self.step(engine, fn, deps, inc=16)

    def play(self):
        nc = self.nc
        with nc.Block() as blk:
            sems = {}
            ctx = []
            for e in ENGS:
                if any(s[0] == e for s in self.steps):
                    cm = nc.semaphore(f"{self.name}_{e}")
                    sems[e] = cm.__enter__()
                    ctx.append(cm)
            watermark = {}

            def play_engine(eng_name, eng):
                dma_cum = 0
                for idx, (e, fn, inc, deps) in enumerate(self.steps):
                    if e != eng_name:
                        continue
                    if inc == 16:
                        if dma_cum > 0:
                            eng.wait_ge(sems[eng_name], dma_cum)
                        dma_cum = self.cum[idx]
                    for d in deps:
                        de, _, _, _ = self.steps[d]
                        need = self.cum[d]
                        key = (eng_name, de)
                        if watermark.get(key, -1) >= need:
                            continue
                        watermark[key] = need
                        eng.wait_ge(sems[de], need)
                    ret = fn(eng)
                    if isinstance(ret, int):
                        assert ret == inc, (idx, ret, inc)
                    elif ret is None:
                        raise RuntimeError(f"step {idx} returned None; must return last instruction")
                    else:
                        ret.then_inc(sems[eng_name], inc)

            for e in ENGS:
                if e not in sems:
                    continue
                if e == "sync":
                    @blk.sync
                    def _(s, e=e):
                        play_engine(e, s)
                elif e == "tensor":
                    @blk.tensor
                    def _(s, e=e):
                        play_engine(e, s)
                elif e == "vector":
                    @blk.vector
                    def _(s, e=e):
                        play_engine(e, s)
                elif e == "scalar":
                    @blk.scalar
                    def _(s, e=e):
                        play_engine(e, s)
                elif e == "gpsimd":
                    @blk.gpsimd
                    def _(s, e=e):
                        play_engine(e, s)
            for cm in ctx:
                cm.__exit__(None, None, None)


def dma16(ap_out, ap_in):
    def fn(eng):
        return eng.dma_start(ap_out, ap_in)
    return fn


def rearr(dram, p=128):
    return dram.ap().rearrange("(c p) t -> p c t", p=p)


class Builder:
    def __init__(self, nc: bass.Bass, debug=(), preload=()):
        self.nc = nc
        self.debug = set(debug)
        self.preload = set(preload)
        self.inp = {}
        self.scr = {}

    def dram_in(self, name, shape, dtype=F32):
        h = self.nc.dram_tensor(name, shape, dtype, kind="ExternalInput")
        self.inp[name] = h
        return h

    def scratch(self, name, shape, dtype=F32):
        kind = "ExternalOutput" if name in self.debug else "Internal"
        if name in self.preload:
            kind = "ExternalInput"
        h = self.nc.dram_tensor(name, shape, dtype, kind=kind)
        self.scr[name] = h
        return h


def phase_p1a(nc, B):
    TT = 512
    NT = T // TT
    inp, scr = B.inp, B.scr
    with (
        nc.sbuf_tensor("p1a_x", [128, 2, NCK, TT], F32) as xb,
        nc.sbuf_tensor("p1a_sq", [128, NCK, TT], F32) as sqb,
        nc.sbuf_tensor("p1a_xn", [128, 2, NCK, TT], F32) as xnb,
        nc.sbuf_tensor("p1a_ones", [128, 1], F32) as onescol,
        nc.sbuf_tensor("p1a_ones1", [1, 128], F32) as onesrow,
        nc.sbuf_tensor("p1a_rows", [1, 6 * TT], F32) as rows,
        nc.psum_tensor("p1a_ps_s", [1, 2 * TT], F32) as ps_s,
        nc.psum_tensor("p1a_ps_q", [1, 2 * TT], F32) as ps_q,
        nc.psum_tensor("p1a_ps_mu", [128, TT], F32) as ps_mu,
        nc.psum_tensor("p1a_ps_rs", [128, TT], F32) as ps_rs,
    ):
        P = Prog(nc, "p1a")

        def R(i):
            return rows[0:1, i * TT:(i + 1) * TT]

        c_ones = P.dma("sync", dma16(onescol[:, :], inp["ones128"].ap()))
        c_ones1 = P.dma("sync", dma16(onesrow[:, :], inp["ones1x128"].ap()))

        last_use = [None, None]
        bc_use = None
        stat_use = None
        use_sq = None
        for i in range(NT):
            r = i % 2
            deps = [last_use[r]] if last_use[r] is not None else []
            ldx = P.dma("sync", dma16(xb[:, r, :, :], rearr(inp["xT"])[:, :, i * TT:(i + 1) * TT]), deps)
            sq = P.step("scalar", lambda s, r=r: s.activation(sqb[:, :, :], xb[:, r, :, :], AF.Square),
                        [ldx] + ([use_sq] if use_sq else []))

            def mm_stats(t, r=r, i=i):
                p = (i % 2) * TT
                for c in range(NCK):
                    t.matmul(ps_s[0:1, p:p + TT], onescol[:, :], xb[:, r, c, :],
                             start=(c == 0), stop=(c == NCK - 1))
                last = None
                for c in range(NCK):
                    last = t.matmul(ps_q[0:1, p:p + TT], onescol[:, :], sqb[:, c, :],
                                    start=(c == 0), stop=(c == NCK - 1))
                return last
            mst = P.step("tensor", mm_stats, [ldx, sq, c_ones] + ([stat_use] if stat_use else []))
            use_sq = mst

            def v_stats(v, i=i):
                p = (i % 2) * TT
                v.tensor_scalar(R(2), ps_s[0:1, p:p + TT], 1.0 / D, None, OP.mult)
                v.tensor_scalar(R(1), ps_q[0:1, p:p + TT], 1.0 / D, None, OP.mult)
                v.tensor_tensor(R(0), R(2), R(2), OP.mult)
                v.tensor_tensor(R(3), R(1), R(0), OP.subtract)
                return v.tensor_scalar(R(3), R(3), LN_EPS, None, OP.add)
            vst = P.step("vector", v_stats, [mst])
            stat_use = vst

            def s_rstd(s):
                s.activation(R(5), R(3), AF.Ln)
                return s.activation(R(4), R(5), AF.Exp, scale=-0.5)
            srs = P.step("scalar", s_rstd, [vst])

            def v_newton(v):
                v.tensor_tensor(R(5), R(4), R(4), OP.mult)
                v.tensor_tensor(R(5), R(5), R(3), OP.mult)
                v.tensor_scalar(R(5), R(5), -0.5, 1.5, OP.mult, OP.add)
                return v.tensor_tensor(R(4), R(4), R(5), OP.mult)
            vnw = P.step("vector", v_newton, [srs])

            def mm_bcast(t):
                t.matmul(ps_mu[:, :], onesrow[:, :], R(2), start=True, stop=True)
                return t.matmul(ps_rs[:, :], onesrow[:, :], R(4), start=True, stop=True)
            bc = P.step("tensor", mm_bcast, [vnw, c_ones1] + ([bc_use] if bc_use else []))

            def v_xn(v, r=r):
                last = None
                for c in range(NCK):
                    v.tensor_tensor(xnb[:, r, c, :], xb[:, r, c, :], ps_mu[:, :], OP.subtract)
                    last = v.tensor_tensor(xnb[:, r, c, :], xnb[:, r, c, :], ps_rs[:, :], OP.mult)
                return last
            vxn = P.step("vector", v_xn, [bc])
            bc_use = vxn

            st = P.dma("gpsimd", dma16(rearr(scr["xnT"])[:, :, i * TT:(i + 1) * TT], xnb[:, r, :, :]), [vxn])
            last_use[r] = st
        P.play()



def phase_p1b(nc, B, part):
    TT = 256
    NT = T // TT
    inp, scr = B.inp, B.scr
    projs = ["w", "k", "r"] if part == 1 else ["g", "v"]
    f_of = {"w": 0, "k": 1, "v": 2, "r": 3, "g": 4}
    out_of = {"k": scr["kT"], "r": scr["rT"], "g": scr["gpreT"]}
    pfx = f"p1b{part}"
    with contextlib.ExitStack() as _es:
        xnb = _es.enter_context(nc.sbuf_tensor(f"{pfx}_xn", [128, 2, NCK, TT + 2], F32))
        dxb = _es.enter_context(nc.sbuf_tensor(f"{pfx}_dx", [128, NCK, TT], F32))
        mixb = _es.enter_context(nc.sbuf_tensor(f"{pfx}_mix", [128, NCK, TT] if part == 1 else [1, 1], F32))
        xxxb = _es.enter_context(nc.sbuf_tensor(f"{pfx}_xxx", [32, 2, 5, TT], F32))
        xsb = _es.enter_context(nc.sbuf_tensor(f"{pfx}_xs", [128, 2, NCK, TT], F32))
        stage = _es.enter_context(nc.sbuf_tensor(f"{pfx}_stage", [128, 2, NCK, TT], F32))
        w1b = _es.enter_context(nc.sbuf_tensor(f"{pfx}_w1", [128, NCK, 160] if part == 1 else [1, 1], F32))
        w2b = _es.enter_context(nc.sbuf_tensor(f"{pfx}_w2h", [32, 5 * D], F32))
        c5b = _es.enter_context(nc.sbuf_tensor(f"{pfx}_c5", [128, 5 * NCK], F32))
        tmxb = _es.enter_context(nc.sbuf_tensor(f"{pfx}_tmx", [128, NCK] if part == 1 else [1, 1], F32))
        WA = _es.enter_context(nc.sbuf_tensor(f"{pfx}_WA", [128, NCK, D], F32))
        WB = _es.enter_context(nc.sbuf_tensor(f"{pfx}_WB", [128, NCK, D], F32))
        dw1b = _es.enter_context(nc.sbuf_tensor(f"{pfx}_dw1", [128, NCK, 64] if part == 1 else [1, 1], F32))
        dw2b = _es.enter_context(nc.sbuf_tensor(f"{pfx}_dw2", [64, D] if part == 1 else [1, 1], F32))
        tdecb = _es.enter_context(nc.sbuf_tensor(f"{pfx}_tdec", [128, NCK] if part == 1 else [1, 1], F32))
        tdb = _es.enter_context(nc.sbuf_tensor(f"{pfx}_td", [64, TT] if part == 1 else [1, 1], F32))
        ewb = _es.enter_context(nc.sbuf_tensor(f"{pfx}_ew", [128, NCK, TT] if part == 1 else [1, 1], F32))
        csb = _es.enter_context(nc.sbuf_tensor(f"{pfx}_cs", [128, 2, NCK, TT] if part == 1 else [1, 1], F32))
        psx1 = _es.enter_context(nc.psum_tensor(f"{pfx}_psx1", [96, 512], F32))
        psx2 = _es.enter_context(nc.psum_tensor(f"{pfx}_psx2", [64, 512], F32))
        psm0 = _es.enter_context(nc.psum_tensor(f"{pfx}_psm0", [128, 512], F32))
        psm1 = _es.enter_context(nc.psum_tensor(f"{pfx}_psm1", [128, 512], F32))
        psp0 = _es.enter_context(nc.psum_tensor(f"{pfx}_psp0", [128, 512], F32))
        psp1 = _es.enter_context(nc.psum_tensor(f"{pfx}_psp1", [128, 512], F32))
        psd = _es.enter_context(nc.psum_tensor(f"{pfx}_psd", [64, 512] if part == 1 else [1, 1], F32))
        psm = [psm0, psm1]
        psp = [psp0, psp1]
        P = Prog(nc, pfx)

        pre = {}
        pre["w1"] = P.dma("sync", dma16(w1b[:, :, :], inp["maa_w1"].ap().rearrange("(c p) o -> p c o", p=128))) if part == 1 else None
        pre["w2"] = P.dma("sync", dma16(w2b[:, :], inp["w2h"].ap()))
        pre["c5"] = P.dma("sync", dma16(c5b[:, :], inp["c5col"].ap()))
        pre["tmx"] = P.dma("sync", dma16(tmxb[:, :], inp["tmx"].ap())) if part == 1 else None
        WA_src = inp["wkT"] if part == 1 else inp["wgT"]
        WB_src = inp["wrT"] if part == 1 else inp["wvT"]
        pre["WA"] = P.dma("sync", dma16(WA[:, :, :], WA_src.ap().rearrange("(c p) o -> p c o", p=128)))
        pre["WB"] = P.dma("sync", dma16(WB[:, :, :], WB_src.ap().rearrange("(c p) o -> p c o", p=128)))
        if part == 1:
            pre["dw1"] = P.dma("sync", dma16(dw1b[:, :, :], inp["dw1"].ap().rearrange("(c p) o -> p c o", p=128)))
            pre["dw2"] = P.dma("sync", dma16(dw2b[:, :], inp["dw2"].ap()))
            pre["tdec"] = P.dma("sync", dma16(tdecb[:, :], inp["tdec"].ap()))
        ring_xn = [None, None]
        ring_ew = None
        ring_xs = [None, None]
        ring_stage = [None, None]
        ring_ewcs = [None, None]
        ring_xxx = [None, None]
        use_mix = None
        use_dx = None
        use_psx = None
        use_psm = [None, None]
        use_psp = [None, None]
        use_psd = None
        prev_scan = None

        for i in range(NT):
            r = i % 2
            lo, hi = i * TT - 1, (i + 1) * TT + 1
            s_lo, s_hi = max(lo, 0), min(hi, T)
            d_lo = s_lo - lo
            d_hi = d_lo + (s_hi - s_lo)
            deps = [ring_xn[r]] if ring_xn[r] is not None else []
            ldx = P.dma("sync", dma16(xnb[:, r, :, d_lo:d_hi], rearr(scr["xnT"])[:, :, s_lo:s_hi]), deps)
            if part == 2:
                xdeps = [ring_xxx[r]] if ring_xxx[r] is not None else []
                ldxxx = P.dma("sync", dma16(
                    xxxb[0:32, r, :, :],
                    scr["xxxT"].ap().rearrange("p (f t) -> p f t", f=5)[:, :, i * TT:(i + 1) * TT]), xdeps)

            def v_prep(v, r=r, i=i):
                bx = xnb[:, r, :, :]
                if i == 0:
                    v.memset(xnb[:, r, :, 0:1], 0.0)
                if i == NT - 1:
                    v.memset(xnb[:, r, :, TT + 1:TT + 2], 0.0)
                v.tensor_tensor(dxb[:, :, :], bx[:, :, 0:TT], bx[:, :, 2:TT + 2], OP.add)
                last = v.scalar_tensor_tensor(dxb[:, :, :], dxb[:, :, :], 0.5, bx[:, :, 1:TT + 1],
                                              OP.mult, OP.subtract)
                if part == 1:
                    for c in range(NCK):
                        last = v.scalar_tensor_tensor(mixb[:, c, :], dxb[:, c, :], tmxb[:, c:c + 1],
                                                      bx[:, c, 1:TT + 1], OP.mult, OP.add)
                return last
            vp_deps = [ldx]
            if part == 1:
                vp_deps.append(pre["tmx"])
                if use_mix is not None:
                    vp_deps.append(use_mix)
            if ring_xn[r] is not None:
                vp_deps.append(ring_xn[r])
            if use_dx is not None:
                vp_deps.append(use_dx)
            vprep = P.step("vector", v_prep, vp_deps)

            if part == 1:
                def mm_maa1(t):
                    for c in range(NCK):
                        t.matmul(psx1[:, 0:TT], w1b[:, c, 0:96], mixb[:, c, :],
                                 start=(c == 0), stop=(c == NCK - 1))
                    last = None
                    for c in range(NCK):
                        last = t.matmul(psx2[:, 0:TT], w1b[:, c, 96:160], mixb[:, c, :],
                                        start=(c == 0), stop=(c == NCK - 1))
                    return last
                mmx = P.step("tensor", mm_maa1, [vprep, pre["w1"]] + ([use_psx] if use_psx else []))
                use_mix = mmx

                def s_xxx(s, r=r):
                    s.activation(xxxb[0:32, r, 0, :], psx1[0:32, 0:TT], AF.Tanh)
                    s.activation(xxxb[0:32, r, 1, :], psx1[32:64, 0:TT], AF.Tanh)
                    s.activation(xxxb[0:32, r, 2, :], psx1[64:96, 0:TT], AF.Tanh)
                    s.activation(xxxb[0:32, r, 3, :], psx2[0:32, 0:TT], AF.Tanh)
                    return s.activation(xxxb[0:32, r, 4, :], psx2[32:64, 0:TT], AF.Tanh)
                sx_deps = [mmx]
                if ring_xxx[r] is not None:
                    sx_deps.append(ring_xxx[r])
                sxxx = P.step("scalar", s_xxx, sx_deps)
                use_psx = sxxx
                stxxx = P.dma("gpsimd", dma16(
                    scr["xxxT"].ap().rearrange("p (f t) -> p f t", f=5)[:, :, i * TT:(i + 1) * TT],
                    xxxb[0:32, r, :, :]), [sxxx])
                xxx_ready = sxxx
                xxx_consumers = [stxxx]
            else:
                xxx_ready = ldxxx
                xxx_consumers = []

            last_step_of_tile = None
            for pi, p in enumerate(projs):
                pr = pi % 2
                f = f_of[p]
                xs_steps = []
                for oc in range(NCK):
                    def mm_one(t, pr=pr, f=f, r=r, oc=oc):
                        return t.matmul(psm[pr][:, 0:TT], w2b[:, f * D + oc * 128: f * D + (oc + 1) * 128],
                                        xxxb[0:32, r, f, :], start=True, stop=True)
                    mdeps = [xxx_ready, vprep, pre["w2"]]
                    if use_psm[pr] is not None:
                        mdeps.append(use_psm[pr])
                    mt = P.step("tensor", mm_one, mdeps)

                    def v_xs(v, pr=pr, r=r, oc=oc, f=f):
                        v.scalar_tensor_tensor(xsb[:, pr, oc, :], psm[pr][:, 0:TT],
                                               c5b[:, f * NCK + oc:f * NCK + oc + 1],
                                               dxb[:, oc, :], OP.add, OP.mult)
                        return v.tensor_tensor(xsb[:, pr, oc, :], xsb[:, pr, oc, :],
                                               xnb[:, r, oc, 1:TT + 1], OP.add)
                    xdeps2 = [mt, pre["c5"]]
                    if ring_xs[pr] is not None:
                        xdeps2.append(ring_xs[pr])
                    vxs = P.step("vector", v_xs, xdeps2)
                    use_psm[pr] = vxs
                    xs_steps.append(vxs)
                xxx_consumers.append(xs_steps[-1])

                if p == "w":
                    def mm_d1(t, pr=pr):
                        last = None
                        for c in range(NCK):
                            last = t.matmul(psd[:, 0:TT], dw1b[:, c, :], xsb[:, pr, c, :],
                                            start=(c == 0), stop=(c == NCK - 1))
                        return last
                    d1deps = xs_steps + [pre["dw1"]]
                    if use_psd is not None:
                        d1deps.append(use_psd)
                    md1 = P.step("tensor", mm_d1, d1deps)
                    std = P.step("scalar", lambda s: s.activation(tdb[:, :], psd[:, 0:TT], AF.Tanh), [md1])
                    use_psd = std
                    ew_steps = []
                    for oc in range(NCK):
                        def mm_d2(t, oc=oc):
                            return t.matmul(psp[oc % 2][:, 0:TT], dw2b[:, oc * 128:(oc + 1) * 128],
                                            tdb[:, :], start=True, stop=True)
                        ddeps = [std, pre["dw2"]]
                        if use_psp[oc % 2] is not None:
                            ddeps.append(use_psp[oc % 2])
                        md2 = P.step("tensor", mm_d2, ddeps)

                        def s_ew(s, oc=oc):
                            return s.activation(ewb[:, oc, :], psp[oc % 2][:, 0:TT],
                                                AF.Exp, bias=tdecb[:, oc:oc + 1], scale=1.0)
                        ewdeps = [md2, pre["tdec"]]
                        if ring_ew is not None:
                            ewdeps.append(ring_ew)
                        if prev_scan is not None:
                            ewdeps.append(prev_scan)
                        sew = P.step("scalar", s_ew, ewdeps)
                        use_psp[oc % 2] = sew
                        ew_steps.append(sew)

                    def v_scan(v, r=r, i=i):
                        last = None
                        for oc in range(NCK):
                            init = 0.0 if i == 0 else csb[:, 1 - r, oc, TT - 1:TT]
                            last = v.tensor_tensor_scan(csb[:, r, oc, :], ewb[:, oc, :],
                                                        ewb[:, oc, :], init, OP.add, OP.bypass)
                        return last
                    sdeps = list(ew_steps) + ([prev_scan] if prev_scan else [])
                    if ring_ewcs[r] is not None:
                        sdeps.append(ring_ewcs[r])
                    vscan = P.step("vector", v_scan, sdeps)
                    prev_scan = vscan
                    stew = P.dma("gpsimd", dma16(rearr(scr["ewT"])[:, :, i * TT:(i + 1) * TT], ewb[:, :, :]),
                          list(ew_steps))
                    st2 = P.dma("gpsimd", dma16(rearr(scr["csT"])[:, :, i * TT:(i + 1) * TT], csb[:, r, :, :]),
                                [vscan])
                    ring_ewcs[r] = st2
                    ring_ew = stew
                    ring_xs[pr] = md1
                    last_step_of_tile = md1
                elif p != "v":
                    W = WA if (p in ("k", "g")) else WB
                    Wdep = pre["WA"] if (p in ("k", "g")) else pre["WB"]
                    evac_steps = []
                    for oc in range(NCK):
                        def mm_p(t, pr=pr, oc=oc, W=W):
                            last = None
                            for c in range(NCK):
                                last = t.matmul(psp[oc % 2][:, 0:TT], W[:, c, oc * 128:(oc + 1) * 128],
                                                xsb[:, pr, c, :], start=(c == 0), stop=(c == NCK - 1))
                            return last
                        pdeps = xs_steps + [Wdep]
                        if use_psp[oc % 2] is not None:
                            pdeps.append(use_psp[oc % 2])
                        mp = P.step("tensor", mm_p, pdeps)

                        def v_evac(v, oc=oc, r=r):
                            return v.tensor_copy(stage[:, r, oc, :], psp[oc % 2][:, 0:TT])
                        edeps = [mp]
                        if ring_stage[r] is not None:
                            edeps.append(ring_stage[r])
                        ev = P.step("vector", v_evac, edeps)
                        use_psp[oc % 2] = ev
                        evac_steps.append(ev)
                    stp = P.dma("gpsimd", dma16(rearr(out_of[p])[:, :, i * TT:(i + 1) * TT],
                                                stage[:, r, :, :]), evac_steps)
                    ring_stage[r] = stp
                    ring_xs[pr] = evac_steps[-1]
                    last_step_of_tile = evac_steps[-1]
                else:
                    evac_steps = []
                    stage_flat = stage[:, r, :, :].rearrange("p c t -> p (c t)")
                    for ts in range(TT // 128):
                        for half in range(2):
                            def mm_v(t, pr=pr, ts=ts, half=half):
                                last = None
                                for c in range(NCK):
                                    last = t.matmul(psp[half][:, :],
                                                    xsb[:, pr, c, ts * 128:(ts + 1) * 128],
                                                    WB[:, c, half * 512:(half + 1) * 512],
                                                    start=(c == 0), stop=(c == NCK - 1))
                                return last
                            vdeps = xs_steps + [pre["WB"]]
                            if use_psp[half] is not None:
                                vdeps.append(use_psp[half])
                            mv = P.step("tensor", mm_v, vdeps)

                            def v_evacv(v, ts=ts, half=half, stage_flat=stage_flat):
                                base = ts * 1024 + half * 512
                                return v.tensor_copy(stage_flat[:, base:base + 512], psp[half][:, :])
                            edeps = [mv]
                            if ring_stage[r] is not None:
                                edeps.append(ring_stage[r])
                            ev = P.step("vector", v_evacv, edeps)
                            use_psp[half] = ev
                            evac_steps.append(ev)
                    stv = P.dma("gpsimd", dma16(
                        scr["vtok"].ap().rearrange("(n p) d -> p n d", p=128)[:, 2 * i:2 * i + 2, :],
                        stage_flat.rearrange("p (n d) -> p n d", d=1024)),
                        evac_steps)
                    ring_stage[r] = stv
                    ring_xs[pr] = evac_steps[-1]
                    last_step_of_tile = evac_steps[-1]
            ring_xn[r] = last_step_of_tile
            use_dx = xs_steps[-1]
            ring_xxx[r] = xxx_consumers[-1] if xxx_consumers else None
        P.play()



P2_NPAIR = 8
P2_FWDLVL = 3
P2_FWD = True
P2_BWD = True
P2_GN = True


def phase_p2(nc, B):
    inp, scr = B.inp, B.scr
    CH = 128
    NCH = T // CH
    with contextlib.ExitStack() as _es:
        b_r = _es.enter_context(nc.sbuf_tensor("p2_r", [128, T], F32))
        b_k = _es.enter_context(nc.sbuf_tensor("p2_k", [128, T], F32))
        b_cs = _es.enter_context(nc.sbuf_tensor("p2_cs", [128, NCH, CH], F32))
        b_t0 = _es.enter_context(nc.sbuf_tensor("p2_t0", [128, NCH, CH], F32))
        b_t1 = _es.enter_context(nc.sbuf_tensor("p2_t1", [128, T], F32))
        b_rf = _es.enter_context(nc.sbuf_tensor("p2_rf", [128, T], F32))
        b_kf = _es.enter_context(nc.sbuf_tensor("p2_kf", [128, T], F32))
        b_rb = _es.enter_context(nc.sbuf_tensor("p2_rb", [128, T], F32))
        b_kb = _es.enter_context(nc.sbuf_tensor("p2_kb", [128, T], F32))
        b_v = _es.enter_context(nc.sbuf_tensor("p2_v", [128, NCH, CH], F32))
        selbuf = _es.enter_context(nc.sbuf_tensor("p2_sel", [128, 2, 256], F32))
        gstg = _es.enter_context(nc.sbuf_tensor("p2_gstg", [2, 1, 1024], F32))
        lstg = _es.enter_context(nc.sbuf_tensor("p2_lstg", [2, 1, 1024], F32))
        st1 = _es.enter_context(nc.sbuf_tensor("p2_st1", [128, 64], F32))
        st2 = _es.enter_context(nc.sbuf_tensor("p2_st2", [128, 64], F32))
        st3 = _es.enter_context(nc.sbuf_tensor("p2_st3", [128, 64], F32))
        st1x = _es.enter_context(nc.sbuf_tensor("p2_st4", [128, 64], F32))
        identb = _es.enter_context(nc.sbuf_tensor("p2_ident", [128, 128], F32))
        ktf_r = _es.enter_context(nc.sbuf_tensor("p2_ktfr", [128, 2, 128], F32))
        ktb_r = _es.enter_context(nc.sbuf_tensor("p2_ktbr", [128, 2, 128], F32))
        gn2b = _es.enter_context(nc.sbuf_tensor("p2_gn2", [128, 2], F32))
        gnexpb = _es.enter_context(nc.sbuf_tensor("p2_gnexp", [2, 128], F32))
        trib = _es.enter_context(nc.sbuf_tensor("p2_tri", [128, 256], U8))
        lnwb = _es.enter_context(nc.sbuf_tensor("p2_lnw", [128, NCK], F32))
        lnbb = _es.enter_context(nc.sbuf_tensor("p2_lnb", [128, NCK], F32))
        psA = [_es.enter_context(nc.psum_tensor(f"p2_psA{j}", [128, 512], F32)) for j in range(2)]
        psY = [_es.enter_context(nc.psum_tensor(f"p2_psY{j}", [128, 512], F32)) for j in range(2)]
        psT = [_es.enter_context(nc.psum_tensor(f"p2_psT{j}", [128, 512], F32)) for j in range(2)]
        U_sb = _es.enter_context(nc.sbuf_tensor("p2_U", [128, 32, 128], F32))
        y_sb = _es.enter_context(nc.sbuf_tensor("p2_y", [128, T], F32))

        ysq = U_sb[:, :, :].rearrange("p a b -> p (a b)")
        ynb = b_v[:, :, :].rearrange("p a b -> p (a b)")

        P = Prog(nc, "p2")
        c_id = P.dma("sync", dma16(identb[:, :], inp["ident"].ap()))
        c_gn2 = P.dma("sync", dma16(gn2b[:, :], inp["gn2"].ap()))
        c_gne = P.dma("sync", dma16(gnexpb[:, :], inp["gnexp"].ap()))
        c_tri = P.dma("sync", dma16(trib[:, :], inp["trimask2"].ap()))
        c_lnw = P.dma("sync", dma16(lnwb[:, :], inp["lnw0"].ap()))
        c_lnb = P.dma("sync", dma16(lnbb[:, :], inp["lnb0"].ap()))
        prev_pair = None
        pv = {}

        for hp in range(P2_NPAIR):
            rows = slice(128 * hp, 128 * hp + 128)
            pdeps = [prev_pair] if prev_pair is not None else []
            ld_r = P.dma("sync", dma16(b_r[:, :], scr["rT"].ap()[rows, :]),
                         [pv["m_rb"]] if pv else [])
            ld_k = P.dma("sync", dma16(b_k[:, :], scr["kT"].ap()[rows, :]),
                         [pv["TA31"]] if pv else [])
            ld_cs = P.dma("sync", dma16(b_cs[:, :, :].rearrange("p c t -> p (c t)"), scr["csT"].ap()[rows, :]),
                          [pv["TA31"]] if pv else [])
            ld_ew = P.dma("sync", dma16(b_t1[:, :], scr["ewT"].ap()[rows, :]),
                          [pv["TA31"]] if pv else [])
            ld_v = P.dma("sync", dma16(
                b_v[:, :, :], scr["vtok"].ap().rearrange("(n p) d -> p n d", p=128)[:, :, 128 * hp:128 * hp + 128]),
                [pv["styn"]] if pv else [])
            csf = b_cs[:, :, :].rearrange("p c t -> p (c t)")
            t0f = b_t0[:, :, :].rearrange("p c t -> p (c t)")

            def v_csb(v):
                return v.tensor_tensor(t0f, csf, b_t1[:, :], OP.subtract)
            s_csb = P.step("vector", v_csb, [ld_cs, ld_ew] + ([pv["TC31"]] if pv else []))

            def v_cfp(v):
                v.tensor_scalar(b_t1[:, :], csf, csf[:, 2048:2049], 60.0, OP.subtract, OP.min)
                return v.tensor_scalar(b_t1[:, :], b_t1[:, :], -60.0, None, OP.max)
            s_cfp = P.step("vector", v_cfp, [s_csb])

            def v_cbp(v):
                v.tensor_scalar(csf, t0f, t0f[:, 2048:2049], 60.0, OP.subtract, OP.min)
                return v.tensor_scalar(csf, csf, -60.0, None, OP.max)
            s_cbp = P.step("vector", v_cbp, [s_cfp])

            e1 = P.step("scalar", lambda s: s.activation(t0f, b_t1[:, :], AF.Exp, scale=-1.0), [s_cbp])
            m_rf = P.step("vector", lambda v: v.tensor_tensor(b_rf[:, :], b_r[:, :], t0f, OP.mult),
                          [e1, ld_r] + ([pv["TC31"]] if pv else []))
            e2 = P.step("scalar", lambda s: s.activation(t0f, b_t1[:, :], AF.Exp, scale=1.0), [m_rf])
            m_kf = P.step("vector", lambda v: v.tensor_tensor(b_kf[:, :], b_k[:, :], t0f, OP.mult),
                          [e2, ld_k] + ([pv["TA31"]] if pv else []))
            e3 = P.step("scalar", lambda s: s.activation(b_t1[:, :], csf, AF.Exp, scale=1.0), [m_kf])
            m_rb = P.step("vector", lambda v: v.tensor_tensor(b_rb[:, :], b_r[:, :], b_t1[:, :], OP.mult),
                          [e3] + ([pv["TC31"]] if pv else []))
            e4 = P.step("scalar", lambda s: s.activation(b_t1[:, :], csf, AF.Exp, scale=-1.0), [m_rb])
            m_kb = P.step("vector", lambda v: v.tensor_tensor(b_kb[:, :], b_k[:, :], b_t1[:, :], OP.mult),
                          [e4] + ([pv["TA31"]] if pv else []))

            rfh1 = csf
            kfh1 = t0f
            rbh1 = b_t1
            kbh1 = b_k
            cp1 = P.dma("gpsimd", dma16(rfh1[0:64, :], b_rf[64:128, :]), [m_rf, e3, e4])
            cp2 = P.dma("gpsimd", dma16(kfh1[0:64, :], b_kf[64:128, :]), [m_kf])
            cp3 = P.dma("gpsimd", dma16(rbh1[0:64, :], b_rb[64:128, :]), [m_rb])
            cp4 = P.dma("gpsimd", dma16(kbh1[0:64, :], b_kb[64:128, :]), [m_kb])
            prep = [m_rf, m_kf, m_rb, m_kb, ld_v, c_id, c_tri, cp1, cp2, cp3, cp4]
            if pv:
                prep += [pv["app_last"], pv["VC31"]]

            if not P2_FWD:
                st_yn = P.dma("gpsimd", dma16(scr["ynT"].ap()[rows, :], b_rf[:, :]), prep)
                prev_pair = st_yn
                continue
            TA = [None] * NCH
            VA = [None] * NCH
            TB = [None] * NCH
            VB = [None] * NCH

            def rec_TA(c):
                cc = slice(c * CH, (c + 1) * CH)
                j = c % 2

                def f_TA(t, cc=cc, j=j):
                    t.matmul(psA[j][:, 0:128], b_kf[0:64, cc], b_rf[0:64, cc], start=True, stop=True)
                    t.matmul(psA[j][:, 128:256], kfh1[0:64, cc], rfh1[0:64, cc], start=True, stop=True)
                    t.matmul(psA[j][:, 256:384], b_kb[0:64, cc], b_rb[0:64, cc], start=True, stop=True)
                    t.matmul(psA[j][:, 384:512], kbh1[0:64, cc], rbh1[0:64, cc], start=True, stop=True)
                    t.transpose(psT[j][:, 0:128], b_kf[:, cc], identb[:, :])
                    return t.transpose(psT[j][:, 128:256], b_kb[:, cc], identb[:, :])
                d = list(prep)
                if c >= 2:
                    d.append(VB[c - 2])
                TA[c] = P.step("tensor", f_TA, d)

            def rec_VA(c):
                j = c % 2

                def f_VA(v, j=j, c=c):
                    v.select(selbuf[:, c % 2, :], trib[:, :], psA[j][:, 0:256], psA[j][:, 256:512])
                    v.tensor_copy(ktf_r[:, c % 2, :], psT[j][:, 0:128])
                    return v.tensor_copy(ktb_r[:, c % 2, :], psT[j][:, 128:256])
                d = [TA[c]]
                if c >= 2:
                    d.append(TB[c - 2])
                VA[c] = P.step("vector", f_VA, d)

            def rec_TB(c):
                cc = slice(c * CH, (c + 1) * CH)
                j = c % 2

                def f_TB(t, cc=cc, j=j, c=c):
                    t.matmul(psY[j][0:64, 0:128], b_v[:, c, 0:64], selbuf[:, c % 2, 0:128],
                             start=True, stop=True)
                    t.matmul(psY[j][64:128, 0:128], b_v[:, c, 64:128], selbuf[:, c % 2, 128:256],
                             start=True, stop=True, tile_position=(0, 64))
                    t.matmul(psT[j][0:64, 256:320], ktf_r[:, c % 2, 0:64], b_v[:, c, 0:64],
                             start=True, stop=True)
                    t.matmul(psT[j][64:128, 256:320], ktf_r[:, c % 2, 64:128], b_v[:, c, 64:128],
                             start=True, stop=True, tile_position=(0, 64))
                    t.matmul(psT[j][0:64, 320:384], ktb_r[:, c % 2, 0:64], b_v[:, c, 0:64],
                             start=True, stop=True)
                    return t.matmul(psT[j][64:128, 320:384], ktb_r[:, c % 2, 64:128], b_v[:, c, 64:128],
                                    start=True, stop=True, tile_position=(0, 64))
                d = [VA[c]]
                if c >= 2:
                    d.append(VB[c - 2])
                TB[c] = P.step("tensor", f_TB, d)

            def rec_VB(c):
                j = c % 2

                def f_VB(v, j=j, c=c):
                    v.tensor_copy(y_sb[:, c * CH:(c + 1) * CH], psY[j][:, 0:128])
                    return v.tensor_copy(U_sb[:, c, :], psT[j][:, 256:384])
                VB[c] = P.step("vector", f_VB, [TB[c]])

            for c in range(NCH + 1):
                if c < NCH:
                    rec_TA(c)
                    rec_VA(c)
                if c >= 1:
                    rec_TB(c - 1)
                    rec_VB(c - 1)

            def Spre_(c):
                return t0f[:, c * 128:c * 128 + 64]

            def Ssuf_(c):
                return t0f[:, c * 128 + 64:(c + 1) * 128]

            def f_scanS(v):
                v.memset(Spre_(0), 0.0)
                v.memset(Ssuf_(NCH - 1), 0.0)
                for c in range(1, NCH):
                    v.tensor_tensor(Spre_(c), Spre_(c - 1), U_sb[:, c - 1, 0:64], OP.add)
                last = None
                for c in range(NCH - 2, -1, -1):
                    last = v.tensor_tensor(Ssuf_(c), Ssuf_(c + 1), U_sb[:, c + 1, 64:128], OP.add)
                return last
            vscanS = P.step("vector", f_scanS, [VB[NCH - 1], VB[NCH - 2]])

            TC = [None] * NCH
            VC = [None] * NCH

            def rec_TC(c):
                cc = slice(c * CH, (c + 1) * CH)
                j = c % 2

                def f_TC(t, cc=cc, j=j, c=c):
                    t.matmul(psY[j][0:64, 0:128], Spre_(c)[0:64, :], b_rf[0:64, cc],
                             start=True, stop=False)
                    t.matmul(psY[j][0:64, 0:128], Ssuf_(c)[0:64, :], b_rb[0:64, cc],
                             start=False, stop=True)
                    t.matmul(psY[j][64:128, 0:128], Spre_(c)[64:128, :], b_rf[64:128, cc],
                             start=True, stop=False, tile_position=(64, 64))
                    return t.matmul(psY[j][64:128, 0:128], Ssuf_(c)[64:128, :], b_rb[64:128, cc],
                                    start=False, stop=True, tile_position=(64, 64))
                d = [vscanS]
                if c >= 2:
                    d.append(VC[c - 2])
                TC[c] = P.step("tensor", f_TC, d)

            def rec_VC(c):
                j = c % 2

                def f_VC(v, j=j, c=c):
                    return v.tensor_tensor(y_sb[:, c * CH:(c + 1) * CH], y_sb[:, c * CH:(c + 1) * CH],
                                           psY[j][:, 0:128], OP.add)
                VC[c] = P.step("vector", f_VC, [TC[c]])

            for c in range(NCH):
                rec_TC(c)
                rec_VC(c)
            scan_done = VC[NCH - 1]

            g_sq = P.step("scalar", lambda s: s.activation(ysq[:, :], y_sb[:, :], AF.Square),
                          [scan_done, vscanS])
            gst_list = []
            for jt in range(8):
                ts_ = slice(jt * 512, (jt + 1) * 512)
                jr = jt % 2

                def f_gmm(t, ts_=ts_):
                    t.matmul(psA[0][0:2, 0:512], gn2b[:, :], y_sb[:, ts_], start=True, stop=True)
                    return t.matmul(psT[0][0:2, 0:512], gn2b[:, :], ysq[:, ts_], start=True, stop=True)
                d = [g_sq, c_gn2, scan_done]
                if gst_list:
                    d.append(gst_list[-1])
                gmm = P.step("tensor", f_gmm, d)

                def f_gev(v):
                    v.tensor_copy(gstg[:, 0, 0:512], psA[0][0:2, 0:512])
                    return v.tensor_copy(gstg[:, 0, 512:1024], psT[0][0:2, 0:512])
                gev = P.step("vector", f_gev, [gmm] + ([gst_list[-1]] if gst_list else []))
                s1d = P.dma("gpsimd", dma16(scr["gn_a"].ap()[:, jt * 512:(jt + 1) * 512],
                                            gstg[:, 0, 0:512]), [gev])
                s2d = P.dma("gpsimd", dma16(scr["gn_b"].ap()[:, jt * 512:(jt + 1) * 512],
                                            gstg[:, 0, 512:1024]), [gev])
                gst_list.append(s2d)
            rs1 = P.dma("gpsimd", dma16(
                st1[:, :],
                scr["gn_a"].ap().rearrange("h (b c) -> (h b) c", c=64)), [gst_list[-1]])
            rs2 = P.dma("gpsimd", dma16(
                st2[:, :],
                scr["gn_b"].ap().rearrange("h (b c) -> (h b) c", c=64)), [gst_list[-1]])

            def f_var(v):
                v.tensor_scalar(st1[:, :], st1[:, :], 1.0 / 64.0, None, OP.mult)
                v.tensor_scalar(st2[:, :], st2[:, :], 1.0 / 64.0, None, OP.mult)
                v.tensor_tensor(st3[:, :], st1[:, :], st1[:, :], OP.mult)
                v.tensor_tensor(st2[:, :], st2[:, :], st3[:, :], OP.subtract)
                return v.tensor_scalar(st2[:, :], st2[:, :], GN_EPS, None, OP.add)
            g_var = P.step("vector", f_var, [rs1, rs2])

            def f_rs(s):
                s.activation(st3[:, :], st2[:, :], AF.Ln)
                return s.activation(st3[:, :], st3[:, :], AF.Exp, scale=-0.5)
            g_rs = P.step("scalar", f_rs, [g_var])

            def f_fix(v):
                v.tensor_tensor(st1x[:, :], st3[:, :], st3[:, :], OP.mult)
                v.tensor_tensor(st1x[:, :], st1x[:, :], st2[:, :], OP.mult)
                v.tensor_scalar(st1x[:, :], st1x[:, :], -0.5, 1.5, OP.mult, OP.add)
                v.tensor_tensor(st3[:, :], st3[:, :], st1x[:, :], OP.mult)
                v.tensor_tensor(st1[:, :], st1[:, :], st3[:, :], OP.mult)
                return v.tensor_scalar(st1[:, :], st1[:, :], -1.0, None, OP.mult)
            g_fix = P.step("vector", f_fix, [g_rs])

            bs1 = P.dma("gpsimd", dma16(
                scr["gn_a"].ap().rearrange("h (b c) -> (h b) c", c=64), st3[:, :]), [g_fix, rs1, rs2])
            bs2 = P.dma("gpsimd", dma16(
                scr["gn_b"].ap().rearrange("h (b c) -> (h b) c", c=64), st1[:, :]), [g_fix, rs1, rs2])

            app_list = []
            emm_prev = None
            for jt in range(8):
                ts_ = slice(jt * 512, (jt + 1) * 512)
                jr = jt % 2

                def mkld(v=None):
                    pass
                dl = [bs1, bs2]
                dl2 = list(dl)
                if app_list:
                    dl2.append(emm_prev)
                ld1 = P.dma("sync", dma16(lstg[:, 0, 0:512],
                                          scr["gn_a"].ap()[:, jt * 512:(jt + 1) * 512]), dl2)
                ld2 = P.dma("sync", dma16(lstg[:, 0, 512:1024],
                                          scr["gn_b"].ap()[:, jt * 512:(jt + 1) * 512]), dl2)

                def f_emm(t):
                    t.matmul(psA[1][:, 0:512], gnexpb[:, :], lstg[0:2, 0, 0:512], start=True, stop=True)
                    return t.matmul(psT[1][:, 0:512], gnexpb[:, :], lstg[0:2, 0, 512:1024], start=True, stop=True)
                d = [ld1, ld2, c_gne]
                if app_list:
                    d.append(app_list[-1][0])
                emm = P.step("tensor", f_emm, d)
                emm_prev = emm

                def f_app(v, ts_=ts_, hp=hp, jr=jr):
                    yslc = ynb[:, jr * 512:(jr + 1) * 512]
                    v.tensor_tensor(yslc, y_sb[:, ts_], psA[1][:, 0:512], OP.mult)
                    v.tensor_tensor(yslc, yslc, psT[1][:, 0:512], OP.add)
                    return v.tensor_scalar(yslc, yslc, lnwb[:, hp:hp + 1],
                                           lnbb[:, hp:hp + 1], OP.mult, OP.add)
                ad = [emm, c_lnw, c_lnb]
                if len(app_list) >= 2:
                    ad.append(app_list[-2][1])
                va = P.step("vector", f_app, ad)
                styj = P.dma("gpsimd", dma16(scr["ynT"].ap()[rows, jt * 512:(jt + 1) * 512],
                                             ynb[:, jr * 512:(jr + 1) * 512]), [va])
                app_list.append((va, styj))
            st_yn = app_list[-1][1]
            prev_pair = st_yn
            pv = {"m_rb": m_rb, "cp4": cp4, "TA31": TA[NCH - 1], "TB31": TB[NCH - 1],
                  "TC31": TC[NCH - 1], "VC31": VC[NCH - 1], "app_last": app_list[-1][0],
                  "styn": st_yn}
        P.play()


def _proj_steps(P, W, Wdep, rhs_of, psp, use_psp, evac_fn, evac_deps_extra, mm_deps, TTn):
    evs = []
    for oc in range(NCK):
        def f_mm(t, oc=oc):
            last = None
            for c in range(NCK):
                last = t.matmul(psp[oc % 2][:, 0:TTn], W[:, c, oc * 128:(oc + 1) * 128],
                                rhs_of(c), start=(c == 0), stop=(c == NCK - 1))
            return last
        d = list(mm_deps) + [Wdep]
        if use_psp[oc % 2] is not None:
            d.append(use_psp[oc % 2])
        mp = P.step("tensor", f_mm, d)
        ev = P.step(evac_fn[0], lambda e, oc=oc: evac_fn[1](e, oc), [mp] + list(evac_deps_extra))
        use_psp[oc % 2] = ev
        evs.append(ev)
    return evs


def phase_p3a(nc, B):
    TTn = 256
    NT = T // TTn
    inp, scr = B.inp, B.scr
    with contextlib.ExitStack() as _es:
        yn = _es.enter_context(nc.sbuf_tensor("p3a_yn", [128, 2, NCK, TTn], F32))
        gp = _es.enter_context(nc.sbuf_tensor("p3a_gp", [128, 2, NCK, TTn], F32))
        gs = _es.enter_context(nc.sbuf_tensor("p3a_gs", [128, NCK, TTn], F32))
        yng = _es.enter_context(nc.sbuf_tensor("p3a_yng", [128, NCK, TTn], F32))
        stage = _es.enter_context(nc.sbuf_tensor("p3a_stage", [128, 2, NCK, TTn], F32))
        gout = _es.enter_context(nc.sbuf_tensor("p3a_gout", [128, 2, NCK, TTn], F32))
        WO = _es.enter_context(nc.sbuf_tensor("p3a_wo", [128, NCK, D], F32))
        psp0 = _es.enter_context(nc.psum_tensor("p3a_psp0", [128, 512], F32))
        psp1 = _es.enter_context(nc.psum_tensor("p3a_psp1", [128, 512], F32))
        psp = [psp0, psp1]
        P = Prog(nc, "p3a")
        wo_ld = P.dma("sync", dma16(WO[:, :, :], inp["woT"].ap().rearrange("(c p) o -> p c o", p=128)))
        ring_ld = [None, None]
        ring_st = [None, None]
        ring_g = [None, None]
        use_psp = [None, None]
        use_gs = None
        use_yng = None
        for i in range(NT):
            r = i % 2
            tt = slice(i * TTn, (i + 1) * TTn)
            d0 = [ring_ld[r]] if ring_ld[r] is not None else []
            ld_yn = P.dma("sync", dma16(yn[:, r, :, :], rearr(scr["ynT"])[:, :, tt]), d0)
            ld_gp = P.dma("sync", dma16(gp[:, r, :, :], rearr(scr["gpreT"])[:, :, tt]), d0)

            s_sig = P.step("scalar", lambda s, r=r: s.activation(gs[:, :, :], gp[:, r, :, :], AF.Sigmoid),
                           [ld_gp] + ([use_gs] if use_gs else []))

            def f_g(v, r=r):
                v.tensor_tensor(gout[:, r, :, :], gp[:, r, :, :], gs[:, :, :], OP.mult)
                return v.tensor_tensor(yng[:, :, :], yn[:, r, :, :], gout[:, r, :, :], OP.mult)
            d = [s_sig, ld_yn]
            if use_yng is not None:
                d.append(use_yng)
            if ring_g[r] is not None:
                d.append(ring_g[r])
            v_g = P.step("vector", f_g, d)
            use_gs = v_g

            st_g = P.dma("gpsimd", dma16(rearr(scr["gT"])[:, :, tt], gout[:, r, :, :]), [v_g])
            ring_g[r] = st_g

            def mk_evac(e, oc, r=r):
                return e.tensor_copy(stage[:, r, oc, :], psp[oc % 2][:, 0:TTn])
            evs = _proj_steps(P, WO, wo_ld, lambda c: yng[:, c, :], psp, use_psp,
                              ("vector", mk_evac), [ring_st[r]] if ring_st[r] is not None else [],
                              [v_g], TTn)
            use_yng = evs[-1]
            st_s0 = P.dma("gpsimd", dma16(rearr(scr["s0T"])[:, :, tt], stage[:, r, :, :]), evs)
            ring_st[r] = st_s0
            ring_ld[r] = evs[-1]
        P.play()


def phase_p3b(nc, B):
    TTn = 256
    NT = T // TTn
    inp, scr = B.inp, B.scr
    with contextlib.ExitStack() as _es:
        xn = _es.enter_context(nc.sbuf_tensor("p3b_xn", [128, 2, NCK, TTn + 1], F32))
        xb = _es.enter_context(nc.sbuf_tensor("p3b_x", [128, 2, NCK, TTn], F32))
        s0 = _es.enter_context(nc.sbuf_tensor("p3b_s0", [128, 2, NCK, TTn], F32))
        db = _es.enter_context(nc.sbuf_tensor("p3b_d", [128, NCK, TTn], F32))
        xr0 = _es.enter_context(nc.sbuf_tensor("p3b_xr0", [128, NCK, TTn], F32))
        xk0 = _es.enter_context(nc.sbuf_tensor("p3b_xk0", [128, NCK, TTn], F32))
        r0 = _es.enter_context(nc.sbuf_tensor("p3b_r0", [128, NCK, TTn], F32))
        k0 = _es.enter_context(nc.sbuf_tensor("p3b_k0", [128, NCK, TTn], F32))
        ffn = _es.enter_context(nc.sbuf_tensor("p3b_ffn", [128, NCK, TTn], F32))
        k0q = xr0
        fft = xk0
        WR = _es.enter_context(nc.sbuf_tensor("p3b_wr", [128, NCK, D], F32))
        WK = _es.enter_context(nc.sbuf_tensor("p3b_wk", [128, NCK, D], F32))
        WV = _es.enter_context(nc.sbuf_tensor("p3b_wv", [128, NCK, D], F32))
        murb = _es.enter_context(nc.sbuf_tensor("p3b_mur", [128, NCK], F32))
        mukb = _es.enter_context(nc.sbuf_tensor("p3b_muk", [128, NCK], F32))
        psp0 = _es.enter_context(nc.psum_tensor("p3b_psp0", [128, 512], F32))
        psp1 = _es.enter_context(nc.psum_tensor("p3b_psp1", [128, 512], F32))
        psp = [psp0, psp1]
        P = Prog(nc, "p3b")
        wld = [P.dma("sync", dma16(WR[:, :, :], inp["wrfT"].ap().rearrange("(c p) o -> p c o", p=128))),
               P.dma("sync", dma16(WK[:, :, :], inp["wkfT"].ap().rearrange("(c p) o -> p c o", p=128))),
               P.dma("sync", dma16(WV[:, :, :], inp["wvfT"].ap().rearrange("(c p) o -> p c o", p=128))),
               P.dma("sync", dma16(murb[:, :], inp["mur"].ap())),
               P.dma("sync", dma16(mukb[:, :], inp["muk"].ap()))]
        ring = [None, None]
        use_psp = [None, None]
        use_mid = None
        for i in range(NT):
            r = i % 2
            tt = slice(i * TTn, (i + 1) * TTn)
            lo = i * TTn - 1
            s_lo = max(lo, 0)
            d_lo = s_lo - lo
            d0 = [ring[r]] if ring[r] is not None else []
            ld_xn = P.dma("sync", dma16(xn[:, r, :, d_lo:TTn + 1],
                                        rearr(scr["xnT"])[:, :, s_lo:(i + 1) * TTn]), d0)
            ld_x = P.dma("sync", dma16(xb[:, r, :, :], rearr(inp["xT"])[:, :, tt]), d0)
            ld_s0 = P.dma("sync", dma16(s0[:, r, :, :], rearr(scr["s0T"])[:, :, tt]), d0)

            def f_mix(v, r=r, i=i):
                if i == 0:
                    v.memset(xn[:, r, :, 0:1], 0.0)
                bx = xn[:, r, :, :]
                v.tensor_tensor(db[:, :, :], bx[:, :, 1:TTn + 1], bx[:, :, 0:TTn], OP.subtract)
                for c in range(NCK):
                    v.scalar_tensor_tensor(xr0[:, c, :], db[:, c, :], murb[:, c:c + 1],
                                           bx[:, c, 0:TTn], OP.mult, OP.add)
                last = None
                for c in range(NCK):
                    last = v.scalar_tensor_tensor(xk0[:, c, :], db[:, c, :], mukb[:, c:c + 1],
                                                  bx[:, c, 0:TTn], OP.mult, OP.add)
                return last
            d = [ld_xn, wld[3], wld[4]]
            if use_mid is not None:
                d.append(use_mid)
            v_mix = P.step("vector", f_mix, d)

            def ev_r0(e, oc):
                return e.activation(r0[:, oc, :], psp[oc % 2][:, 0:TTn], AF.Sigmoid)
            evs_r = _proj_steps(P, WR, wld[0], lambda c: xr0[:, c, :], psp, use_psp,
                                ("scalar", ev_r0), [], [v_mix], TTn)

            def ev_k0(e, oc):
                return e.activation(k0[:, oc, :], psp[oc % 2][:, 0:TTn], AF.Relu)
            evs_k = _proj_steps(P, WK, wld[1], lambda c: xk0[:, c, :], psp, use_psp,
                                ("scalar", ev_k0), [], [v_mix], TTn)

            v_k0q = P.step("vector", lambda v: v.tensor_tensor(k0q[:, :, :], k0[:, :, :], k0[:, :, :], OP.mult),
                           evs_k + evs_r)

            def ev_ffn(e, oc):
                return e.tensor_tensor(ffn[:, oc, :], psp[oc % 2][:, 0:TTn], r0[:, oc, :], OP.mult)
            evs_v = _proj_steps(P, WV, wld[2], lambda c: k0q[:, c, :], psp, use_psp,
                                ("vector", ev_ffn), evs_r, [v_k0q], TTn)

            s_g0 = P.step("scalar", lambda s: s.activation(fft[:, :, :], ffn[:, :, :], AF.Sigmoid),
                          evs_v)

            def f_out(v, r=r):
                v.tensor_tensor(fft[:, :, :], fft[:, :, :], ffn[:, :, :], OP.mult)
                v.tensor_tensor(fft[:, :, :], fft[:, :, :], s0[:, r, :, :], OP.mult)
                return v.tensor_tensor(xn[:, r, :, 0:TTn], xb[:, r, :, :], fft[:, :, :], OP.add)
            v_out = P.step("vector", f_out, [s_g0, ld_x, ld_s0])
            use_mid = v_out

            st = P.dma("gpsimd", dma16(rearr(scr["x1T"])[:, :, tt], xn[:, r, :, 0:TTn]), [v_out])
            ring[r] = st
        P.play()


def phase_p4(nc, B):
    TTn = 256
    NT = T // TTn
    inp, scr = B.inp, B.scr
    with contextlib.ExitStack() as _es:
        xb = _es.enter_context(nc.sbuf_tensor("p4_x", [128, 2, NCK, TTn], F32))
        sq = _es.enter_context(nc.sbuf_tensor("p4_sq", [128, NCK, TTn], F32))
        xn1 = _es.enter_context(nc.sbuf_tensor("p4_xn1", [128, 2, NCK, TTn], F32))
        xn2 = _es.enter_context(nc.sbuf_tensor("p4_xn2", [128, 2, NCK, TTn], F32))
        stage = _es.enter_context(nc.sbuf_tensor("p4_stage", [128, 2, NCK, TTn], F32))
        stgh = _es.enter_context(nc.sbuf_tensor("p4_stgh", [128, 2, NCK, TTn], BF16))
        stgl = _es.enter_context(nc.sbuf_tensor("p4_stgl", [128, 2, NCK, TTn], BF16))
        vstage = _es.enter_context(nc.sbuf_tensor("p4_vstage", [128, 2, 2 * D], F32))
        rows = _es.enter_context(nc.sbuf_tensor("p4_rows", [1, 6 * TTn], F32))
        onescol = _es.enter_context(nc.sbuf_tensor("p4_ones", [128, 1], F32))
        onesrow = _es.enter_context(nc.sbuf_tensor("p4_ones1", [1, 128], F32))
        WQ = _es.enter_context(nc.sbuf_tensor("p4_wq", [128, NCK, D], F32))
        WK = _es.enter_context(nc.sbuf_tensor("p4_wk", [128, NCK, D], F32))
        WV = _es.enter_context(nc.sbuf_tensor("p4_wv", [128, NCK, D], F32))
        nwb = _es.enter_context(nc.sbuf_tensor("p4_nw", [128, NCK], F32))
        bqb = _es.enter_context(nc.sbuf_tensor("p4_bq", [128, NCK], F32))
        bkb = _es.enter_context(nc.sbuf_tensor("p4_bk", [128, NCK], F32))
        bvb = _es.enter_context(nc.sbuf_tensor("p4_bv", [128, NCK], F32))
        ps_s = _es.enter_context(nc.psum_tensor("p4_ps_s", [1, 512], F32))
        ps_q = _es.enter_context(nc.psum_tensor("p4_ps_q", [1, 512], F32))
        ps_mu = _es.enter_context(nc.psum_tensor("p4_ps_mu", [128, 512], F32))
        ps_rs = _es.enter_context(nc.psum_tensor("p4_ps_rs", [128, 512], F32))
        psp0 = _es.enter_context(nc.psum_tensor("p4_psp0", [128, 512], F32))
        psp1 = _es.enter_context(nc.psum_tensor("p4_psp1", [128, 512], F32))
        psp = [psp0, psp1]
        P = Prog(nc, "p4")

        def R(i):
            return rows[0:1, i * TTn:(i + 1) * TTn]

        wld = [P.dma("sync", dma16(WQ[:, :, :], inp["wqT1"].ap().rearrange("(c p) o -> p c o", p=128))),
               P.dma("sync", dma16(WK[:, :, :], inp["wkT1"].ap().rearrange("(c p) o -> p c o", p=128))),
               P.dma("sync", dma16(WV[:, :, :], inp["wvT1"].ap().rearrange("(c p) o -> p c o", p=128))),
               P.dma("sync", dma16(onescol[:, :], inp["ones128"].ap())),
               P.dma("sync", dma16(onesrow[:, :], inp["ones1x128"].ap())),
               P.dma("sync", dma16(nwb[:, :], inp["normw1"].ap())),
               P.dma("sync", dma16(bqb[:, :], inp["bq1"].ap())),
               P.dma("sync", dma16(bkb[:, :], inp["bk1"].ap())),
               P.dma("sync", dma16(bvb[:, :], inp["bv1"].ap()))]
        ring = [None, None]
        use_psp = [None, None]
        use_sq = None
        use_stats = None
        use_bc = None
        for i in range(NT):
            r = i % 2
            tt = slice(i * TTn, (i + 1) * TTn)
            d0 = [ring[r]] if ring[r] is not None else []
            ldx = P.dma("sync", dma16(xb[:, r, :, :], rearr(scr["x1T"])[:, :, tt]), d0)

            def norm_pass(src_ap, dst, eps, vr_extra, tag, i=i, r=r):
                sqst = P.step("scalar", lambda s: s.activation(sq[:, :, :], src_ap, AF.Square),
                              [ldx] + ([use_sq[tag]] if use_sq and use_sq.get(tag) else []))
                return sqst
            sq1 = P.step("scalar", lambda s, r=r: s.activation(sq[:, :, :], xb[:, r, :, :], AF.Square),
                         [ldx] + ([use_sq] if use_sq else []))

            def f_stats1(t, r=r):
                for c in range(NCK):
                    t.matmul(ps_s[0:1, 0:TTn], onescol[:, :], xb[:, r, c, :],
                             start=(c == 0), stop=(c == NCK - 1))
                last = None
                for c in range(NCK):
                    last = t.matmul(ps_q[0:1, 0:TTn], onescol[:, :], sq[:, c, :],
                                    start=(c == 0), stop=(c == NCK - 1))
                return last
            mst1 = P.step("tensor", f_stats1, [ldx, sq1, wld[3]] + ([use_stats] if use_stats else []))

            def f_vst1(v):
                v.tensor_scalar(R(2), ps_s[0:1, 0:TTn], 1.0 / D, None, OP.mult)
                v.tensor_scalar(R(1), ps_q[0:1, 0:TTn], 1.0 / D, None, OP.mult)
                v.tensor_tensor(R(0), R(2), R(2), OP.mult)
                v.tensor_tensor(R(3), R(1), R(0), OP.subtract)
                return v.tensor_scalar(R(3), R(3), LN_EPS, None, OP.add)
            vst1 = P.step("vector", f_vst1, [mst1])
            use_stats = vst1

            def f_rs1(s):
                s.activation(R(5), R(3), AF.Ln)
                return s.activation(R(4), R(5), AF.Exp, scale=-0.5)
            srs1 = P.step("scalar", f_rs1, [vst1])

            def f_nw1(v):
                v.tensor_tensor(R(5), R(4), R(4), OP.mult)
                v.tensor_tensor(R(5), R(5), R(3), OP.mult)
                v.tensor_scalar(R(5), R(5), -0.5, 1.5, OP.mult, OP.add)
                return v.tensor_tensor(R(4), R(4), R(5), OP.mult)
            vnw1 = P.step("vector", f_nw1, [srs1])

            def f_bc1(t):
                t.matmul(ps_mu[:, 0:TTn], onesrow[:, :], R(2), start=True, stop=True)
                return t.matmul(ps_rs[:, 0:TTn], onesrow[:, :], R(4), start=True, stop=True)
            bc1 = P.step("tensor", f_bc1, [vnw1, wld[4]] + ([use_bc] if use_bc else []))

            def f_xn1(v, r=r):
                last = None
                for c in range(NCK):
                    v.tensor_tensor(xn1[:, r, c, :], xb[:, r, c, :], ps_mu[:, 0:TTn], OP.subtract)
                    last = v.tensor_tensor(xn1[:, r, c, :], xn1[:, r, c, :], ps_rs[:, 0:TTn], OP.mult)
                return last
            vxn1 = P.step("vector", f_xn1, [bc1])
            st_xn1 = P.dma("gpsimd", dma16(rearr(scr["xn1T"])[:, :, tt], xn1[:, r, :, :]), [vxn1])

            sq2 = P.step("scalar", lambda s, r=r: s.activation(sq[:, :, :], xn1[:, r, :, :], AF.Square),
                         [vxn1, mst1])
            use_sq = sq2

            def f_stats2(t):
                last = None
                for c in range(NCK):
                    last = t.matmul(ps_q[0:1, 0:TTn], onescol[:, :], sq[:, c, :],
                                    start=(c == 0), stop=(c == NCK - 1))
                return last
            mst2 = P.step("tensor", f_stats2, [sq2, vst1])

            def f_vst2(v):
                v.tensor_scalar(R(3), ps_q[0:1, 0:TTn], 1.0 / D, RMS_EPS, OP.mult, OP.add)
                return v.tensor_scalar(R(3), R(3), 1.0, None, OP.mult)
            vst2 = P.step("vector", f_vst2, [mst2])

            def f_rs2(s):
                s.activation(R(5), R(3), AF.Ln)
                return s.activation(R(4), R(5), AF.Exp, scale=-0.5)
            srs2 = P.step("scalar", f_rs2, [vst2])

            def f_nw2(v):
                v.tensor_tensor(R(5), R(4), R(4), OP.mult)
                v.tensor_tensor(R(5), R(5), R(3), OP.mult)
                v.tensor_scalar(R(5), R(5), -0.5, 1.5, OP.mult, OP.add)
                return v.tensor_tensor(R(4), R(4), R(5), OP.mult)
            vnw2 = P.step("vector", f_nw2, [srs2])

            def f_bc2(t):
                return t.matmul(ps_rs[:, 0:TTn], onesrow[:, :], R(4), start=True, stop=True)
            bc2 = P.step("tensor", f_bc2, [vnw2, vxn1])
            use_bc = None

            def f_xn2(v, r=r):
                last = None
                for c in range(NCK):
                    v.tensor_tensor(xn2[:, r, c, :], xn1[:, r, c, :], ps_rs[:, 0:TTn], OP.mult)
                    last = v.tensor_scalar(xn2[:, r, c, :], xn2[:, r, c, :], nwb[:, c:c + 1],
                                           None, OP.mult)
                return last
            vxn2 = P.step("vector", f_xn2, [bc2, wld[5]])
            use_bc = vxn2
            st_xn2 = P.dma("gpsimd", dma16(rearr(scr["xn2T"])[:, :, tt], xn2[:, r, :, :]), [vxn2])

            def ev_q(e, oc, r=r):
                return e.tensor_scalar(stage[:, r, oc, :], psp[oc % 2][:, 0:TTn],
                                       bqb[:, oc:oc + 1], None, OP.add)
            evq = _proj_steps(P, WQ, wld[0], lambda c, r=r: xn2[:, r, c, :], psp, use_psp,
                              ("vector", ev_q), [wld[6]] + ([ring[r]] if ring[r] is not None else []),
                              [vxn2], TTn)

            def f_hl(v, r=r):
                v.tensor_copy(stgh[:, r, :, :], stage[:, r, :, :])
                return v.tensor_tensor(stgl[:, r, :, :], stage[:, r, :, :], stgh[:, r, :, :],
                                       OP.subtract)
            hl_q = P.step("vector", f_hl, evq + ([ring[r]] if ring[r] is not None else []))
            st_qh = P.dma("gpsimd", dma16(rearr(scr["q1hT"])[:, :, tt], stgh[:, r, :, :]), [hl_q])
            st_ql = P.dma("gpsimd", dma16(rearr(scr["q1lT"])[:, :, tt], stgl[:, r, :, :]), [hl_q])

            def ev_k(e, oc, r=r):
                return e.tensor_scalar(stage[:, r, oc, :], psp[oc % 2][:, 0:TTn],
                                       bkb[:, oc:oc + 1], None, OP.add)
            evk = _proj_steps(P, WK, wld[1], lambda c, r=r: xn2[:, r, c, :], psp, use_psp,
                              ("vector", ev_k), [wld[7], hl_q], [vxn2], TTn)
            hl_k = P.step("vector", f_hl, evk + [st_qh, st_ql])
            st_kh = P.dma("gpsimd", dma16(rearr(scr["k1hT"])[:, :, tt], stgh[:, r, :, :]), [hl_k])
            st_k = P.dma("gpsimd", dma16(rearr(scr["k1lT"])[:, :, tt], stgl[:, r, :, :]), [hl_k])

            evs_v = []
            vsf = vstage[:, r, :]
            for ts_ in range(TTn // 128):
                for half in range(2):
                    def f_mmv(t, ts_=ts_, half=half, r=r):
                        last = None
                        for c in range(NCK):
                            last = t.matmul(psp[half][:, :],
                                            xn2[:, r, c, ts_ * 128:(ts_ + 1) * 128],
                                            WV[:, c, half * 512:(half + 1) * 512],
                                            start=(c == 0), stop=(c == NCK - 1))
                        return last
                    d = [vxn2, wld[2]]
                    if use_psp[half] is not None:
                        d.append(use_psp[half])
                    mv = P.step("tensor", f_mmv, d)

                    def f_evv(v, ts_=ts_, half=half, vsf=vsf):
                        base = ts_ * 1024 + half * 512
                        return v.tensor_copy(vsf[:, base:base + 512], psp[half][:, :])
                    d = [mv, wld[8]]
                    if ring[r] is not None:
                        d.append(ring[r])
                    ev = P.step("vector", f_evv, d)
                    use_psp[half] = ev
                    evs_v.append(ev)
            st_v = P.dma("gpsimd", dma16(
                scr["v1tok"].ap().rearrange("(n p) d -> p n d", p=128)[:, 2 * i:2 * i + 2, :],
                vsf.rearrange("p (n d) -> p n d", d=1024)), evs_v)
            ring[r] = st_v
        P.play()


P5_NH = 16
P5_NI = 8


def phase_p5(nc, B):
    inp, scr = B.inp, B.scr
    NI = P5_NI
    with contextlib.ExitStack() as _es:
        qh = _es.enter_context(nc.sbuf_tensor("p5_qh", [64, T], BF16))
        qs2 = _es.enter_context(nc.sbuf_tensor("p5_qs2", [128, T], BF16))
        ks2 = _es.enter_context(nc.sbuf_tensor("p5_ks2", [128, T], BF16))
        kl = _es.enter_context(nc.sbuf_tensor("p5_kl", [64, T], BF16))
        vb = _es.enter_context(nc.sbuf_tensor("p5_v", [128, 32, 66], F32))
        pb = _es.enter_context(nc.sbuf_tensor("p5_p", [128, 4, 512], F32))
        zz = _es.enter_context(nc.sbuf_tensor("p5_z", [128, 128], F32))
        shb = _es.enter_context(nc.sbuf_tensor("p5_shift", [128, 1], F32))
        dmk = _es.enter_context(nc.sbuf_tensor("p5_dmk", [128, 128], U8))
        dmki = _es.enter_context(nc.sbuf_tensor("p5_dmki", [128, 128], U8))
        yst = _es.enter_context(nc.sbuf_tensor("p5_yst", [66, 3, 512], F32))
        lrow = _es.enter_context(nc.sbuf_tensor("p5_l", [16, T], F32))
        psS = [_es.enter_context(nc.psum_tensor(f"p5_psS{j}", [128, 512], F32)) for j in range(4)]
        psYl = [_es.enter_context(nc.psum_tensor(f"p5_psY{j}", [128, 512], F32)) for j in range(3)]
        P = Prog(nc, "p5")
        c_dm0 = P.dma("sync", dma16(dmk[:, :], inp["dmask"].ap()))
        c_dm = P.dma("sync", dma16(dmki[:, :], inp["dmaski"].ap()))
        c_z = P.step("vector", lambda v: (v.memset(zz[:, :], 0.0),
                                          v.memset(shb[:, :], -EXP_SHIFT))[1])
        prev_head = None
        for h in range(P5_NH):
            hrow = slice(64 * (h % 2), 64 * (h % 2) + 64) if False else None
            pdeps = [prev_head] if prev_head is not None else []
            hrows = slice(64 * h, 64 * h + 64)
            ld_q1 = P.dma("sync", dma16(qh[:, :], scr["q1hT"].ap()[hrows, :]), pdeps)
            ld_q2 = P.dma("sync", dma16(qs2[0:64, :], scr["q1lT"].ap()[hrows, :]), pdeps)
            ld_q3 = P.dma("sync", dma16(qs2[64:128, :], scr["q1hT"].ap()[hrows, :]), pdeps)
            ld_k1 = P.dma("sync", dma16(ks2[0:64, :], scr["k1hT"].ap()[hrows, :]), pdeps)
            ld_k2 = P.dma("sync", dma16(ks2[64:128, :], scr["k1lT"].ap()[hrows, :]), pdeps)
            ld_k3 = P.dma("sync", dma16(kl[:, :], scr["k1lT"].ap()[hrows, :]), pdeps)
            ld_q = ld_q3
            ld_k = ld_k3
            ld_v = P.dma("sync", dma16(
                vb[:, :, 0:64],
                scr["v1tok"].ap().rearrange("(n p) d -> p n d", p=128)[:, :, 64 * h:64 * h + 64]), pdeps)
            vload = P.step("vector", lambda v: (v.memset(vb[:, :, 64:65], 1.0),
                                                v.memset(vb[:, :, 65:66], 0.0))[1], [ld_v] + pdeps)
            head_tail = []
            use_ps = [None, None, None, None]
            use_pb = [None, None, None, None]
            use_py = [None, None, None]
            LA = 4
            seq = [(ib, jc) for ib in range(NI) for jc in range(4 * ib + 4)]
            pready_of = {}
            pv_of = {}
            last_pv_of_ib = {}
            for ib in range(NI):
                last_pv_of_ib[ib] = 4 * ib + 3

            def rec_score(idx):
                ib, jc = seq[idx]
                i0 = ib * 512
                d_off = max(0, jc * 128 - i0)
                jj = idx % 4

                def f_sc(t, jc=jc, d_off=d_off, jj=jj, i0=i0):
                    t.matmul(psS[jj][:, d_off:512], ks2[0:64, jc * 128:(jc + 1) * 128],
                             qh[:, i0 + d_off:i0 + 512], start=True, stop=False)
                    t.matmul(psS[jj][:, d_off:512], kl[:, jc * 128:(jc + 1) * 128],
                             qs2[0:64, i0 + d_off:i0 + 512], start=False, stop=False)
                    return t.matmul(psS[jj][:, d_off:512], ks2[:, jc * 128:(jc + 1) * 128],
                                    qs2[:, i0 + d_off:i0 + 512], start=False, stop=True)
                d = [ld_q1, ld_q2, ld_q, ld_k1, ld_k2, ld_k]
                if use_ps[jj] is not None:
                    d.append(use_ps[jj])
                msc = P.step("tensor", f_sc, d)

                def f_exp(s, d_off=d_off, jj=jj):
                    return s.activation(pb[:, jj, d_off:512], psS[jj][:, d_off:512],
                                        AF.Exp, scale=0.125, bias=shb[:, 0:1])
                d = [msc, c_z]
                if use_pb[jj] is not None:
                    d.append(use_pb[jj])
                sexp = P.step("scalar", f_exp, d)
                use_ps[jj] = sexp
                pready = sexp
                if d_off > 0 or jc * 128 == i0:
                    dd = d_off

                    def f_msk(v, jj=jj, dd=dd):
                        return v.copy_predicated(pb[:, jj, dd:dd + 128], dmki[:, :], zz[:, :])
                    pready = P.step("vector", f_msk, [sexp, c_z, c_dm])
                pready_of[idx] = (sexp, pready)

            def rec_pv(idx):
                ib, jc = seq[idx]
                i0 = ib * 512
                d_off = max(0, jc * 128 - i0)
                jj = idx % 4
                njc = 4 * ib + 4
                psY = psYl[ib % 3]
                diag = (d_off > 0)
                exp_done, mask_done = pready_of[idx]
                if not diag:
                    def f_pv(t, jc=jc, d_off=d_off, jj=jj, first=(jc == 0), last=(jc == njc - 1), psY=psY):
                        return t.matmul(psY[0:66, d_off:512], vb[:, jc, :], pb[:, jj, d_off:512],
                                        start=first, stop=last)
                    d = [mask_done, vload]
                    if use_py[ib % 3] is not None:
                        d.append(use_py[ib % 3])
                    mpv = P.step("tensor", f_pv, d)
                else:
                    def f_pv_m(t, jc=jc, d_off=d_off, jj=jj, last=(jc == njc - 1), psY=psY):
                        return t.matmul(psY[0:66, d_off + 128:512], vb[:, jc, :],
                                        pb[:, jj, d_off + 128:512], start=False, stop=last)
                    def f_pv_s(t, jc=jc, d_off=d_off, jj=jj, last=(jc == njc - 1), psY=psY):
                        return t.matmul(psY[0:66, d_off:d_off + 128], vb[:, jc, :],
                                        pb[:, jj, d_off:d_off + 128], start=False, stop=last)
                    d = [exp_done, vload]
                    if use_py[ib % 3] is not None:
                        d.append(use_py[ib % 3])
                    if d_off + 128 < 512:
                        P.step("tensor", f_pv_m, d)
                    mpv = P.step("tensor", f_pv_s, [mask_done, vload] +
                                 ([use_py[ib % 3]] if use_py[ib % 3] is not None else []))
                use_pb[jj] = mpv
                pv_of[idx] = mpv
                if jc == njc - 1:
                    def f_ev(s, ib=ib, psY=psY):
                        s.copy(yst[0:64, ib % 3, :], psY[0:64, 0:512])
                        return s.copy(yst[64:65, ib % 3, :], psY[64:65, 0:512])
                    d = [mpv]
                    if len(head_tail) >= 3:
                        d.append(head_tail[-3])
                    evy = P.step("scalar", f_ev, d)
                    use_py[ib % 3] = evy
                    st_y = P.dma("gpsimd", dma16(scr["attraw"].ap()[64 * h:64 * h + 64, i0:i0 + 512],
                                                 yst[0:64, ib % 3, :]), [evy])
                    st_l = P.dma("gpsimd", dma16(scr["l1s"].ap()[h:h + 1, i0:i0 + 512],
                                                 yst[64:65, ib % 3, :]), [evy])
                    head_tail.append(st_l)

            for idx in range(len(seq) + LA):
                if idx >= LA:
                    rec_pv(idx - LA)
                if idx < len(seq):
                    rec_score(idx)
            prev_head = head_tail[-1]

        if P5_NH < 16:
            P.play()
            return
        ld_l = P.dma("sync", dma16(lrow[:, :], scr["l1s"].ap()), [prev_head])
        e_ln = P.step("scalar", lambda s: (s.activation(lrow[:, :], lrow[:, :], AF.Ln),
                                           s.activation(lrow[:, :], lrow[:, :], AF.Exp, scale=-1.0))[1],
                      [ld_l])
        st_li = P.dma("gpsimd", dma16(scr["linv1"].ap(), lrow[:, :]), [e_ln])
        P.play()


def phase_p6(nc, B):
    TTn = 256
    NT = T // TTn
    inp, scr = B.inp, B.scr
    outT = B.out
    with contextlib.ExitStack() as _es:
        ar = _es.enter_context(nc.sbuf_tensor("p6_ar", [128, 2, NCK, TTn], F32))
        li = _es.enter_context(nc.sbuf_tensor("p6_li", [16, 2, TTn], F32))
        att = _es.enter_context(nc.sbuf_tensor("p6_att", [128, NCK, TTn], F32))
        xn1 = _es.enter_context(nc.sbuf_tensor("p6_xn1", [128, 2, NCK, TTn], F32))
        xn2 = _es.enter_context(nc.sbuf_tensor("p6_xn2", [128, 2, NCK, TTn], F32))
        x1b = _es.enter_context(nc.sbuf_tensor("p6_x1", [128, 2, NCK, TTn], F32))
        s1 = _es.enter_context(nc.sbuf_tensor("p6_s1", [128, NCK, TTn], F32))
        sm = _es.enter_context(nc.sbuf_tensor("p6_sm", [64, TTn], F32))
        hb = _es.enter_context(nc.sbuf_tensor("p6_h", [128, NCK, TTn], F32))
        gsm = _es.enter_context(nc.sbuf_tensor("p6_gsm", [64, TTn], F32))
        of = _es.enter_context(nc.sbuf_tensor("p6_of", [128, NCK, TTn], F32))
        oft = _es.enter_context(nc.sbuf_tensor("p6_oft", [128, NCK, TTn], F32))
        outs = _es.enter_context(nc.sbuf_tensor("p6_out", [128, 2, NCK, TTn], F32))
        WO = _es.enter_context(nc.sbuf_tensor("p6_wo", [128, NCK, D], F32))
        WIN = _es.enter_context(nc.sbuf_tensor("p6_win", [128, NCK, 64], F32))
        WOUT = _es.enter_context(nc.sbuf_tensor("p6_wout", [64, D], F32))
        W1E = _es.enter_context(nc.sbuf_tensor("p6_w1e", [64, NCK * 128], F32))
        W2R = _es.enter_context(nc.sbuf_tensor("p6_w2r", [128, NCK * 64], F32))
        LEX = _es.enter_context(nc.sbuf_tensor("p6_lex", [16, NCK * 128], F32))
        bob = _es.enter_context(nc.sbuf_tensor("p6_bo", [128, NCK], F32))
        binb = _es.enter_context(nc.sbuf_tensor("p6_bin", [64, 1], F32))
        b1b = _es.enter_context(nc.sbuf_tensor("p6_b1", [128, NCK], F32))
        b2b = _es.enter_context(nc.sbuf_tensor("p6_b2", [64, 1], F32))
        boutb = _es.enter_context(nc.sbuf_tensor("p6_bout", [128, NCK], F32))
        psp0 = _es.enter_context(nc.psum_tensor("p6_psp0", [128, 512], F32))
        psp1 = _es.enter_context(nc.psum_tensor("p6_psp1", [128, 512], F32))
        psE = _es.enter_context(nc.psum_tensor("p6_psE", [128, 512], F32))
        psF = _es.enter_context(nc.psum_tensor("p6_psF", [128, 512], F32))
        psp = [psp0, psp1]
        P = Prog(nc, "p6")
        wld = [P.dma("sync", dma16(WO[:, :, :], inp["woT1"].ap().rearrange("(c p) o -> p c o", p=128))),
               P.dma("sync", dma16(WIN[:, :, :], inp["win1"].ap().rearrange("(c p) o -> p c o", p=128))),
               P.dma("sync", dma16(WOUT[:, :], inp["wout1"].ap())),
               P.dma("sync", dma16(W1E[:, :], inp["w1exp"].ap())),
               P.dma("sync", dma16(W2R[:, :], inp["w2red"].ap())),
               P.dma("sync", dma16(LEX[:, :], inp["lexp"].ap())),
               P.dma("sync", dma16(bob[:, :], inp["bo1"].ap())),
               P.dma("sync", dma16(binb[:, :], inp["bin1"].ap())),
               P.dma("sync", dma16(b1b[:, :], inp["b1f"].ap())),
               P.dma("sync", dma16(b2b[:, :], inp["b2c"].ap())),
               P.dma("sync", dma16(boutb[:, :], inp["bout1"].ap()))]
        ring = [None, None]
        use_psp = [None, None]
        use_mid = None
        use_psE = None
        use_psF = None
        for i in range(NT):
            r = i % 2
            tt = slice(i * TTn, (i + 1) * TTn)
            d0 = [ring[r]] if ring[r] is not None else []
            ld_ar = P.dma("sync", dma16(ar[:, r, :, :], rearr(scr["attraw"])[:, :, tt]), d0)
            ld_li = P.dma("sync", dma16(li[:, r, :], scr["linv1"].ap()[:, tt]), d0)
            ld_xn1 = P.dma("sync", dma16(xn1[:, r, :, :], rearr(scr["xn1T"])[:, :, tt]), d0)
            ld_xn2 = P.dma("sync", dma16(xn2[:, r, :, :], rearr(scr["xn2T"])[:, :, tt]), d0)
            ld_x1 = P.dma("sync", dma16(x1b[:, r, :, :], rearr(scr["x1T"])[:, :, tt]), d0)

            att_steps = []
            for c in range(NCK):
                def f_le(t, c=c, r=r):
                    return t.matmul(psE[:, 0:TTn], LEX[:, c * 128:(c + 1) * 128], li[0:16, r, :],
                                    start=True, stop=True)
                d = [ld_li, wld[5]]
                if use_psE is not None:
                    d.append(use_psE)
                mle = P.step("tensor", f_le, d)

                def f_att(v, c=c, r=r):
                    return v.tensor_tensor(att[:, c, :], ar[:, r, c, :], psE[:, 0:TTn], OP.mult)
                d = [mle, ld_ar]
                if use_mid is not None:
                    d.append(use_mid)
                vat = P.step("vector", f_att, d)
                use_psE = vat
                att_steps.append(vat)

            def ev_s1(e, oc, r=r):
                e.tensor_scalar(s1[:, oc, :], psp[oc % 2][:, 0:TTn], bob[:, oc:oc + 1], None, OP.add)
                return e.tensor_tensor(s1[:, oc, :], s1[:, oc, :], xn2[:, r, oc, :], OP.add)
            evs1 = _proj_steps(P, WO, wld[0], lambda c: att[:, c, :], psp, use_psp,
                               ("vector", ev_s1), [wld[6], ld_xn2], att_steps, TTn)

            def f_win(t, r=r):
                last = None
                for c in range(NCK):
                    last = t.matmul(psF[0:64, 0:TTn], WIN[:, c, :], xn1[:, r, c, :],
                                    start=(c == 0), stop=(c == NCK - 1))
                return last
            d = [ld_xn1, wld[1]]
            if use_psF is not None:
                d.append(use_psF)
            mwin = P.step("tensor", f_win, d)
            ev_sm = P.step("vector", lambda v: v.tensor_scalar(sm[:, :], psF[0:64, 0:TTn],
                                                               binb[:, 0:1], None, OP.add),
                           [mwin, wld[7]])
            use_psF = ev_sm

            h_steps = []
            for c in range(NCK):
                def f_ex(t, c=c):
                    return t.matmul(psE[:, 0:TTn], W1E[:, c * 128:(c + 1) * 128], sm[:, :],
                                    start=True, stop=True)
                d = [ev_sm, wld[3]]
                if use_psE is not None:
                    d.append(use_psE)
                mex = P.step("tensor", f_ex, d)
                s_h = P.step("scalar", lambda s, c=c: s.activation(hb[:, c, :], psE[:, 0:TTn],
                                                                   AF.Relu, bias=b1b[:, c:c + 1]),
                             [mex, wld[8]])
                use_psE = s_h
                h_steps.append(s_h)

            def f_red(t):
                last = None
                for c in range(NCK):
                    last = t.matmul(psF[0:64, 0:TTn], W2R[:, c * 64:(c + 1) * 64], hb[:, c, :],
                                    start=(c == 0), stop=(c == NCK - 1))
                return last
            mred = P.step("tensor", f_red, h_steps + [wld[4], ev_sm])
            ev_gsm = P.step("vector", lambda v: v.tensor_scalar(gsm[:, :], psF[0:64, 0:TTn],
                                                                b2b[:, 0:1], None, OP.add),
                            [mred, wld[9]])
            use_psF = ev_gsm

            def ev_of(e, oc):
                return e.tensor_scalar(of[:, oc, :], psp[oc % 2][:, 0:TTn], boutb[:, oc:oc + 1],
                                       None, OP.add)
            def wout_mm():
                evs = []
                for oc in range(NCK):
                    def f_mm(t, oc=oc):
                        return t.matmul(psp[oc % 2][:, 0:TTn], WOUT[:, oc * 128:(oc + 1) * 128],
                                        gsm[:, :], start=True, stop=True)
                    d = [ev_gsm, wld[2]]
                    if use_psp[oc % 2] is not None:
                        d.append(use_psp[oc % 2])
                    mp = P.step("tensor", f_mm, d)
                    ev = P.step("vector", lambda e, oc=oc: ev_of(e, oc), [mp, wld[10]])
                    use_psp[oc % 2] = ev
                    evs.append(ev)
                return evs
            evs_of = wout_mm()

            s_sig = P.step("scalar", lambda s: s.activation(oft[:, :, :], of[:, :, :], AF.Sigmoid),
                           evs_of)

            def f_out(v, r=r):
                v.tensor_tensor(oft[:, :, :], oft[:, :, :], of[:, :, :], OP.mult)
                v.tensor_tensor(oft[:, :, :], oft[:, :, :], s1[:, :, :], OP.mult)
                return v.tensor_tensor(outs[:, r, :, :], x1b[:, r, :, :], oft[:, :, :], OP.add)
            v_out = P.step("vector", f_out, [s_sig, ld_x1] + evs1)
            use_mid = v_out

            st = P.dma("gpsimd", dma16(rearr(outT)[:, :, tt], outs[:, r, :, :]), [v_out])
            ring[r] = st
        P.play()


def build(debug=(), phases=(1, 2, 3, 4, 5, 6, 7), preload=()):
    nc = bass.Bass("TRN2", target_bir_lowering=False, detect_race_conditions=False)
    B = Builder(nc, debug, preload)

    B.dram_in("xT", [D, T])
    B.dram_in("wrT", [D, D]); B.dram_in("wkT", [D, D]); B.dram_in("wvT", [D, D])
    B.dram_in("wgT", [D, D]); B.dram_in("woT", [D, D])
    B.dram_in("maa_w1", [D, 160])
    B.dram_in("w2h", [32, 5 * D])
    B.dram_in("c5col", [128, 5 * NCK])
    B.dram_in("tmx", [128, NCK]); B.dram_in("tdec", [128, NCK])
    B.dram_in("dw1", [D, 64]); B.dram_in("dw2", [64, D])
    B.dram_in("lnw0", [128, NCK]); B.dram_in("lnb0", [128, NCK])
    B.dram_in("wrfT", [D, D]); B.dram_in("wkfT", [D, D]); B.dram_in("wvfT", [D, D])
    B.dram_in("mur", [128, NCK]); B.dram_in("muk", [128, NCK])
    B.dram_in("normw1", [128, NCK])
    B.dram_in("wqT1", [D, D]); B.dram_in("wkT1", [D, D]); B.dram_in("wvT1", [D, D]); B.dram_in("woT1", [D, D])
    B.dram_in("bq1", [128, NCK]); B.dram_in("bk1", [128, NCK]); B.dram_in("bv1", [128, NCK]); B.dram_in("bo1", [128, NCK])
    B.dram_in("win1", [D, 64]); B.dram_in("bin1", [64, 1])
    B.dram_in("w1exp", [64, NCK * 128]); B.dram_in("b1f", [128, NCK])
    B.dram_in("w2red", [128, NCK * 64]); B.dram_in("b2c", [64, 1])
    B.dram_in("wout1", [64, D]); B.dram_in("bout1", [128, NCK])
    B.dram_in("ones128", [128, 1]); B.dram_in("ones1x128", [1, 128])
    B.dram_in("gn2", [128, 2]); B.dram_in("gnexp", [2, 128]); B.dram_in("ident", [128, 128])
    B.dram_in("lexp", [16, NCK * 128])
    B.dram_in("trimask2", [128, 256], U8)
    B.dram_in("dmask", [128, 128], U8); B.dram_in("dmaski", [128, 128], U8)

    for nm, shp in [
        ("xnT", [D, T]), ("xxxT", [32, 5 * T]), ("rT", [D, T]), ("kT", [D, T]),
        ("gpreT", [D, T]), ("ewT", [D, T]), ("csT", [D, T]), ("vtok", [T, D]),
        ("ynT", [D, T]), ("x1T", [D, T]), ("xn1T", [D, T]), ("xn2T", [D, T]),
        ("v1tok", [T, D]), ("attraw", [D, T]),
        ("l1s", [16, T]), ("linv1", [16, T]), ("gn_a", [2, T]), ("gn_b", [2, T]),
        ("gT", [D, T]), ("s0T", [D, T]),
    ]:
        B.scratch(nm, shp)

    for nm in ["q1hT", "q1lT", "k1hT", "k1lT"]:
        B.scratch(nm, [D, T], BF16)
    outT = nc.dram_tensor("outT", [D, T], F32, kind="ExternalOutput")
    B.out = outT

    if 1 in phases:
        phase_p1a(nc, B)
    if 2 in phases:
        phase_p1b(nc, B, 1)
        phase_p1b(nc, B, 2)
    if 3 in phases:
        phase_p2(nc, B)
    if 4 in phases:
        phase_p3a(nc, B)
        phase_p3b(nc, B)
    if 5 in phases:
        phase_p4(nc, B)
    if 6 in phases:
        phase_p5(nc, B)
    if 7 in phases:
        phase_p6(nc, B)
    return nc


def _host_consts(inputs):
    g = inputs
    d = {}
    d["wrT"] = g["L0_attn_W_r"].T
    d["wkT"] = g["L0_attn_W_k"].T
    d["wvT"] = g["L0_attn_W_v"].T
    d["wgT"] = g["L0_attn_W_g"].T
    d["woT"] = g["L0_attn_W_o"].T
    d["maa_w1"] = g["L0_attn_time_maa_w1"]
    w2 = g["L0_attn_time_maa_w2"]
    d["w2h"] = np.concatenate([w2[f] for f in range(5)], axis=1)
    c5 = np.stack([g["L0_attn_time_maa_w"].reshape(D), g["L0_attn_time_maa_k"].reshape(D),
                   g["L0_attn_time_maa_v"].reshape(D), g["L0_attn_time_maa_r"].reshape(D),
                   g["L0_attn_time_maa_g"].reshape(D)])
    col = lambda v: np.ascontiguousarray(np.asarray(v).reshape(NCK, 128).T)
    d["c5col"] = np.concatenate([col(c5[f]) for f in range(5)], axis=1)
    d["tmx"] = col(g["L0_attn_time_maa_x"])
    d["tdec"] = col(g["L0_attn_time_decay"])
    d["dw1"] = g["L0_attn_time_decay_w1"]
    d["dw2"] = g["L0_attn_time_decay_w2"]
    d["lnw0"] = col(g["L0_attn_ln_w"])
    d["lnb0"] = col(g["L0_attn_ln_b"])
    d["wrfT"] = g["L0_ffn_W_r"].T
    d["wkfT"] = g["L0_ffn_W_k"].T
    d["wvfT"] = g["L0_ffn_W_v"].T
    d["mur"] = col(g["L0_ffn_mu_r"])
    d["muk"] = col(g["L0_ffn_mu_k"])
    d["normw1"] = col(g["L1_attn_norm_w"])
    d["wqT1"] = g["L1_attn_Wq"].T
    d["wkT1"] = g["L1_attn_Wk"].T
    d["wvT1"] = g["L1_attn_Wv"].T
    d["woT1"] = g["L1_attn_Wo"].T
    d["bq1"] = col(g["L1_attn_bq"])
    d["bk1"] = col(g["L1_attn_bk"])
    d["bv1"] = col(g["L1_attn_bv"])
    d["bo1"] = col(g["L1_attn_bo"])
    d["win1"] = g["L1_ffn_Win"]
    d["bin1"] = np.asarray(g["L1_ffn_bin_"]).reshape(64, 1)
    W1 = np.asarray(g["L1_ffn_W1"])
    w1exp = np.zeros((64, NCK * 128), dtype=np.float32)
    for c in range(NCK):
        for p in range(128):
            m = 8 * c + p // 16
            w1exp[m, c * 128 + p] = W1[m, p % 16]
    d["w1exp"] = w1exp
    d["b1f"] = col(np.asarray(g["L1_ffn_b1"]).reshape(D))
    W2 = np.asarray(g["L1_ffn_W2"])
    w2red = np.zeros((128, NCK * 64), dtype=np.float32)
    for c in range(NCK):
        for p in range(128):
            m = 8 * c + p // 16
            w2red[p, c * 64 + m] = W2[m, p % 16]
    d["w2red"] = w2red
    d["b2c"] = np.asarray(g["L1_ffn_b2"]).reshape(64, 1)
    d["wout1"] = g["L1_ffn_Wout"]
    d["bout1"] = col(g["L1_ffn_bout"])
    d["ones128"] = np.ones((128, 1), dtype=np.float32)
    d["ones1x128"] = np.ones((1, 128), dtype=np.float32)
    gn2 = np.zeros((128, 2), dtype=np.float32)
    gn2[0:64, 0] = 1.0
    gn2[64:128, 1] = 1.0
    d["gn2"] = gn2
    gnexp = np.zeros((2, 128), dtype=np.float32)
    gnexp[0, 0:64] = 1.0
    gnexp[1, 64:128] = 1.0
    d["gnexp"] = gnexp
    d["ident"] = np.eye(128, dtype=np.float32)
    lexp = np.zeros((16, NCK * 128), dtype=np.float32)
    for c in range(NCK):
        for p in range(128):
            lexp[2 * c + p // 64, c * 128 + p] = 1.0
    d["lexp"] = lexp
    tri = (np.arange(128)[None, :] >= np.arange(128)[:, None]).astype(np.uint8)
    d["trimask2"] = np.tile(tri, (1, 2))
    d["dmask"] = tri
    d["dmaski"] = (1 - tri).astype(np.uint8)
    out = {}
    for k, v in d.items():
        v = np.asarray(v)
        if v.dtype == np.uint8:
            out[k] = np.ascontiguousarray(v)
        else:
            out[k] = np.ascontiguousarray(v.astype(np.float32))
    return out


def kernel(**inputs):
    consts = _host_consts(inputs)
    nc = build()
    x = np.asarray(inputs["L0_attn_x"])
    in_maps = []
    for b in range(8):
        m = dict(consts)
        m["xT"] = np.ascontiguousarray(x[b].T)
        in_maps.append(m)
    res = run_bass_kernel_spmd(nc, in_maps, core_ids=list(range(8)))
    out = np.stack([res.results[b]["outT"].T for b in range(8)], axis=0)
    return np.ascontiguousarray(out.astype(np.float32))
